# version 16
# baseline (speedup 1.0000x reference)
import numpy as np
import ml_dtypes
from contextlib import ExitStack

import concourse.bass as bass
import concourse.mybir as mybir
from concourse.bass_utils import run_bass_kernel_spmd

F32 = mybir.dt.float32
BF16 = mybir.dt.bfloat16
AF = mybir.ActivationFunctionType
ALU = mybir.AluOpType
AX = mybir.AxisListType
bfd = ml_dtypes.bfloat16

S, D, H, HD, DF = 8192, 1024, 16, 64, 4096
NT = S // 128
NG = S // 512
CRND = 12582912.0
EPS_LN = 1e-5
EPS_Z = 1e-6
QEPS = 1e-5
NW1 = float(DF * D)

ENGINES = ("sync", "vector", "scalar", "tensor", "gpsimd")
_SU_MODE = 3


class _Op:
    __slots__ = ("eng", "fn", "waits", "name", "dma", "stream", "inc", "val", "gidx")


class Pipe:

    def __init__(self, nc, ctx, tag):
        self.nc = nc
        self.ctx = ctx
        self.tag = tag
        self.ops = []
        self.byname = {}
        self.sems = {}

    def _sem(self, eng, stream):
        key = (eng, stream)
        if key not in self.sems:
            name = f"{self.tag}_{eng}_{stream}" if stream else f"{self.tag}_{eng}"
            self.sems[key] = self.ctx.enter_context(self.nc.semaphore(name))
        return self.sems[key]

    def add(self, eng, fn, waits=(), name=None, dma=False, stream=None):
        op = _Op()
        op.eng, op.fn, op.name, op.dma = eng, fn, name, dma
        op.stream = (stream or "d") if dma else None
        op.waits = tuple(w for w in waits if w is not None)
        op.inc = False
        op.val = None
        op.gidx = len(self.ops)
        self.ops.append(op)
        if name is not None:
            assert name not in self.byname, name
            self.byname[name] = op
        return op

    def build(self):
        for op in self.ops:
            if op.dma:
                op.inc = True
            for w in op.waits:
                dep = self.byname[w]
                assert dep.gidx < op.gidx, (op.name, w)
                dep.inc = True
        counts = {}
        for op in self.ops:
            if op.inc:
                k = (op.eng, op.stream)
                counts[k] = counts.get(k, 0) + (16 if op.dma else 1)
                op.val = counts[k]
                self._sem(op.eng, op.stream)
        per_eng = {}
        for op in self.ops:
            per_eng.setdefault(op.eng, []).append(op)
        with self.nc.Block() as block:
            for eng in ENGINES:
                if eng not in per_eng:
                    continue
                ops = per_eng[eng]

                def mk(ops=ops):
                    def emit(E):
                        for op in ops:
                            for w in op.waits:
                                dep = self.byname[w]
                                E.wait_ge(self._sem(dep.eng, dep.stream), dep.val)
                            instr = op.fn(E)
                            if op.inc:
                                assert instr is not None, op.name
                                instr.then_inc(self._sem(op.eng, op.stream),
                                               16 if op.dma else 1)
                    return emit

                getattr(block, eng)(mk())


def build_program(dump_internal=False, stop_after=None):
    nc = bass.Bass("TRN2", target_bir_lowering=False, detect_race_conditions=False)

    x_d = nc.dram_tensor("x", [S, D], F32, kind="ExternalInput").ap()
    wq_d = nc.dram_tensor("wq", [D, D], F32, kind="ExternalInput").ap()
    wk_d = nc.dram_tensor("wk", [D, D], F32, kind="ExternalInput").ap()
    wv_d = nc.dram_tensor("wv", [D, D], F32, kind="ExternalInput").ap()
    wo_d = nc.dram_tensor("wo", [D, D], F32, kind="ExternalInput").ap()
    bq_d = nc.dram_tensor("bq", [1, D], F32, kind="ExternalInput").ap()
    bk_d = nc.dram_tensor("bk", [1, D], F32, kind="ExternalInput").ap()
    bv_d = nc.dram_tensor("bv", [1, D], F32, kind="ExternalInput").ap()
    bo_d = nc.dram_tensor("bo", [1, D], F32, kind="ExternalInput").ap()
    w1_d = nc.dram_tensor("w1", [DF, D], F32, kind="ExternalInput").ap()
    w2_d = nc.dram_tensor("w2", [D, DF], F32, kind="ExternalInput").ap()
    cw0_d = nc.dram_tensor("cw0", [128, 32], F32, kind="ExternalInput").ap()
    cw1_d = nc.dram_tensor("cw1", [128, 32], F32, kind="ExternalInput").ap()
    cw2_d = nc.dram_tensor("cw2", [128, 32], F32, kind="ExternalInput").ap()
    cb_d = nc.dram_tensor("cb", [128, 32], F32, kind="ExternalInput").ap()
    idb_d = nc.dram_tensor("identb", [128, 128], BF16, kind="ExternalInput").ap()
    idf_d = nc.dram_tensor("identf", [128, 128], F32, kind="ExternalInput").ap()
    ob_d = nc.dram_tensor("onesb", [1, 128], BF16, kind="ExternalInput").ap()
    ofr_d = nc.dram_tensor("onesfr", [1, 128], F32, kind="ExternalInput").ap()
    ofc_d = nc.dram_tensor("onesfc", [128, 1], F32, kind="ExternalInput").ap()

    out_d = nc.dram_tensor("out", [S, D], F32, kind="ExternalOutput").ap()
    ikind = "ExternalOutput" if dump_internal else "Internal"
    u_d = nc.dram_tensor("u_scr", [S, D], F32, kind=ikind).ap()
    h_d = nc.dram_tensor("h_scr", [DF, S], BF16, kind=ikind).ap()
    w1q_d = nc.dram_tensor("w1q_scr", [DF, D], BF16, kind=ikind).ap()
    w2q_d = nc.dram_tensor("w2q_scr", [D, DF], BF16, kind=ikind).ap()

    hview = h_d.rearrange("(c p) t -> p c t", p=128)
    w1qv = w1q_d.rearrange("(c p) j -> p c j", p=128)

    ctx = ExitStack()
    with ctx:
        def sb(name, shape, dt):
            return ctx.enter_context(nc.sbuf_tensor(name, shape, dt))

        identb = sb("identb_s", [128, 128], BF16)
        identf = sb("identf_s", [128, 128], F32)
        onesb = sb("onesb_s", [1, 128], BF16)
        onesfr = sb("onesfr_s", [1, 128], F32)
        onesfc = sb("onesfc_s", [128, 1], F32)
        bqb = sb("bqb", [1, D], BF16)
        bkb = sb("bkb", [1, D], BF16)
        bvb = sb("bvb", [1, D], BF16)
        bob = sb("bob", [1, D], BF16)
        cw0s = sb("cw0s", [128, 32], F32)
        cw1s = sb("cw1s", [128, 32], F32)
        cw2s = sb("cw2s", [128, 32], F32)
        cbs = sb("cbs", [128, 32], F32)
        wscal = sb("wscal", [128, 4], F32)
        epsln = sb("epsln", [128, 1], F32)

        ctxA = ctx.enter_context(ExitStack())
        w1qT = ctxA.enter_context(nc.sbuf_tensor("w1qT", [128, 8 * DF], BF16))
        kvblk = ctxA.enter_context(nc.sbuf_tensor("kvblk", [128, 8 * 128], BF16))
        ksumb = ctxA.enter_context(nc.sbuf_tensor("ksumb", [128, D], F32))

        def ln_head(pipe, pfx, j, src0, src1, st, bnrow, waits):
            def a_fn(E, src0=src0, src1=src1, st=st, bnrow=bnrow):
                E.bn_stats(bnrow[:, 0:6], src0)
                E.bn_stats(bnrow[:, 6:12], src1)
                E.drain()
                E.bn_aggr(st[:, 0:2], bnrow[:, :])
                E.drain()
                E.tensor_scalar(st[:, 2:3], st[:, 0:1], -1.0, None, ALU.mult)
                return E.drain()
            pipe.add("vector", a_fn, waits=waits, name=f"{pfx}A.{j}")

            def q_fn(E, st=st):
                E.activation(st[:, 4:5], st[:, 1:2], AF.Ln, bias=epsln[:, :])
                E.drain()
                return E.activation(st[:, 3:4], st[:, 4:5], AF.Exp, scale=-0.5)
            pipe.add("scalar", q_fn, waits=(f"{pfx}A.{j}",), name=f"{pfx}Q.{j}")

        with ExitStack() as cs:
            pipe = Pipe(nc, ctx, "su")
            wst = [cs.enter_context(nc.sbuf_tensor(f"wst{i}", [128, 2048], F32))
                   for i in range(2)]
            wtf = [cs.enter_context(nc.sbuf_tensor(f"wtf{i}", [128, 2048], F32))
                   for i in range(2)]
            wqst = [cs.enter_context(nc.sbuf_tensor(f"wqst{i}", [128, 2048], BF16))
                    for i in range(2)]
            wsums = cs.enter_context(nc.sbuf_tensor("wsums", [128, 32], F32))
            vv1 = cs.enter_context(nc.sbuf_tensor("vv1", [128, 2], F32))
            scrow = cs.enter_context(nc.sbuf_tensor("scrow", [1, 4], F32))
            w1p = [cs.enter_context(nc.sbuf_tensor(f"w1p{i}", [128, 512], BF16))
                   for i in range(4)]
            psmB = cs.enter_context(nc.psum_tensor("psmB", [1, 32], F32))
            psmC = cs.enter_context(nc.psum_tensor("psmC", [128, 4], F32))
            sutp = cs.enter_context(nc.psum_tensor("sutp", [128, 512], BF16))

            pipe.add("gpsimd", lambda E: E.memset(epsln[:, :], EPS_LN), name="eps")
            for nm, dst, srch in (("ldidb", identb, idb_d), ("ldidf", identf, idf_d),
                                  ("ldob", onesb, ob_d), ("ldofr", onesfr, ofr_d),
                                  ("ldofc", onesfc, ofc_d), ("ldcw0", cw0s, cw0_d),
                                  ("ldcw1", cw1s, cw1_d), ("ldcw2", cw2s, cw2_d),
                                  ("ldcb", cbs, cb_d)):
                pipe.add("sync", lambda E, d=dst, s=srch: E.dma_start(d[:, :], s[:, :]),
                         name=nm, dma=True, stream="cl")
            for nm, dst, srch in (("ldbq", bqb, bq_d), ("ldbk", bkb, bk_d),
                                  ("ldbv", bvb, bv_d), ("ldbo", bob, bo_d)):
                pipe.add("gpsimd", lambda E, d=dst, s=srch: E.dma_start(d[:, :], s[:, :]),
                         name=nm, dma=True, stream="bl")

            def w_piece_ap(widx, k):
                if widx == 0:
                    return w1_d.rearrange("(c p) j -> p c j", p=128)[:, 2 * k:2 * k + 2, :]
                c, hf = k // 2, k % 2
                return w2_d[c * 128:(c + 1) * 128, hf * 2048:(hf + 1) * 2048]

            def wq_piece_ap(widx, k):
                if widx == 0:
                    return w1qv[:, 2 * k:2 * k + 2, :]
                c, hf = k // 2, k % 2
                return w2q_d[c * 128:(c + 1) * 128, hf * 2048:(hf + 1) * 2048]

            for widx in (0, 1):
                for k in range(16):
                    kk = widx * 16 + k
                    pipe.add("gpsimd",
                             lambda E, w=widx, k=k, kk=kk:
                             E.dma_start(wst[kk % 2][:, :].rearrange(
                                 "p (c j) -> p c j", j=1024) if w == 0
                                 else wst[kk % 2][:, :], w_piece_ap(w, k)),
                             waits=(f"ar{kk-2}",) if kk >= 2 else (),
                             name=f"al{kk}", dma=True, stream=f"a{kk % 2}")
                    pipe.add("vector",
                             lambda E, kk=kk:
                             E.reduce_sum(wsums[:, kk:kk + 1], wst[kk % 2][:, :],
                                          AX.X, apply_absolute_value=True),
                             waits=(f"al{kk}",), name=f"ar{kk}")

            def vra(E):
                E.drain()
                E.reduce_sum(vv1[0:128, 0:1], wsums[:, 0:16], AX.X)
                return E.reduce_sum(vv1[0:128, 1:2], wsums[:, 16:32], AX.X)
            pipe.add("vector", vra, waits=("ar31", "ldcb"), name="vrA")

            def tra(E):
                E.matmul(psmB[0:1, 0:1], vv1[0:128, 0:1], onesfc[:, 0:1],
                         start=True, stop=False)
                return E.matmul(psmB[0:1, 1:2], vv1[0:128, 1:2], onesfc[:, 0:1],
                                start=False, stop=True)
            pipe.add("tensor", tra, waits=("vrA",), name="trB")

            def mchain(E):
                E.tensor_scalar(scrow[0:1, 0:1], psmB[0:1, 0:1], 1.0 / NW1, QEPS,
                                ALU.mult, ALU.max)
                E.tensor_scalar(scrow[0:1, 2:3], psmB[0:1, 1:2], 1.0 / NW1, QEPS,
                                ALU.mult, ALU.max)
                E.drain()
                E.reciprocal(scrow[0:1, 1:2], scrow[0:1, 0:1])
                E.drain()
                return E.reciprocal(scrow[0:1, 3:4], scrow[0:1, 2:3])
            pipe.add("vector", mchain, waits=("trB",), name="mch")

            def bcmm(E):
                r = None
                for j in range(4):
                    r = E.matmul(psmC[:, j:j + 1], onesfr[0:1, :], scrow[0:1, j:j + 1],
                                 start=(j == 0), stop=(j == 3))
                return r
            pipe.add("tensor", bcmm, waits=("mch", "ldcb"), name="bcm")

            def evw(E):
                E.tensor_copy(wscal[:, :], psmC[:, :])
                return E.drain()
            pipe.add("vector", evw, waits=("bcm",), name="evw")

            for widx in (0, 1):
                swcol = wscal[:, 1:2] if widx == 0 else wscal[:, 3:4]
                for k in range(16):
                    kk = widx * 16 + k
                    pipe.add("gpsimd",
                             lambda E, w=widx, k=k, kk=kk:
                             E.dma_start(wst[kk % 2][:, :].rearrange(
                                 "p (c j) -> p c j", j=1024) if w == 0
                                 else wst[kk % 2][:, :], w_piece_ap(w, k)),
                             waits=(f"tq{kk-2}",) if kk >= 2 else ("ar31",),
                             name=f"tl{kk}", dma=True, stream=f"t{kk % 2}")

                    def trn(E, kk=kk, sc=swcol):
                        E.tensor_scalar(wtf[kk % 2][:, :], wst[kk % 2][:, :], sc, CRND,
                                        ALU.mult, ALU.add)
                        E.tensor_scalar(wtf[kk % 2][:, :], wtf[kk % 2][:, :], CRND, 1.0,
                                        ALU.subtract, ALU.min)
                        return E.tensor_scalar(wqst[kk % 2][:, :], wtf[kk % 2][:, :],
                                               -1.0, None, ALU.max)
                    pipe.add("vector", trn,
                             waits=(f"tl{kk}", "evw") + ((f"ts{kk-2}",) if kk >= 2 else ()),
                             name=f"tq{kk}")
                    pipe.add("gpsimd",
                             lambda E, w=widx, k=k, kk=kk:
                             E.dma_start(wq_piece_ap(w, k),
                                         wqst[kk % 2][:, :].rearrange(
                                             "p (c j) -> p c j", j=1024) if w == 0
                                         else wqst[kk % 2][:, :]),
                             waits=(f"tq{kk}",), name=f"ts{kk}", dma=True,
                             stream=f"s{kk % 2}")

            for kb in range(32):
                for cfh in range(2):
                    i = kb * 2 + cfh
                    cf0 = cfh * 4
                    pipe.add("gpsimd",
                             lambda E, kb=kb, cf0=cf0, i=i: E.dma_start(
                                 w1p[i % 4][:, :],
                                 w1q_d[kb * 128:(kb + 1) * 128,
                                       cf0 * 128:(cf0 + 4) * 128]),
                             waits=((f"ts{kb // 2}",) if _SU_MODE >= 3 else ())
                             + ((f"wpt{i-4}",) if (i >= 4 and _SU_MODE >= 2) else ()),
                             name=f"wpl{i}", dma=True, stream=f"wp{i % 4}")
                    if _SU_MODE < 2:
                        continue

                    def wpt_fn(E, i=i):
                        r = None
                        for k in range(4):
                            r = E.transpose(
                                sutp[:, k * 128:(k + 1) * 128],
                                w1p[i % 4][:, k * 128:(k + 1) * 128], identb[:, :])
                        return r
                    pipe.add("tensor", wpt_fn,
                             waits=(f"wpl{i}",)
                             + ((f"we{i-1}",) if i >= 1 else ("ldidb",)),
                             name=f"wpt{i}")

                    def we_fn(E, kb=kb, cf0=cf0, i=i):
                        r = None
                        for k in range(4):
                            r = E.activation(
                                w1qT[:, (cf0 + k) * DF + kb * 128:
                                     (cf0 + k) * DF + (kb + 1) * 128],
                                sutp[:, k * 128:(k + 1) * 128], AF.Copy)
                        return r
                    pipe.add("scalar", we_fn, waits=(f"wpt{i}",), name=f"we{i}")
            pipe.add("scalar", lambda E: None,
                     waits=(("we63",) if _SU_MODE >= 2 else ()) + ("ts31", "ts30"))
            pipe.add("gpsimd", lambda E: None, waits=("ldbo",))
            pipe.add("sync", lambda E: None, waits=("ldcb",))
            pipe.build()
            su_ts30 = (pipe.sems[("gpsimd", "s0")], pipe.byname["ts30"].val)
            su_ts31 = (pipe.sems[("gpsimd", "s1")], pipe.byname["ts31"].val)
            su_we = ((pipe.sems[("scalar", None)], pipe.byname["we63"].val)
                     if _SU_MODE >= 2 else (None, 0))
        if stop_after == "su":
            return nc

        with ExitStack() as cs:
            pipe = Pipe(nc, ctx, "p1")
            wk_s = cs.enter_context(nc.sbuf_tensor("wk_s", [128, 8 * D], BF16))
            wv_s = cs.enter_context(nc.sbuf_tensor("wv_s", [128, 8 * D], BF16))
            xb = [cs.enter_context(nc.sbuf_tensor(f"xb{i}", [128, D], F32)) for i in range(2)]
            st = [cs.enter_context(nc.sbuf_tensor(f"st{i}", [128, 10], F32)) for i in range(2)]
            bnrow = cs.enter_context(nc.sbuf_tensor("bnrow", [128, 12], F32))
            xc = [cs.enter_context(nc.sbuf_tensor(f"p1xc{i}", [128, D], F32)) for i in range(2)]
            lnx = [cs.enter_context(nc.sbuf_tensor(f"lnx{i}", [128, D], BF16)) for i in range(2)]
            lnxT = [cs.enter_context(nc.sbuf_tensor(f"lnxT{i}", [128, D], BF16)) for i in range(2)]
            t1 = [cs.enter_context(nc.sbuf_tensor(f"t1{i}", [128, D], F32)) for i in range(2)]
            km = [cs.enter_context(nc.sbuf_tensor(f"km{i}", [128, D], BF16)) for i in range(2)]
            vs = [cs.enter_context(nc.sbuf_tensor(f"vs{i}", [128, D], BF16)) for i in range(2)]
            ksr = cs.enter_context(nc.sbuf_tensor("ksr", [128, D], F32))
            kv16ctx = ExitStack()
            kv16 = kv16ctx.enter_context(nc.psum_tensor("kv16", [64, 1024], F32))
            p1ps = ExitStack()
            tp = [p1ps.enter_context(nc.psum_tensor(f"tp{i}", [128, D], BF16)) for i in range(2)]
            pk = p1ps.enter_context(nc.psum_tensor("pk", [128, D], F32))
            pv = p1ps.enter_context(nc.psum_tensor("pv", [128, D], F32))

            pipe.add("gpsimd", lambda E: E.dma_start(
                wk_s[:, :].rearrange("p (c j) -> p c j", j=D),
                wk_d.rearrange("(c p) j -> p c j", p=128)), name="ldwk", dma=True,
                stream="wk")
            pipe.add("gpsimd", lambda E: E.dma_start(
                wv_s[:, :].rearrange("p (c j) -> p c j", j=D),
                wv_d.rearrange("(c p) j -> p c j", p=128)), name="ldwv", dma=True,
                stream="wv")
            pipe.add("gpsimd", lambda E: E.memset(ksr[:, :], 0.0), name="ksz")

            def proj(E, lT, w_s, brow, ps):
                r = None
                for hf in (0, 1):
                    for c in range(8):
                        E.matmul(ps[:, hf * 512:(hf + 1) * 512],
                                 lT[:, c * 128:(c + 1) * 128],
                                 w_s[:, c * D + hf * 512:c * D + hf * 512 + 512],
                                 start=(c == 0), stop=False)
                    r = E.matmul(ps[:, hf * 512:(hf + 1) * 512], onesb[0:1, :],
                                 brow[0:1, hf * 512:(hf + 1) * 512],
                                 start=False, stop=True)
                return r

            def h1_vs(i):
                ip = i % 2
                pipe.add("sync",
                         lambda E, i=i, ip=ip: E.dma_start(
                             xb[ip][:, :], x_d[i * 128:(i + 1) * 128, :]),
                         waits=(f"C.{i-2}",) if i >= 2 else (),
                         name=f"LD.{i}", dma=True, stream=f"x{i % 2}")
                ln_head(pipe, "", i, xb[ip][:, 0:512], xb[ip][:, 512:1024],
                        st[ip], bnrow,
                        waits=(f"LD.{i}",) + ((f"P.{i-2}",) if i >= 2 else ()))

                def c_fn(E, ip=ip):
                    return E.tensor_scalar(xc[ip][:, :], xb[ip][:, :], st[ip][:, 2:3],
                                           None, ALU.add)
                pipe.add("vector", c_fn,
                         waits=(f"Q.{i}",) + ((f"P.{i-2}",) if i >= 2 else ()),
                         name=f"C.{i}")
                pipe.add("scalar",
                         lambda E, ip=ip: E.activation(lnx[ip][:, :], xc[ip][:, :],
                                                       AF.Identity,
                                                       scale=st[ip][:, 3:4]),
                         waits=(f"C.{i}",) + ((f"P1.{i-2}",) if i >= 2 else ()),
                         name=f"P.{i}")

            def h1_pe(i):
                ip = i % 2

                def p1_fn(E, ip=ip):
                    r = None
                    for c in range(8):
                        r = E.transpose(tp[ip][:, c * 128:(c + 1) * 128],
                                        lnx[ip][:, c * 128:(c + 1) * 128], identb[:, :])
                    return r
                pipe.add("tensor", p1_fn,
                         waits=(f"P.{i}",) + ((f"V3.{i-2}",) if i >= 2 else ()),
                         name=f"P1.{i}")

            def h2(i):
                ip = i % 2
                pipe.add("vector",
                         lambda E, ip=ip: E.tensor_copy(lnxT[ip][:, :], tp[ip][:, :]),
                         waits=(f"P1.{i}",) + ((f"P3.{i-2}",) if i >= 2 else ()),
                         name=f"V3.{i}")
                if i + 2 < NT:
                    h1_vs(i + 2)
                pipe.add("tensor", lambda E, ip=ip: proj(E, lnxT[ip], wk_s, bkb, pk),
                         waits=(f"V3.{i}",)
                         + ((f"V5.{i-1}",) if i >= 1 else ("ldwk",)),
                         name=f"P2.{i}")
                pipe.add("tensor", lambda E, ip=ip: proj(E, lnxT[ip], wv_s, bvb, pv),
                         waits=((f"S6.{i-1}",) if i >= 1 else ("ldwv",)),
                         name=f"P3.{i}")
                pipe.add("vector",
                         lambda E, ip=ip: E.tensor_scalar(t1[ip][:, :], pk[:, :], 0.0,
                                                          None, ALU.min),
                         waits=(f"P2.{i}",), name=f"V4.{i}")
                pipe.add("scalar",
                         lambda E, ip=ip: E.activation(t1[ip][:, :], t1[ip][:, :], AF.Exp),
                         waits=(f"V4.{i}",), name=f"S4.{i}")
                pipe.add("vector",
                         lambda E, ip=ip: E.scalar_tensor_tensor(
                             km[ip][:, :], pk[:, :], 0.0, t1[ip][:, :],
                             ALU.max, ALU.add),
                         waits=(f"S4.{i}",)
                         + ((f"P4.{i-2}",) if i >= 2 else ()),
                         name=f"V5.{i}")
                pipe.add("scalar",
                         lambda E, ip=ip: E.activation(vs[ip][:, :], pv[:, :], AF.Copy),
                         waits=(f"P3.{i}",) + ((f"P4.{i-2}",) if i >= 2 else ()),
                         name=f"S6.{i}")
                pipe.add("vector",
                         lambda E, ip=ip: E.tensor_add(ksr[:, :], ksr[:, :], km[ip][:, :]),
                         waits=(("ksz",) if i == 0 else ()), name=f"V6.{i}")

                def p4(E, i=i, ip=ip):
                    r = None
                    for h in range(16):
                        r = E.matmul(kv16[:, h * 64:(h + 1) * 64],
                                     km[ip][:, h * 64:(h + 1) * 64],
                                     vs[ip][:, h * 64:(h + 1) * 64],
                                     start=(i == 0 and h % 8 == 0),
                                     stop=(i == NT - 1 and h % 8 == 7))
                    return r
                pipe.add("tensor", p4, waits=(f"V5.{i}", f"S6.{i}"), name=f"P4.{i}")
                if i + 2 < NT:
                    h1_pe(i + 2)

            h1_vs(0)
            h1_pe(0)
            h1_vs(1)
            h1_pe(1)
            for i in range(NT):
                h2(i)

            pipe.build()
            p1ps.close()

            with ExitStack() as ce:
                pipe = Pipe(nc, ctx, "p1e")
                kvsb = ce.enter_context(nc.sbuf_tensor("kvsb", [64, 1024], BF16))
                ksrow = ce.enter_context(nc.sbuf_tensor("ksrow", [1, D], F32))
                pks = ce.enter_context(nc.psum_tensor("pks", [1, D], F32))
                pkb = ce.enter_context(nc.psum_tensor("pkb", [128, D], F32))
                pipe.add("vector", lambda E: E.tensor_copy(kvsb[:, :], kv16[:, :]),
                         name="evkv")
                pipe.add("gpsimd", lambda E: E.memset(kvblk[:, :], 0.0), name="kvz")
                for h in range(16):
                    p, odd = h // 2, h % 2
                    pipe.add("gpsimd",
                             lambda E, h=h, p=p, odd=odd: E.dma_start(
                                 kvblk[odd * 64:odd * 64 + 64,
                                       p * 128 + odd * 64:p * 128 + odd * 64 + 64],
                                 kvsb[0:64, h * 64:(h + 1) * 64]),
                             waits=("evkv", "kvz"), name=f"kvb{h}", dma=True, stream="kv")

                def cs_mm(E):
                    r = None
                    for hf in (0, 1):
                        r = E.matmul(pks[0:1, hf * 512:(hf + 1) * 512], onesfc[:, 0:1],
                                     ksr[:, hf * 512:(hf + 1) * 512], start=True, stop=True)
                    return r
                pipe.add("tensor", cs_mm, name="csm")
                pipe.add("vector", lambda E: E.tensor_copy(ksrow[:, :], pks[:, :]),
                         waits=("csm",), name="evks")

                def kb_mm(E):
                    r = None
                    for hf in (0, 1):
                        r = E.matmul(pkb[:, hf * 512:(hf + 1) * 512], onesfr[0:1, :],
                                     ksrow[0:1, hf * 512:(hf + 1) * 512],
                                     start=True, stop=True)
                    return r
                pipe.add("tensor", kb_mm, waits=("evks",), name="kbm")
                pipe.add("vector", lambda E: E.tensor_copy(ksumb[:, :], pkb[:, :]),
                         waits=("kbm",), name="evkb")
                pipe.add("gpsimd", lambda E: None, waits=("kvb15",))
                pipe.build()
            kv16ctx.close()
        if stop_after == "p1":
            return nc

        with ExitStack() as cs:
            pipe = Pipe(nc, ctx, "a")
            wq_s = cs.enter_context(nc.sbuf_tensor("wq_s", [128, 8 * D], BF16))
            wo_s = cs.enter_context(nc.sbuf_tensor("wo_s", [128, 8 * D], BF16))
            xb = [cs.enter_context(nc.sbuf_tensor(f"axb{i}", [128, D], F32)) for i in range(7)]
            st = [cs.enter_context(nc.sbuf_tensor(f"ast{i}", [128, 10], F32)) for i in range(2)]
            st2 = [cs.enter_context(nc.sbuf_tensor(f"ast2{i}", [128, 10], F32)) for i in range(2)]
            bnrow = cs.enter_context(nc.sbuf_tensor("abnrow", [128, 12], F32))
            bnrow2 = cs.enter_context(nc.sbuf_tensor("abnrow2", [128, 12], F32))
            xc = [cs.enter_context(nc.sbuf_tensor(f"axc{i}", [128, D], F32)) for i in range(2)]
            xc2 = cs.enter_context(nc.sbuf_tensor("axc2", [128, D], F32))
            lnx = [cs.enter_context(nc.sbuf_tensor(f"alnx{i}", [128, D], BF16)) for i in range(2)]
            lnxT = cs.enter_context(nc.sbuf_tensor("alnxT", [128, D], BF16))
            t1 = cs.enter_context(nc.sbuf_tensor("at1", [128, D], F32))
            qm = cs.enter_context(nc.sbuf_tensor("aqm", [128, D], F32))
            zden = cs.enter_context(nc.sbuf_tensor("azden", [128, 16], F32))
            qms = cs.enter_context(nc.sbuf_tensor("aqms", [128, D], BF16))
            qmsT = cs.enter_context(nc.sbuf_tensor("aqmsT", [128, 8 * 512], BF16))
            yT = cs.enter_context(nc.sbuf_tensor("ayT", [128, 8 * 512], BF16))
            ust = [cs.enter_context(nc.sbuf_tensor(f"aust{i}", [128, D], F32)) for i in range(2)]
            tp = [cs.enter_context(nc.psum_tensor(f"atp{i}", [128, D], BF16)) for i in range(2)]
            qp = cs.enter_context(nc.psum_tensor("aqp", [128, D], F32))
            yp = [cs.enter_context(nc.psum_tensor(f"ayp{i}", [128, 512], F32)) for i in range(2)]
            ao = cs.enter_context(nc.psum_tensor("aao", [128, D], F32))

            pipe.add("gpsimd", lambda E: E.dma_start(
                wq_s[:, :].rearrange("p (c j) -> p c j", j=D),
                wq_d.rearrange("(c p) j -> p c j", p=128)), name="ldwq", dma=True,
                stream="wq")
            pipe.add("gpsimd", lambda E: E.dma_start(
                wo_s[:, :].rearrange("p (c j) -> p c j", j=D),
                wo_d.rearrange("(c p) j -> p c j", p=128)), name="ldwo", dma=True,
                stream="wo")

            qmsTv = qmsT[:, :].rearrange("p (c t) -> p c t", t=512)

            aob = cs.enter_context(nc.sbuf_tensor("aaob", [128, D], F32))
            qb = [cs.enter_context(nc.sbuf_tensor(f"aqb{i}", [128, D], F32))
                  for i in range(2)]

            def h1a_vs(j):
                jp, j4 = j % 2, j % 7
                pipe.add("sync",
                         lambda E, j=j, j4=j4: E.dma_start(
                             xb[j4][:, :], x_d[j * 128:(j + 1) * 128, :]),
                         waits=(f"RC.{j-7}",) if j >= 7 else (),
                         name=f"LDX.{j}", dma=True, stream=f"x{j % 7}")
                ln_head(pipe, "", j, xb[j4][:, 0:512], xb[j4][:, 512:1024],
                        st[jp], bnrow,
                        waits=(f"LDX.{j}",) + ((f"P.{j-2}",) if j >= 2 else ()))

                def c_fn(E, jp=jp, j4=j4):
                    return E.tensor_scalar(xc[jp][:, :], xb[j4][:, :], st[jp][:, 2:3],
                                           None, ALU.add)
                pipe.add("vector", c_fn,
                         waits=(f"Q.{j}",) + ((f"P.{j-2}",) if j >= 2 else ()),
                         name=f"C.{j}")
                pipe.add("scalar",
                         lambda E, jp=jp: E.activation(lnx[jp][:, :], xc[jp][:, :],
                                                       AF.Identity,
                                                       scale=st[jp][:, 3:4]),
                         waits=(f"C.{j}",) + ((f"P1.{j-2}",) if j >= 2 else ()),
                         name=f"P.{j}")

            def h1a_pe(j):
                jp = j % 2

                def p1_fn(E, jp=jp):
                    r = None
                    for c in range(8):
                        r = E.transpose(tp[jp][:, c * 128:(c + 1) * 128],
                                        lnx[jp][:, c * 128:(c + 1) * 128],
                                        identb[:, :])
                    return r
                pipe.add("tensor", p1_fn,
                         waits=(f"P.{j}",) + ((f"V8.{j-2}",) if j >= 2 else ()),
                         name=f"P1.{j}")

            def h2a(j):
                g, s_ = j // 4, j % 4
                jp = j % 2
                pipe.add("vector",
                         lambda E, jp=jp: E.tensor_copy(lnxT[:, :], tp[jp][:, :]),
                         waits=(f"P1.{j}",) + ((f"P2.{j-1}",) if j >= 1 else ()),
                         name=f"V3.{j}")
                if j + 2 < S // 128:
                    h1a_vs(j + 2)
                pipe.add("tensor", lambda E: proj(E, lnxT, wq_s, bqb, qp),
                         waits=(f"V3.{j}",)
                         + ((f"SCQ.{j-1}",) if j >= 1 else ("ldwq",)),
                         name=f"P2.{j}")
                pipe.add("scalar",
                         lambda E, jp=jp: E.activation(qb[jp][:, :], qp[:, :], AF.Copy),
                         waits=(f"P2.{j}",) + ((f"V5.{j-2}",) if j >= 2 else ()),
                         name=f"SCQ.{j}")
                pipe.add("vector",
                         lambda E, jp=jp: E.tensor_scalar(t1[:, :], qb[jp][:, :], 0.0,
                                                          None, ALU.min),
                         waits=(f"SCQ.{j}",), name=f"V4.{j}")
                pipe.add("scalar",
                         lambda E: E.activation(t1[:, :], t1[:, :], AF.Exp),
                         waits=(f"V4.{j}",), name=f"S4.{j}")
                pipe.add("vector",
                         lambda E, jp=jp: E.scalar_tensor_tensor(
                             qm[:, :], qb[jp][:, :], 0.0, t1[:, :],
                             ALU.max, ALU.add),
                         waits=(f"S4.{j}",), name=f"V5.{j}")

                def v6(E):
                    E.tensor_mul(t1[:, :], qm[:, :], ksumb[:, :])
                    E.tensor_reduce(zden[:, :],
                                    t1[:, :].rearrange("p (h d) -> p h d", d=64),
                                    AX.X, ALU.add)
                    E.drain()
                    E.tensor_scalar(zden[:, :], zden[:, :], EPS_Z, None,
                                    ALU.add)
                    E.drain()
                    E.reciprocal(zden[:, :], zden[:, :])
                    return E.drain()
                pipe.add("vector", v6, name=f"V6.{j}")

                def v7(E):
                    r = None
                    for h in range(16):
                        r = E.tensor_scalar(qms[:, h * 64:(h + 1) * 64],
                                            qm[:, h * 64:(h + 1) * 64],
                                            zden[:, h:h + 1], None, ALU.mult)
                    return r
                pipe.add("vector", v7,
                         waits=((f"P3.{j-1}",) if j >= 1 else ()), name=f"V7.{j}")

                def p3(E, jp=jp):
                    r = None
                    for c in range(8):
                        r = E.transpose(tp[jp][:, c * 128:(c + 1) * 128],
                                        qms[:, c * 128:(c + 1) * 128],
                                        identb[:, :])
                    return r
                pipe.add("tensor", p3, waits=(f"V7.{j}", f"V3.{j}"), name=f"P3.{j}")
                pipe.add("vector",
                         lambda E, jp=jp, s_=s_: E.tensor_copy(
                             qmsTv[:, :, s_ * 128:(s_ + 1) * 128],
                             tp[jp][:, :].rearrange("p (c t) -> p c t", t=128)),
                         waits=(f"P3.{j}",)
                         + ((f"PY7.{g-1}",) if (s_ == 0 and g >= 1) else ()),
                         name=f"V8.{j}")
                if j + 2 < S // 128:
                    h1a_pe(j + 2)


            def ystage(g):
                for p in range(8):
                    pipe.add("tensor",
                             lambda E, p=p: E.matmul(
                                 yp[p % 2][:, :], kvblk[:, p * 128:(p + 1) * 128],
                                 qmsT[:, p * 512:(p + 1) * 512], start=True, stop=True),
                             waits=((f"V8.{g*4+3}",) if p == 0 else ())
                             + ((f"SY{p-2}.{g}",) if p >= 2 else ()),
                             name=f"PY{p}.{g}")
                    pipe.add("scalar",
                             lambda E, p=p: E.activation(
                                 yT[:, p * 512:(p + 1) * 512],
                                 yp[p % 2][:, :], AF.Copy),
                             waits=(f"PY{p}.{g}",)
                             + ((f"P5.{(g-1)*4+3}",) if (p == 0 and g >= 1) else ()),
                             name=f"SY{p}.{g}")

            def tail(j):
                g, s_ = j // 4, j % 4
                jp, j4 = j % 2, j % 7

                def p5(E, s_=s_):
                    r = None
                    for hf in (0, 1):
                        for c in range(8):
                            E.matmul(ao[:, hf * 512:(hf + 1) * 512],
                                     yT[:, c * 512 + s_ * 128:c * 512 + (s_ + 1) * 128],
                                     wo_s[:, c * D + hf * 512:c * D + hf * 512 + 512],
                                     start=(c == 0), stop=False)
                        r = E.matmul(ao[:, hf * 512:(hf + 1) * 512], onesb[0:1, :],
                                     bob[0:1, hf * 512:(hf + 1) * 512],
                                     start=False, stop=True)
                    return r
                pipe.add("tensor", p5,
                         waits=(f"SY7.{g}",)
                         + ((f"SC.{j-1}",) if j >= 1 else ("ldwo",)),
                         name=f"P5.{j}")
                pipe.add("scalar",
                         lambda E: E.activation(aob[:, :], ao[:, :], AF.Copy),
                         waits=(f"P5.{j}",) + ((f"RC.{j-1}",) if j >= 1 else ()),
                         name=f"SC.{j}")
                ln_head(pipe, "R", j, aob[:, 0:512], aob[:, 512:1024],
                        st2[jp], bnrow2,
                        waits=(f"SC.{j}",) + ((f"RC.{j-2}",) if j >= 2 else ()))

                def rc_fn(E, jp=jp, j4=j4):
                    E.tensor_scalar(xc2[:, :], aob[:, :], st2[jp][:, 2:3],
                                    None, ALU.add)
                    return E.scalar_tensor_tensor(ust[jp][:, :], xc2[:, :],
                                                  st2[jp][:, 3:4], xb[j4][:, :],
                                                  ALU.mult, ALU.add)
                pipe.add("vector", rc_fn,
                         waits=(f"RQ.{j}",) + ((f"STU.{j-2}",) if j >= 2 else ()),
                         name=f"RC.{j}")
                pipe.add("sync",
                         lambda E, j=j, jp=jp: E.dma_start(
                             u_d[j * 128:(j + 1) * 128, :], ust[jp][:, :]),
                         waits=(f"RC.{j}",), name=f"STU.{j}", dma=True,
                         stream=f"u{j % 2}")

            h1a_vs(0)
            h1a_pe(0)
            h1a_vs(1)
            h1a_pe(1)
            for g in range(NG):
                for s_ in range(4):
                    h2a(g * 4 + s_)
                ystage(g)
                for s_ in range(4):
                    tail(g * 4 + s_)

            pipe.add("sync", lambda E: None,
                     waits=(f"STU.{S//128-2}", f"STU.{S//128-1}"))
            pipe.build()
        if stop_after == "a":
            return nc

        with ExitStack() as cs:
            pipe = Pipe(nc, ctx, "b1")
            ub = [cs.enter_context(nc.sbuf_tensor(f"bub{i}", [128, D], F32)) for i in range(4)]
            st = [cs.enter_context(nc.sbuf_tensor(f"bst{i}", [128, 10], F32)) for i in range(4)]
            bnrow = [cs.enter_context(nc.sbuf_tensor(f"bbnrow{i}", [128, 12], F32))
                     for i in range(4)]
            xc = [cs.enter_context(nc.sbuf_tensor(f"bxc{i}", [128, D], F32)) for i in range(2)]
            t1 = cs.enter_context(nc.sbuf_tensor("bt1", [128, D], F32))
            xqg = [cs.enter_context(nc.sbuf_tensor(f"bxqg{i}", [128, D], BF16)) for i in range(4)]
            xqT = [cs.enter_context(nc.sbuf_tensor(f"bxqT{i}", [128, 8 * 512], BF16))
                   for i in range(2)]
            hst = [cs.enter_context(nc.sbuf_tensor(f"bhst{i}", [128, 2048], BF16))
                   for i in range(2)]
            tp = [cs.enter_context(nc.psum_tensor(f"btp{i}", [128, D], BF16)) for i in range(2)]
            hp = [cs.enter_context(nc.psum_tensor(f"bhp{i}", [128, 512], F32)) for i in range(4)]

            def front_vs(j):
                jq = j % 4
                jp = j % 2
                pipe.add("gpsimd",
                         lambda E, j=j, jq=jq: E.dma_start(
                             ub[jq][:, :], u_d[j * 128:(j + 1) * 128, :]),
                         waits=(f"C.{j-4}",) if j >= 4 else (),
                         name=f"LDU.{j}", dma=True, stream=f"u{j % 4}")
                ln_head(pipe, "", j, ub[jq][:, 0:512], ub[jq][:, 512:1024],
                        st[jq], bnrow[jq],
                        waits=(f"LDU.{j}",) + ((f"V5.{j-4}",) if j >= 4 else ()))

                def c_fn(E, jp=jp, jq=jq):
                    E.tensor_scalar(xc[jp][:, :], ub[jq][:, :], st[jq][:, 2:3],
                                    None, ALU.add)
                    E.reduce_max(st[jq][:, 5:6], xc[jp][:, :], AX.X,
                                 apply_absolute_value=True)
                    return E.drain()
                pipe.add("vector", c_fn,
                         waits=(f"Q.{j}",) + ((f"V5.{j-2}",) if j >= 2 else ()),
                         name=f"C.{j}")

                def v4(E, jq=jq):
                    E.tensor_scalar(st[jq][:, 6:7], st[jq][:, 5:6], st[jq][:, 3:4],
                                    None, ALU.mult)
                    E.drain()
                    E.tensor_scalar(st[jq][:, 6:7], st[jq][:, 6:7], QEPS,
                                    1.0 / 127.0, ALU.max, ALU.mult)
                    E.drain()
                    E.reciprocal(st[jq][:, 7:8], st[jq][:, 6:7])
                    E.tensor_scalar(st[jq][:, 8:9], st[jq][:, 6:7], wscal[:, 0:1],
                                    None, ALU.mult)
                    E.drain()
                    E.tensor_scalar(st[jq][:, 9:10], st[jq][:, 7:8], st[jq][:, 3:4],
                                    None, ALU.mult)
                    return E.drain()
                pipe.add("vector", v4, name=f"V4.{j}")

                def v5(E, jp=jp, jq=jq, j4=j % 4):
                    E.tensor_scalar(t1[:, :], xc[jp][:, :], st[jq][:, 9:10], CRND,
                                    ALU.mult, ALU.add)
                    return E.tensor_scalar(xqg[j4][:, :], t1[:, :], CRND,
                                           st[jq][:, 8:9],
                                           ALU.subtract, ALU.mult)
                pipe.add("vector", v5,
                         waits=((f"P1.{j-4}",) if j >= 4 else ()), name=f"V5.{j}")

            def front_pe(j):
                jp, j4 = j % 2, j % 4
                gp = (j // 4) % 2
                s_ = j % 4

                def p1_fn(E, j4=j4, jp=jp):
                    r = None
                    for c in range(8):
                        r = E.transpose(tp[jp][:, c * 128:(c + 1) * 128],
                                        xqg[j4][:, c * 128:(c + 1) * 128],
                                        identb[:, :])
                    return r
                pipe.add("tensor", p1_fn,
                         waits=(f"V5.{j}",) + ((f"V6.{j-2}",) if j >= 2 else ()),
                         name=f"P1.{j}")
                pipe.add("vector",
                         lambda E, jp=jp, gp=gp, s_=s_: E.tensor_copy(
                             xqT[gp][:, :].rearrange("p (c t) -> p c t", t=512)
                             [:, :, s_ * 128:(s_ + 1) * 128],
                             tp[jp][:, :].rearrange("p (c t) -> p c t", t=128)),
                         waits=(f"P1.{j}",)
                         + ((f"PW31.{(j//4)-2}",) if (s_ == 0 and j >= 8) else ()),
                         name=f"V6.{j}")

            for s_ in range(4):
                front_vs(s_)
            for s_ in range(4):
                front_pe(s_)
            pe_at = {8: 0, 16: 1, 24: 2, 28: 3}
            for g in range(NG):
                gp = g % 2
                if g + 1 < NG:
                    for k in range(4):
                        front_vs(4 * (g + 1) + k)
                for c in range(32):
                    F = g * 8 + c // 4

                    def pw(E, c=c, gp=gp, g=g):
                        if g == 0 and c < 4:
                            E.wait_ge(su_we[0], su_we[1])
                        r = None
                        for cf in range(8):
                            r = E.matmul(hp[c % 4][:, :],
                                         w1qT[:, cf * DF + c * 128:cf * DF + (c + 1) * 128],
                                         xqT[gp][:, cf * 512:(cf + 1) * 512],
                                         start=(cf == 0), stop=(cf == 7))
                        return r
                    pipe.add("tensor", pw,
                             waits=(f"V6.{g*4+3}",)
                             + ((f"SE{c-4}.{g}",) if c >= 4 else
                                ((f"SE{28+c}.{g-1}",) if g >= 1 else ())),
                             name=f"PW{c}.{g}")
                    pipe.add("scalar",
                             lambda E, c=c: E.activation(
                                 hst[(c // 4) % 2][:, (c % 4) * 512:(c % 4 + 1) * 512],
                                 hp[c % 4][:, :], AF.Copy),
                             waits=(f"PW{c}.{g}",)
                             + ((f"STH.{F-2}",) if (c % 4 == 0 and F >= 2) else ()),
                             name=f"SE{c}.{g}")
                    if c % 4 == 3:
                        pipe.add("sync",
                                 lambda E, c=c, g=g: E.dma_start(
                                     hview[:, c - 3:c + 1, g * 512:(g + 1) * 512],
                                     hst[(c // 4) % 2][:, :].rearrange(
                                         "p (s t) -> p s t", t=512)),
                                 waits=(f"SE{c}.{g}", f"SE{c-1}.{g}",
                                        f"SE{c-2}.{g}", f"SE{c-3}.{g}"),
                                 name=f"STH.{F}", dma=True, stream=f"h{F % 2}")
                    if c in pe_at and g + 1 < NG:
                        front_pe(4 * (g + 1) + pe_at[c])
            pipe.add("sync", lambda E: None,
                     waits=(f"STH.{NG*8-2}", f"STH.{NG*8-1}"))
            pipe.build()
        if stop_after == "b1":
            return nc
        ctxA.close()

        with ExitStack() as cs:
            pipe = Pipe(nc, ctx, "b2")
            w2T = cs.enter_context(nc.sbuf_tensor("w2T", [128, 32 * D], BF16))
            w2p = [cs.enter_context(nc.sbuf_tensor(f"w2p{i}", [128, 512], BF16))
                   for i in range(4)]
            hload = cs.enter_context(nc.sbuf_tensor("hload", [128, 8 * 514], BF16))
            hsil = cs.enter_context(nc.sbuf_tensor("hsil", [128, 32 * 512], BF16))
            cvt = [cs.enter_context(nc.sbuf_tensor(f"cvt{i}", [128, 512], BF16))
                   for i in range(4)]
            tqf = [cs.enter_context(nc.sbuf_tensor(f"tqf{i}", [128, 512], F32))
                   for i in range(2)]
            tq2 = [cs.enter_context(nc.sbuf_tensor(f"tq2_{i}", [128, 512], F32))
                   for i in range(4)]
            crndb = cs.enter_context(nc.sbuf_tensor("crndb", [128, 1], F32))
            ncrndb = cs.enter_context(nc.sbuf_tensor("ncrndb", [128, 1], F32))
            rmax = cs.enter_context(nc.sbuf_tensor("rmax", [128, 512], BF16))
            rmin = cs.enter_context(nc.sbuf_tensor("rmin", [128, 512], BF16))
            s2st = cs.enter_context(nc.sbuf_tensor("s2st", [128, 8], F32))
            s2row = cs.enter_context(nc.sbuf_tensor("s2row", [1, 512], F32))
            s2b = cs.enter_context(nc.sbuf_tensor("s2b", [128, 512], BF16))
            hqf = cs.enter_context(nc.sbuf_tensor("hqf", [128, 32 * 512], BF16))
            st2 = [cs.enter_context(nc.sbuf_tensor(f"cst{i}", [128, 10], F32)) for i in range(2)]
            bnrow2 = cs.enter_context(nc.sbuf_tensor("cbnrow", [128, 12], F32))
            xc2 = cs.enter_context(nc.sbuf_tensor("cxc2", [128, D], F32))
            ub = [cs.enter_context(nc.sbuf_tensor(f"cub{i}", [128, D], F32)) for i in range(2)]
            ost = [cs.enter_context(nc.sbuf_tensor(f"cost{i}", [128, D], F32)) for i in range(2)]
            oa = cs.enter_context(nc.psum_tensor("coa", [128, D], F32))
            ob = cs.enter_context(nc.psum_tensor("cob", [128, D], F32))
            tps = cs.enter_context(nc.psum_tensor("ctps", [128, 512], BF16))
            tpf = cs.enter_context(nc.psum_tensor("ctpf", [1, 512], F32))
            s2p = cs.enter_context(nc.psum_tensor("cs2p", [128, 512], F32))
            w2tp = cs.enter_context(nc.psum_tensor("cw2tp", [128, 512], BF16))

            hlv = hload[:, :].rearrange("p (c t) -> p c t", t=514)
            pipe.add("gpsimd",
                     lambda E: E.memset(hlv[:, :, 0:1], 0.0), name="hz0")
            pipe.add("gpsimd", lambda E: E.memset(crndb[:, :], CRND), name="crC")
            pipe.add("gpsimd", lambda E: E.memset(ncrndb[:, :], -CRND), name="crN")

            def w2_load(i):
                ccb, fb = i // 8, i % 8

                def ld(E, ccb=ccb, fb=fb, i=i):
                    if i == 0:
                        E.wait_ge(su_ts30[0], su_ts30[1])
                        E.wait_ge(su_ts31[0], su_ts31[1])
                    return E.dma_start(
                        w2p[i % 4][:, :],
                        w2q_d[fb * 128:(fb + 1) * 128,
                              ccb * 512:(ccb + 1) * 512])
                pipe.add("gpsimd", ld,
                         waits=(f"W2T{i-4}",) if i >= 4 else (),
                         name=f"w2pl{i}", dma=True, stream=f"w2p{i % 4}")

            def w2_tr_ev(i):
                ccb, fb = i // 8, i % 8

                def tr(E, i=i):
                    r = None
                    for k in range(4):
                        r = E.transpose(
                            w2tp[:, k * 128:(k + 1) * 128],
                            w2p[i % 4][:, k * 128:(k + 1) * 128], identb[:, :])
                    return r
                pipe.add("tensor", tr,
                         waits=(f"w2pl{i}",)
                         + ((f"W2E{i-1}",) if i >= 1 else ()),
                         name=f"W2T{i}")

                def ev(E, ccb=ccb, fb=fb, i=i):
                    r = None
                    for k in range(4):
                        cc = ccb * 4 + k
                        r = E.activation(
                            w2T[:, cc * D + fb * 128:cc * D + (fb + 1) * 128],
                            w2tp[:, k * 128:(k + 1) * 128], AF.Copy)
                    return r
                pipe.add("scalar", ev, waits=(f"W2T{i}",), name=f"W2E{i}")

            def conv_one(g, c):
                cs8 = c % 8
                lhw = []
                if c >= 8:
                    lhw.append(f"CV{c-8}.{g}")
                elif g >= 1:
                    lhw.append(f"CV{c+24}.{g-1}")
                if g == 0:
                    pipe.add("sync",
                             lambda E, c=c, cs8=cs8: E.dma_start(
                                 hlv[:, cs8:cs8 + 1, 1:514],
                                 hview[:, c:c + 1, 0:513]),
                             waits=tuple(lhw), name=f"LH{c}.{g}", dma=True,
                             stream=f"lh{c % 8}")
                elif g == NG - 1:
                    pipe.add("sync",
                             lambda E, c=c, g=g, cs8=cs8: E.dma_start(
                                 hlv[:, cs8:cs8 + 1, 0:513],
                                 hview[:, c:c + 1, g * 512 - 1:S]),
                             waits=tuple(lhw), name=f"LH{c}.{g}", dma=True,
                             stream=f"lh{c % 8}")
                else:
                    pipe.add("sync",
                             lambda E, c=c, g=g, cs8=cs8: E.dma_start(
                                 hlv[:, cs8:cs8 + 1, :],
                                 hview[:, c:c + 1, g * 512 - 1:g * 512 + 513]),
                             waits=tuple(lhw), name=f"LH{c}.{g}", dma=True,
                             stream=f"lh{c % 8}")

                def cv(E, c=c, cs8=cs8):
                    E.tensor_scalar(cvt[c % 4][:, :], hlv[:, cs8, 0:512],
                                    cw0s[:, c:c + 1], None, ALU.mult)
                    E.scalar_tensor_tensor(cvt[c % 4][:, :], hlv[:, cs8, 1:513],
                                           cw1s[:, c:c + 1], cvt[c % 4][:, :],
                                           ALU.mult, ALU.add)
                    return E.scalar_tensor_tensor(cvt[c % 4][:, :], hlv[:, cs8, 2:514],
                                                  cw2s[:, c:c + 1], cvt[c % 4][:, :],
                                                  ALU.mult, ALU.add)
                waits = [f"LH{c}.{g}"]
                if c >= 4:
                    waits.append(f"SL{c-4}.{g}")
                elif g >= 1:
                    waits.append(f"SL{28+c}.{g-1}")
                if g == 0:
                    waits.append("hz0")
                if g == NG - 1:
                    waits.append("hz15")
                pipe.add("vector", cv, waits=tuple(waits), name=f"CV{c}.{g}")

                def sl(cc):
                    pipe.add("scalar",
                             lambda E, cc=cc: E.activation(
                                 hsil[:, cc * 512:(cc + 1) * 512], cvt[cc % 4][:, :],
                                 AF.Silu, bias=cbs[:, cc:cc + 1]),
                             waits=(f"CV{cc}.{g}",)
                             + ((f"HQ{cc}.{g-1}",) if g >= 1 else ()),
                             name=f"SL{cc}.{g}")

                def mxmn(cc):
                    if cc == 0:
                        pipe.add("vector",
                                 lambda E: E.tensor_copy(rmax[:, :], hsil[:, 0:512]),
                                 waits=(f"SL0.{g}",)
                                 + ((f"TRX.{g-1}",) if g >= 1 else ()),
                                 name=f"MX0.{g}")
                        pipe.add("vector",
                                 lambda E: E.tensor_copy(rmin[:, :], hsil[:, 0:512]),
                                 waits=(f"SL0.{g}",)
                                 + ((f"TRN.{g-1}",) if g >= 1 else ()),
                                 name=f"MN0.{g}")
                    else:
                        pipe.add("vector",
                                 lambda E, cc=cc: E.tensor_max(
                                     rmax[:, :], rmax[:, :],
                                     hsil[:, cc * 512:(cc + 1) * 512]),
                                 waits=(f"SL{cc}.{g}",), name=f"MX{cc}.{g}")
                        pipe.add("vector",
                                 lambda E, cc=cc: E.tensor_tensor(
                                     rmin[:, :], rmin[:, :],
                                     hsil[:, cc * 512:(cc + 1) * 512],
                                     ALU.min),
                                 waits=(f"SL{cc}.{g}",), name=f"MN{cc}.{g}")
                if c >= 1:
                    sl(c - 1)
                if c >= 2:
                    mxmn(c - 2)
                if c == 31:
                    sl(31)
                    mxmn(30)
                    mxmn(31)

            def s2stage(g):
                def trx(E):
                    r = None
                    for b in range(4):
                        r = E.transpose(tps[:, b * 128:(b + 1) * 128],
                                        rmax[:, b * 128:(b + 1) * 128], identb[:, :])
                    return r
                pipe.add("tensor", trx, waits=(f"MX31.{g}",), name=f"TRX.{g}")
                pipe.add("vector",
                         lambda E: E.reduce_max(
                             s2st[:, 0:4],
                             tps[:, :].rearrange("p (b t) -> p b t", t=128), AX.X),
                         waits=(f"TRX.{g}",), name=f"VRX.{g}")

                def trn_(E):
                    r = None
                    for b in range(4):
                        r = E.transpose(tps[:, b * 128:(b + 1) * 128],
                                        rmin[:, b * 128:(b + 1) * 128], identb[:, :])
                    return r
                pipe.add("tensor", trn_, waits=(f"MN31.{g}", f"VRX.{g}"),
                         name=f"TRN.{g}")

                def vrn(E):
                    E.tensor_reduce(s2st[:, 4:8],
                                    tps[:, :].rearrange("p (b t) -> p b t", t=128),
                                    AX.X, ALU.min)
                    E.drain()
                    E.tensor_scalar(s2st[:, 4:8], s2st[:, 4:8], -1.0, None, ALU.mult)
                    E.drain()
                    E.tensor_tensor(s2st[:, 0:4], s2st[:, 0:4], s2st[:, 4:8], ALU.max)
                    E.drain()
                    E.tensor_scalar(s2st[:, 0:4], s2st[:, 0:4], QEPS, 1.0 / 127.0,
                                    ALU.max, ALU.mult)
                    E.drain()
                    E.reciprocal(s2st[:, 0:4], s2st[:, 0:4])
                    return E.drain()
                pipe.add("vector", vrn, waits=(f"TRN.{g}",), name=f"VRN.{g}")

                def trs(E):
                    r = None
                    for b in range(4):
                        r = E.matmul(tpf[0:1, b * 128:(b + 1) * 128],
                                     s2st[:, b:b + 1], identf[:, :],
                                     start=(b == 0), stop=(b == 3))
                    return r
                pipe.add("tensor", trs, waits=(f"VRN.{g}",), name=f"TRS.{g}")
                pipe.add("vector", lambda E: E.tensor_copy(s2row[:, :], tpf[:, :]),
                         waits=(f"TRS.{g}",), name=f"EVS.{g}")
                pipe.add("tensor",
                         lambda E: E.matmul(s2p[:, :], onesfr[0:1, :], s2row[0:1, :],
                                            start=True, stop=True),
                         waits=(f"EVS.{g}",), name=f"S2M.{g}")
                pipe.add("vector", lambda E: E.tensor_copy(s2b[:, :], s2p[:, :]),
                         waits=(f"S2M.{g}",), name=f"S2D.{g}")

            def hq_one(g, c):
                def vhq(E, c=c):
                    return E.tensor_mul(tqf[c % 2][:, :],
                                        hsil[:, c * 512:(c + 1) * 512],
                                        s2b[:, :])
                w = [f"S2D.{g}"]
                if c >= 2:
                    w.append(f"RD1_{c-2}.{g}")
                elif g >= 1:
                    w.append(f"RD1_{30+c}.{g-1}")
                pipe.add("vector", vhq, waits=tuple(w), name=f"HQ{c}.{g}")
                pipe.add("scalar",
                         lambda E, c=c: E.activation(
                             tq2[c % 4][:, :], tqf[c % 2][:, :],
                             AF.Identity, bias=crndb[:, :]),
                         waits=(f"HQ{c}.{g}",)
                         + (("crC", "crN") if (g == 0 and c == 0) else ()),
                         name=f"RD1_{c}.{g}")

            def hq_rd2(g, c):
                pipe.add("scalar",
                         lambda E, c=c: E.activation(
                             hqf[:, c * 512:(c + 1) * 512], tq2[c % 4][:, :],
                             AF.Identity, bias=ncrndb[:, :]),
                         waits=(f"PHB{c}.{g-1}",) if g >= 1 else (),
                         name=f"RD2_{c}.{g}")

            def passmm(g, hf):
                tag = "AB"[hf]
                if hf == 0:
                    hq_one(g, 0)
                    hq_one(g, 1)
                for c in range(32):
                    if hf == 0:
                        if c + 2 < 32:
                            hq_one(g, c + 2)
                        hq_rd2(g, c)
                    def ph(E, c=c, hf=hf):
                        r = None
                        for si in (0, 1):
                            ps = oa if si == 0 else ob
                            for oh in (0, 1):
                                r = E.matmul(ps[:, oh * 512:(oh + 1) * 512],
                                             hqf[:, c * 512 + hf * 256 + si * 128:
                                                 c * 512 + hf * 256 + (si + 1) * 128],
                                             w2T[:, c * D + oh * 512:c * D + oh * 512 + 512],
                                             start=(c == 0), stop=(c == 31))
                        return r
                    w = [f"RD2_{c}.{g}"]
                    if c == 0:
                        if hf == 1:
                            w.append(f"RC.{g*4+1}")
                        elif g >= 1:
                            w.append(f"RC.{(g-1)*4+3}")
                    if g == 0 and hf == 0:
                        w.append(f"W2E{(c // 4) * 8 + 7}")
                    pipe.add("tensor", ph, waits=tuple(w), name=f"PH{tag}{c}.{g}")

            def ln4(g, hf):
                tag = "AB"[hf]
                for si in (0, 1):
                    j = g * 4 + hf * 2 + si
                    jp = j % 2
                    ps = oa if si == 0 else ob
                    pipe.add("sync",
                             lambda E, j=j, jp=jp: E.dma_start(
                                 ub[jp][:, :], u_d[j * 128:(j + 1) * 128, :]),
                             waits=(f"RC.{j-2}",) if j >= 2 else (),
                             name=f"LDU.{j}", dma=True, stream=f"u{j % 2}")
                    ln_head(pipe, "R", j, ps[:, 0:512], ps[:, 512:1024],
                            st2[jp], bnrow2,
                            waits=(f"PH{tag}31.{g}",)
                            + ((f"RC.{j-2}",) if j >= 2 else ()))

                    def rc_fn(E, jp=jp, ps=ps):
                        E.tensor_scalar(xc2[:, :], ps[:, :], st2[jp][:, 2:3],
                                        None, ALU.add)
                        return E.scalar_tensor_tensor(ost[jp][:, :], xc2[:, :],
                                                      st2[jp][:, 3:4], ub[jp][:, :],
                                                      ALU.mult, ALU.add)
                    pipe.add("vector", rc_fn,
                             waits=(f"RQ.{j}", f"LDU.{j}")
                             + ((f"STO.{j-2}",) if j >= 2 else ()),
                             name=f"RC.{j}")
                    pipe.add("sync",
                             lambda E, j=j, jp=jp: E.dma_start(
                                 out_d[j * 128:(j + 1) * 128, :], ost[jp][:, :]),
                             waits=(f"RC.{j}",), name=f"STO.{j}", dma=True,
                             stream=f"o{j % 2}")

            for c in range(32):
                conv_one(0, c)
                w2_load(2 * c)
                w2_load(2 * c + 1)
                w2_tr_ev(2 * c)
                w2_tr_ev(2 * c + 1)
            for g in range(NG):
                if g + 1 == NG - 1:
                    pipe.add("gpsimd", lambda E: E.memset(hlv[:, :, 513:514], 0.0),
                             waits=(f"CV31.{NG-2}",), name="hz15")
                s2stage(g)
                passmm(g, 0)
                if g + 1 < NG:
                    for c in range(16):
                        conv_one(g + 1, c)
                ln4(g, 0)
                passmm(g, 1)
                if g + 1 < NG:
                    for c in range(16, 32):
                        conv_one(g + 1, c)
                ln4(g, 1)
            pipe.add("sync", lambda E: None,
                     waits=(f"STO.{S//128-2}", f"STO.{S//128-1}"))
            pipe.build()

    return nc


_NC_CACHE = {}


def kernel(L0_attn_x, L0_attn_wq, L0_attn_bq, L0_attn_wk, L0_attn_bk,
           L0_attn_wv, L0_attn_bv, L0_attn_wo, L0_attn_bo,
           L0_ffn_w1, L0_ffn_conv_w, L0_ffn_conv_b, L0_ffn_w2):
    x = np.asarray(L0_attn_x, dtype=np.float32)
    B = x.shape[0]
    assert x.shape == (B, S, D)

    def f32(a, shape):
        return np.ascontiguousarray(np.asarray(a, dtype=np.float32).reshape(shape))

    cw = f32(L0_ffn_conv_w, (32, 128, 3))
    common = {
        "wq": f32(L0_attn_wq, (D, D)), "wk": f32(L0_attn_wk, (D, D)),
        "wv": f32(L0_attn_wv, (D, D)), "wo": f32(L0_attn_wo, (D, D)),
        "bq": f32(L0_attn_bq, (1, D)), "bk": f32(L0_attn_bk, (1, D)),
        "bv": f32(L0_attn_bv, (1, D)), "bo": f32(L0_attn_bo, (1, D)),
        "w1": f32(L0_ffn_w1, (DF, D)), "w2": f32(L0_ffn_w2, (D, DF)),
        "cw0": np.ascontiguousarray(cw[:, :, 0].T),
        "cw1": np.ascontiguousarray(cw[:, :, 1].T),
        "cw2": np.ascontiguousarray(cw[:, :, 2].T),
        "cb": np.ascontiguousarray(f32(L0_ffn_conv_b, (32, 128)).T),
        "identb": np.eye(128, dtype=bfd),
        "identf": np.eye(128, dtype=np.float32),
        "onesb": np.ones((1, 128), dtype=bfd),
        "onesfr": np.ones((1, 128), dtype=np.float32),
        "onesfc": np.ones((128, 1), dtype=np.float32),
    }
    if "prog" not in _NC_CACHE:
        _NC_CACHE["prog"] = build_program()
    nc = _NC_CACHE["prog"]
    in_maps = [dict(common, x=np.ascontiguousarray(x[n])) for n in range(B)]
    res = run_bass_kernel_spmd(nc, in_maps, core_ids=list(range(B)))
    return np.stack([np.asarray(r["out"]) for r in res.results], axis=0)


# version 18
# speedup vs baseline: 1.0248x; 1.0248x over previous
import numpy as np
import ml_dtypes
from contextlib import ExitStack

import concourse.bass as bass
import concourse.mybir as mybir
from concourse.bass_utils import run_bass_kernel_spmd

F32 = mybir.dt.float32
BF16 = mybir.dt.bfloat16
AF = mybir.ActivationFunctionType
ALU = mybir.AluOpType
AX = mybir.AxisListType
bfd = ml_dtypes.bfloat16

S, D, H, HD, DF = 8192, 1024, 16, 64, 4096
NT = S // 128
NG = S // 512
CRND = 12582912.0
EPS_LN = 1e-5
EPS_Z = 1e-6
QEPS = 1e-5
NW1 = float(DF * D)

ENGINES = ("sync", "vector", "scalar", "tensor", "gpsimd")
_SU_MODE = 3


class _Op:
    __slots__ = ("eng", "fn", "waits", "name", "dma", "stream", "inc", "val", "gidx")


class Pipe:

    def __init__(self, nc, ctx, tag):
        self.nc = nc
        self.ctx = ctx
        self.tag = tag
        self.ops = []
        self.byname = {}
        self.sems = {}

    def _sem(self, eng, stream):
        key = (eng, stream)
        if key not in self.sems:
            name = f"{self.tag}_{eng}_{stream}" if stream else f"{self.tag}_{eng}"
            self.sems[key] = self.ctx.enter_context(self.nc.semaphore(name))
        return self.sems[key]

    def add(self, eng, fn, waits=(), name=None, dma=False, stream=None):
        op = _Op()
        op.eng, op.fn, op.name, op.dma = eng, fn, name, dma
        op.stream = (stream or "d") if dma else None
        op.waits = tuple(w for w in waits if w is not None)
        op.inc = False
        op.val = None
        op.gidx = len(self.ops)
        self.ops.append(op)
        if name is not None:
            assert name not in self.byname, name
            self.byname[name] = op
        return op

    def build(self):
        for op in self.ops:
            if op.dma:
                op.inc = True
            for w in op.waits:
                dep = self.byname[w]
                assert dep.gidx < op.gidx, (op.name, w)
                dep.inc = True
        counts = {}
        for op in self.ops:
            if op.inc:
                k = (op.eng, op.stream)
                counts[k] = counts.get(k, 0) + (16 if op.dma else 1)
                op.val = counts[k]
                self._sem(op.eng, op.stream)
        per_eng = {}
        for op in self.ops:
            per_eng.setdefault(op.eng, []).append(op)
        with self.nc.Block() as block:
            for eng in ENGINES:
                if eng not in per_eng:
                    continue
                ops = per_eng[eng]

                def mk(ops=ops):
                    def emit(E):
                        for op in ops:
                            for w in op.waits:
                                dep = self.byname[w]
                                E.wait_ge(self._sem(dep.eng, dep.stream), dep.val)
                            instr = op.fn(E)
                            if op.inc:
                                assert instr is not None, op.name
                                instr.then_inc(self._sem(op.eng, op.stream),
                                               16 if op.dma else 1)
                    return emit

                getattr(block, eng)(mk())


def build_program(dump_internal=False, stop_after=None):
    nc = bass.Bass("TRN2", target_bir_lowering=False, detect_race_conditions=False)

    x_d = nc.dram_tensor("x", [S, D], F32, kind="ExternalInput").ap()
    wq_d = nc.dram_tensor("wq", [D, D], F32, kind="ExternalInput").ap()
    wk_d = nc.dram_tensor("wk", [D, D], F32, kind="ExternalInput").ap()
    wv_d = nc.dram_tensor("wv", [D, D], F32, kind="ExternalInput").ap()
    wo_d = nc.dram_tensor("wo", [D, D], F32, kind="ExternalInput").ap()
    bq_d = nc.dram_tensor("bq", [1, D], F32, kind="ExternalInput").ap()
    bk_d = nc.dram_tensor("bk", [1, D], F32, kind="ExternalInput").ap()
    bv_d = nc.dram_tensor("bv", [1, D], F32, kind="ExternalInput").ap()
    bo_d = nc.dram_tensor("bo", [1, D], F32, kind="ExternalInput").ap()
    w1_d = nc.dram_tensor("w1", [DF, D], F32, kind="ExternalInput").ap()
    w2_d = nc.dram_tensor("w2", [D, DF], F32, kind="ExternalInput").ap()
    cw0_d = nc.dram_tensor("cw0", [128, 32], F32, kind="ExternalInput").ap()
    cw1_d = nc.dram_tensor("cw1", [128, 32], F32, kind="ExternalInput").ap()
    cw2_d = nc.dram_tensor("cw2", [128, 32], F32, kind="ExternalInput").ap()
    cb_d = nc.dram_tensor("cb", [128, 32], F32, kind="ExternalInput").ap()
    idb_d = nc.dram_tensor("identb", [128, 128], BF16, kind="ExternalInput").ap()
    idf_d = nc.dram_tensor("identf", [128, 128], F32, kind="ExternalInput").ap()
    ob_d = nc.dram_tensor("onesb", [1, 128], BF16, kind="ExternalInput").ap()
    ofr_d = nc.dram_tensor("onesfr", [1, 128], F32, kind="ExternalInput").ap()
    ofc_d = nc.dram_tensor("onesfc", [128, 1], F32, kind="ExternalInput").ap()

    out_d = nc.dram_tensor("out", [S, D], F32, kind="ExternalOutput").ap()
    ikind = "ExternalOutput" if dump_internal else "Internal"
    u_d = nc.dram_tensor("u_scr", [S, D], F32, kind=ikind).ap()
    h_d = nc.dram_tensor("h_scr", [DF, S], BF16, kind=ikind).ap()
    w1q_d = nc.dram_tensor("w1q_scr", [DF, D], BF16, kind=ikind).ap()
    w2q_d = nc.dram_tensor("w2q_scr", [D, DF], BF16, kind=ikind).ap()

    hview = h_d.rearrange("(c p) t -> p c t", p=128)
    w1qv = w1q_d.rearrange("(c p) j -> p c j", p=128)

    ctx = ExitStack()
    with ctx:
        def sb(name, shape, dt):
            return ctx.enter_context(nc.sbuf_tensor(name, shape, dt))

        identb = sb("identb_s", [128, 128], BF16)
        identf = sb("identf_s", [128, 128], F32)
        onesb = sb("onesb_s", [1, 128], BF16)
        onesfr = sb("onesfr_s", [1, 128], F32)
        onesfc = sb("onesfc_s", [128, 1], F32)
        bqb = sb("bqb", [1, D], BF16)
        bkb = sb("bkb", [1, D], BF16)
        bvb = sb("bvb", [1, D], BF16)
        bob = sb("bob", [1, D], BF16)
        cw0s = sb("cw0s", [128, 32], F32)
        cw1s = sb("cw1s", [128, 32], F32)
        cw2s = sb("cw2s", [128, 32], F32)
        cbs = sb("cbs", [128, 32], F32)
        wscal = sb("wscal", [128, 4], F32)
        epsln = sb("epsln", [128, 1], F32)

        ctxA = ctx.enter_context(ExitStack())
        w1qT = ctxA.enter_context(nc.sbuf_tensor("w1qT", [128, 8 * DF], BF16))
        kvblk = ctxA.enter_context(nc.sbuf_tensor("kvblk", [128, 8 * 128], BF16))
        ksumb = ctxA.enter_context(nc.sbuf_tensor("ksumb", [128, D], F32))

        def ln_head(pipe, pfx, j, src0, src1, st, bnrow, waits):
            def a_fn(E, src0=src0, src1=src1, st=st, bnrow=bnrow):
                E.bn_stats(bnrow[:, 0:6], src0)
                E.bn_stats(bnrow[:, 6:12], src1)
                E.drain()
                E.bn_aggr(st[:, 0:2], bnrow[:, :])
                E.drain()
                E.tensor_scalar(st[:, 2:3], st[:, 0:1], -1.0, None, ALU.mult)
                return E.drain()
            pipe.add("vector", a_fn, waits=waits, name=f"{pfx}A.{j}")

            def q_fn(E, st=st):
                E.activation(st[:, 4:5], st[:, 1:2], AF.Ln, bias=epsln[:, :])
                E.drain()
                return E.activation(st[:, 3:4], st[:, 4:5], AF.Exp, scale=-0.5)
            pipe.add("scalar", q_fn, waits=(f"{pfx}A.{j}",), name=f"{pfx}Q.{j}")

        with ExitStack() as cs:
            pipe = Pipe(nc, ctx, "su")
            wst = [cs.enter_context(nc.sbuf_tensor(f"wst{i}", [128, 2048], F32))
                   for i in range(2)]
            wtf = [cs.enter_context(nc.sbuf_tensor(f"wtf{i}", [128, 2048], F32))
                   for i in range(2)]
            wqst = [cs.enter_context(nc.sbuf_tensor(f"wqst{i}", [128, 2048], BF16))
                    for i in range(2)]
            wsums = cs.enter_context(nc.sbuf_tensor("wsums", [128, 32], F32))
            vv1 = cs.enter_context(nc.sbuf_tensor("vv1", [128, 2], F32))
            scrow = cs.enter_context(nc.sbuf_tensor("scrow", [1, 4], F32))
            w1p = [cs.enter_context(nc.sbuf_tensor(f"w1p{i}", [128, 512], BF16))
                   for i in range(4)]
            psmB = cs.enter_context(nc.psum_tensor("psmB", [1, 32], F32))
            psmC = cs.enter_context(nc.psum_tensor("psmC", [128, 4], F32))
            sutp = cs.enter_context(nc.psum_tensor("sutp", [128, 512], BF16))

            pipe.add("gpsimd", lambda E: E.memset(epsln[:, :], EPS_LN), name="eps")
            for nm, dst, srch in (("ldidb", identb, idb_d), ("ldidf", identf, idf_d),
                                  ("ldob", onesb, ob_d), ("ldofr", onesfr, ofr_d),
                                  ("ldofc", onesfc, ofc_d), ("ldcw0", cw0s, cw0_d),
                                  ("ldcw1", cw1s, cw1_d), ("ldcw2", cw2s, cw2_d),
                                  ("ldcb", cbs, cb_d)):
                pipe.add("sync", lambda E, d=dst, s=srch: E.dma_start(d[:, :], s[:, :]),
                         name=nm, dma=True, stream="cl")
            for nm, dst, srch in (("ldbq", bqb, bq_d), ("ldbk", bkb, bk_d),
                                  ("ldbv", bvb, bv_d), ("ldbo", bob, bo_d)):
                pipe.add("gpsimd", lambda E, d=dst, s=srch: E.dma_start(d[:, :], s[:, :]),
                         name=nm, dma=True, stream="bl")

            def w_piece_ap(widx, k):
                if widx == 0:
                    return w1_d.rearrange("(c p) j -> p c j", p=128)[:, 2 * k:2 * k + 2, :]
                c, hf = k // 2, k % 2
                return w2_d[c * 128:(c + 1) * 128, hf * 2048:(hf + 1) * 2048]

            def wq_piece_ap(widx, k):
                if widx == 0:
                    return w1qv[:, 2 * k:2 * k + 2, :]
                c, hf = k // 2, k % 2
                return w2q_d[c * 128:(c + 1) * 128, hf * 2048:(hf + 1) * 2048]

            for widx in (0, 1):
                for k in range(16):
                    kk = widx * 16 + k
                    pipe.add("gpsimd",
                             lambda E, w=widx, k=k, kk=kk:
                             E.dma_start(wst[kk % 2][:, :].rearrange(
                                 "p (c j) -> p c j", j=1024) if w == 0
                                 else wst[kk % 2][:, :], w_piece_ap(w, k)),
                             waits=(f"ar{kk-2}",) if kk >= 2 else (),
                             name=f"al{kk}", dma=True, stream=f"a{kk % 2}")
                    pipe.add("vector",
                             lambda E, kk=kk:
                             E.reduce_sum(wsums[:, kk:kk + 1], wst[kk % 2][:, :],
                                          AX.X, apply_absolute_value=True),
                             waits=(f"al{kk}",), name=f"ar{kk}")

            def vra(E):
                E.drain()
                E.reduce_sum(vv1[0:128, 0:1], wsums[:, 0:16], AX.X)
                return E.reduce_sum(vv1[0:128, 1:2], wsums[:, 16:32], AX.X)
            pipe.add("vector", vra, waits=("ar31", "ldcb"), name="vrA")

            def tra(E):
                E.matmul(psmB[0:1, 0:1], vv1[0:128, 0:1], onesfc[:, 0:1],
                         start=True, stop=False)
                return E.matmul(psmB[0:1, 1:2], vv1[0:128, 1:2], onesfc[:, 0:1],
                                start=False, stop=True)
            pipe.add("tensor", tra, waits=("vrA",), name="trB")

            def mchain(E):
                E.tensor_scalar(scrow[0:1, 0:1], psmB[0:1, 0:1], 1.0 / NW1, QEPS,
                                ALU.mult, ALU.max)
                E.tensor_scalar(scrow[0:1, 2:3], psmB[0:1, 1:2], 1.0 / NW1, QEPS,
                                ALU.mult, ALU.max)
                E.drain()
                E.reciprocal(scrow[0:1, 1:2], scrow[0:1, 0:1])
                E.drain()
                return E.reciprocal(scrow[0:1, 3:4], scrow[0:1, 2:3])
            pipe.add("vector", mchain, waits=("trB",), name="mch")

            def bcmm(E):
                r = None
                for j in range(4):
                    r = E.matmul(psmC[:, j:j + 1], onesfr[0:1, :], scrow[0:1, j:j + 1],
                                 start=(j == 0), stop=(j == 3))
                return r
            pipe.add("tensor", bcmm, waits=("mch", "ldcb"), name="bcm")

            def evw(E):
                E.tensor_copy(wscal[:, :], psmC[:, :])
                return E.drain()
            pipe.add("vector", evw, waits=("bcm",), name="evw")

            for widx in (0, 1):
                swcol = wscal[:, 1:2] if widx == 0 else wscal[:, 3:4]
                for k in range(16):
                    kk = widx * 16 + k
                    pipe.add("gpsimd",
                             lambda E, w=widx, k=k, kk=kk:
                             E.dma_start(wst[kk % 2][:, :].rearrange(
                                 "p (c j) -> p c j", j=1024) if w == 0
                                 else wst[kk % 2][:, :], w_piece_ap(w, k)),
                             waits=(f"tq{kk-2}",) if kk >= 2 else ("ar31",),
                             name=f"tl{kk}", dma=True, stream=f"t{kk % 2}")

                    def trn(E, kk=kk, sc=swcol):
                        E.tensor_scalar(wtf[kk % 2][:, :], wst[kk % 2][:, :], sc, CRND,
                                        ALU.mult, ALU.add)
                        E.tensor_scalar(wtf[kk % 2][:, :], wtf[kk % 2][:, :], CRND, 1.0,
                                        ALU.subtract, ALU.min)
                        return E.tensor_scalar(wqst[kk % 2][:, :], wtf[kk % 2][:, :],
                                               -1.0, None, ALU.max)
                    pipe.add("vector", trn,
                             waits=(f"tl{kk}", "evw") + ((f"ts{kk-2}",) if kk >= 2 else ()),
                             name=f"tq{kk}")
                    pipe.add("gpsimd",
                             lambda E, w=widx, k=k, kk=kk:
                             E.dma_start(wq_piece_ap(w, k),
                                         wqst[kk % 2][:, :].rearrange(
                                             "p (c j) -> p c j", j=1024) if w == 0
                                         else wqst[kk % 2][:, :]),
                             waits=(f"tq{kk}",), name=f"ts{kk}", dma=True,
                             stream=f"s{kk % 2}")

            for kb in range(32):
                for cfh in range(2):
                    i = kb * 2 + cfh
                    cf0 = cfh * 4
                    pipe.add("gpsimd",
                             lambda E, kb=kb, cf0=cf0, i=i: E.dma_start(
                                 w1p[i % 4][:, :],
                                 w1q_d[kb * 128:(kb + 1) * 128,
                                       cf0 * 128:(cf0 + 4) * 128]),
                             waits=((f"ts{kb // 2}",) if _SU_MODE >= 3 else ())
                             + ((f"wpt{i-4}",) if (i >= 4 and _SU_MODE >= 2) else ()),
                             name=f"wpl{i}", dma=True, stream=f"wp{i % 4}")
                    if _SU_MODE < 2:
                        continue

                    def wpt_fn(E, i=i):
                        r = None
                        for k in range(4):
                            r = E.transpose(
                                sutp[:, k * 128:(k + 1) * 128],
                                w1p[i % 4][:, k * 128:(k + 1) * 128], identb[:, :])
                        return r
                    pipe.add("tensor", wpt_fn,
                             waits=(f"wpl{i}",)
                             + ((f"we{i-1}",) if i >= 1 else ("ldidb",)),
                             name=f"wpt{i}")

                    def we_fn(E, kb=kb, cf0=cf0, i=i):
                        r = None
                        for k in range(4):
                            r = E.activation(
                                w1qT[:, (cf0 + k) * DF + kb * 128:
                                     (cf0 + k) * DF + (kb + 1) * 128],
                                sutp[:, k * 128:(k + 1) * 128], AF.Copy)
                        return r
                    pipe.add("scalar", we_fn, waits=(f"wpt{i}",), name=f"we{i}")
            pipe.add("scalar", lambda E: None,
                     waits=(("we63",) if _SU_MODE >= 2 else ()) + ("ts31", "ts30"))
            pipe.add("gpsimd", lambda E: None, waits=("ldbo",))
            pipe.add("sync", lambda E: None, waits=("ldcb",))
            pipe.build()
            su_ts30 = (pipe.sems[("gpsimd", "s0")], pipe.byname["ts30"].val)
            su_ts31 = (pipe.sems[("gpsimd", "s1")], pipe.byname["ts31"].val)
            su_we = ((pipe.sems[("scalar", None)], pipe.byname["we63"].val)
                     if _SU_MODE >= 2 else (None, 0))
        if stop_after == "su":
            return nc

        with ExitStack() as cs:
            pipe = Pipe(nc, ctx, "p1")
            wk_s = cs.enter_context(nc.sbuf_tensor("wk_s", [128, 8 * D], BF16))
            wv_s = cs.enter_context(nc.sbuf_tensor("wv_s", [128, 8 * D], BF16))
            xb = [cs.enter_context(nc.sbuf_tensor(f"xb{i}", [128, D], F32)) for i in range(2)]
            st = [cs.enter_context(nc.sbuf_tensor(f"st{i}", [128, 10], F32)) for i in range(2)]
            bnrow = cs.enter_context(nc.sbuf_tensor("bnrow", [128, 12], F32))
            xc = [cs.enter_context(nc.sbuf_tensor(f"p1xc{i}", [128, D], F32)) for i in range(2)]
            lnx = [cs.enter_context(nc.sbuf_tensor(f"lnx{i}", [128, D], BF16)) for i in range(2)]
            lnxT = [cs.enter_context(nc.sbuf_tensor(f"lnxT{i}", [128, D], BF16)) for i in range(2)]
            t1 = [cs.enter_context(nc.sbuf_tensor(f"t1{i}", [128, D], F32)) for i in range(2)]
            km = [cs.enter_context(nc.sbuf_tensor(f"km{i}", [128, D], BF16)) for i in range(2)]
            vs = [cs.enter_context(nc.sbuf_tensor(f"vs{i}", [128, D], BF16)) for i in range(2)]
            ksr = cs.enter_context(nc.sbuf_tensor("ksr", [128, D], F32))
            kv16ctx = ExitStack()
            kv16 = kv16ctx.enter_context(nc.psum_tensor("kv16", [64, 1024], F32))
            p1ps = ExitStack()
            tp = [p1ps.enter_context(nc.psum_tensor(f"tp{i}", [128, D], BF16)) for i in range(2)]
            pk = p1ps.enter_context(nc.psum_tensor("pk", [128, D], F32))
            pv = p1ps.enter_context(nc.psum_tensor("pv", [128, D], F32))

            pipe.add("gpsimd", lambda E: E.dma_start(
                wk_s[:, :].rearrange("p (c j) -> p c j", j=D),
                wk_d.rearrange("(c p) j -> p c j", p=128)), name="ldwk", dma=True,
                stream="wk")
            pipe.add("gpsimd", lambda E: E.dma_start(
                wv_s[:, :].rearrange("p (c j) -> p c j", j=D),
                wv_d.rearrange("(c p) j -> p c j", p=128)), name="ldwv", dma=True,
                stream="wv")
            pipe.add("gpsimd", lambda E: E.memset(ksr[:, :], 0.0), name="ksz")

            def proj(E, lT, w_s, brow, ps):
                r = None
                for hf in (0, 1):
                    for c in range(8):
                        E.matmul(ps[:, hf * 512:(hf + 1) * 512],
                                 lT[:, c * 128:(c + 1) * 128],
                                 w_s[:, c * D + hf * 512:c * D + hf * 512 + 512],
                                 start=(c == 0), stop=False)
                    r = E.matmul(ps[:, hf * 512:(hf + 1) * 512], onesb[0:1, :],
                                 brow[0:1, hf * 512:(hf + 1) * 512],
                                 start=False, stop=True)
                return r

            def h1_vs(i):
                ip = i % 2
                pipe.add("sync",
                         lambda E, i=i, ip=ip: E.dma_start(
                             xb[ip][:, :], x_d[i * 128:(i + 1) * 128, :]),
                         waits=(f"C.{i-2}",) if i >= 2 else (),
                         name=f"LD.{i}", dma=True, stream=f"x{i % 2}")
                ln_head(pipe, "", i, xb[ip][:, 0:512], xb[ip][:, 512:1024],
                        st[ip], bnrow,
                        waits=(f"LD.{i}",) + ((f"P.{i-2}",) if i >= 2 else ()))

                def c_fn(E, ip=ip):
                    return E.tensor_scalar(xc[ip][:, :], xb[ip][:, :], st[ip][:, 2:3],
                                           None, ALU.add)
                pipe.add("vector", c_fn,
                         waits=(f"Q.{i}",) + ((f"P.{i-2}",) if i >= 2 else ()),
                         name=f"C.{i}")
                pipe.add("scalar",
                         lambda E, ip=ip: E.activation(lnx[ip][:, :], xc[ip][:, :],
                                                       AF.Identity,
                                                       scale=st[ip][:, 3:4]),
                         waits=(f"C.{i}",) + ((f"P1.{i-2}",) if i >= 2 else ()),
                         name=f"P.{i}")

            def h1_pe(i):
                ip = i % 2

                def p1_fn(E, ip=ip):
                    r = None
                    for c in range(8):
                        r = E.transpose(tp[ip][:, c * 128:(c + 1) * 128],
                                        lnx[ip][:, c * 128:(c + 1) * 128], identb[:, :])
                    return r
                pipe.add("tensor", p1_fn,
                         waits=(f"P.{i}",) + ((f"V3.{i-2}",) if i >= 2 else ()),
                         name=f"P1.{i}")

            def h2(i):
                ip = i % 2
                pipe.add("vector",
                         lambda E, ip=ip: E.tensor_copy(lnxT[ip][:, :], tp[ip][:, :]),
                         waits=(f"P1.{i}",) + ((f"P3.{i-2}",) if i >= 2 else ()),
                         name=f"V3.{i}")
                if i + 2 < NT:
                    h1_vs(i + 2)
                pipe.add("tensor", lambda E, ip=ip: proj(E, lnxT[ip], wk_s, bkb, pk),
                         waits=(f"V3.{i}",)
                         + ((f"V5.{i-1}",) if i >= 1 else ("ldwk",)),
                         name=f"P2.{i}")
                pipe.add("tensor", lambda E, ip=ip: proj(E, lnxT[ip], wv_s, bvb, pv),
                         waits=((f"S6.{i-1}",) if i >= 1 else ("ldwv",)),
                         name=f"P3.{i}")
                pipe.add("vector",
                         lambda E, ip=ip: E.tensor_scalar(t1[ip][:, :], pk[:, :], 0.0,
                                                          None, ALU.min),
                         waits=(f"P2.{i}",), name=f"V4.{i}")
                pipe.add("scalar",
                         lambda E, ip=ip: E.activation(t1[ip][:, :], t1[ip][:, :], AF.Exp),
                         waits=(f"V4.{i}",), name=f"S4.{i}")
                pipe.add("vector",
                         lambda E, ip=ip: E.scalar_tensor_tensor(
                             km[ip][:, :], pk[:, :], 0.0, t1[ip][:, :],
                             ALU.max, ALU.add),
                         waits=(f"S4.{i}",)
                         + ((f"P4.{i-2}",) if i >= 2 else ()),
                         name=f"V5.{i}")
                pipe.add("scalar",
                         lambda E, ip=ip: E.activation(vs[ip][:, :], pv[:, :], AF.Copy),
                         waits=(f"P3.{i}",) + ((f"P4.{i-2}",) if i >= 2 else ()),
                         name=f"S6.{i}")
                pipe.add("vector",
                         lambda E, ip=ip: E.tensor_add(ksr[:, :], ksr[:, :], km[ip][:, :]),
                         waits=(("ksz",) if i == 0 else ()), name=f"V6.{i}")

                def p4(E, i=i, ip=ip):
                    r = None
                    for h in range(16):
                        r = E.matmul(kv16[:, h * 64:(h + 1) * 64],
                                     km[ip][:, h * 64:(h + 1) * 64],
                                     vs[ip][:, h * 64:(h + 1) * 64],
                                     start=(i == 0 and h % 8 == 0),
                                     stop=(i == NT - 1 and h % 8 == 7))
                    return r
                pipe.add("tensor", p4, waits=(f"V5.{i}", f"S6.{i}"), name=f"P4.{i}")
                if i + 2 < NT:
                    h1_pe(i + 2)

            h1_vs(0)
            h1_pe(0)
            h1_vs(1)
            h1_pe(1)
            for i in range(NT):
                h2(i)

            pipe.build()
            p1ps.close()

            with ExitStack() as ce:
                pipe = Pipe(nc, ctx, "p1e")
                kvsb = ce.enter_context(nc.sbuf_tensor("kvsb", [64, 1024], BF16))
                ksrow = ce.enter_context(nc.sbuf_tensor("ksrow", [1, D], F32))
                pks = ce.enter_context(nc.psum_tensor("pks", [1, D], F32))
                pkb = ce.enter_context(nc.psum_tensor("pkb", [128, D], F32))
                pipe.add("vector", lambda E: E.tensor_copy(kvsb[:, :], kv16[:, :]),
                         name="evkv")
                pipe.add("gpsimd", lambda E: E.memset(kvblk[:, :], 0.0), name="kvz")
                for h in range(16):
                    p, odd = h // 2, h % 2
                    pipe.add("gpsimd",
                             lambda E, h=h, p=p, odd=odd: E.dma_start(
                                 kvblk[odd * 64:odd * 64 + 64,
                                       p * 128 + odd * 64:p * 128 + odd * 64 + 64],
                                 kvsb[0:64, h * 64:(h + 1) * 64]),
                             waits=("evkv", "kvz"), name=f"kvb{h}", dma=True, stream="kv")

                def cs_mm(E):
                    r = None
                    for hf in (0, 1):
                        r = E.matmul(pks[0:1, hf * 512:(hf + 1) * 512], onesfc[:, 0:1],
                                     ksr[:, hf * 512:(hf + 1) * 512], start=True, stop=True)
                    return r
                pipe.add("tensor", cs_mm, name="csm")
                pipe.add("vector", lambda E: E.tensor_copy(ksrow[:, :], pks[:, :]),
                         waits=("csm",), name="evks")

                def kb_mm(E):
                    r = None
                    for hf in (0, 1):
                        r = E.matmul(pkb[:, hf * 512:(hf + 1) * 512], onesfr[0:1, :],
                                     ksrow[0:1, hf * 512:(hf + 1) * 512],
                                     start=True, stop=True)
                    return r
                pipe.add("tensor", kb_mm, waits=("evks",), name="kbm")
                pipe.add("vector", lambda E: E.tensor_copy(ksumb[:, :], pkb[:, :]),
                         waits=("kbm",), name="evkb")
                pipe.add("gpsimd", lambda E: None, waits=("kvb15",))
                pipe.build()
            kv16ctx.close()
        if stop_after == "p1":
            return nc

        with ExitStack() as cs:
            pipe = Pipe(nc, ctx, "a")
            wq_s = cs.enter_context(nc.sbuf_tensor("wq_s", [128, 8 * D], BF16))
            wo_s = cs.enter_context(nc.sbuf_tensor("wo_s", [128, 8 * D], BF16))
            xb = [cs.enter_context(nc.sbuf_tensor(f"axb{i}", [128, D], F32)) for i in range(7)]
            st = [cs.enter_context(nc.sbuf_tensor(f"ast{i}", [128, 10], F32)) for i in range(2)]
            st2 = [cs.enter_context(nc.sbuf_tensor(f"ast2{i}", [128, 10], F32)) for i in range(2)]
            bnrow = cs.enter_context(nc.sbuf_tensor("abnrow", [128, 12], F32))
            bnrow2 = cs.enter_context(nc.sbuf_tensor("abnrow2", [128, 12], F32))
            xc = [cs.enter_context(nc.sbuf_tensor(f"axc{i}", [128, D], F32)) for i in range(2)]
            xc2 = cs.enter_context(nc.sbuf_tensor("axc2", [128, D], F32))
            lnx = [cs.enter_context(nc.sbuf_tensor(f"alnx{i}", [128, D], BF16)) for i in range(2)]
            lnxT = cs.enter_context(nc.sbuf_tensor("alnxT", [128, D], BF16))
            t1 = cs.enter_context(nc.sbuf_tensor("at1", [128, D], F32))
            qm = cs.enter_context(nc.sbuf_tensor("aqm", [128, D], F32))
            zden = cs.enter_context(nc.sbuf_tensor("azden", [128, 16], F32))
            qms = cs.enter_context(nc.sbuf_tensor("aqms", [128, D], BF16))
            qmsT = cs.enter_context(nc.sbuf_tensor("aqmsT", [128, 8 * 512], BF16))
            yT = cs.enter_context(nc.sbuf_tensor("ayT", [128, 8 * 512], BF16))
            ust = [cs.enter_context(nc.sbuf_tensor(f"aust{i}", [128, D], F32)) for i in range(2)]
            tp = [cs.enter_context(nc.psum_tensor(f"atp{i}", [128, D], BF16)) for i in range(2)]
            qp = cs.enter_context(nc.psum_tensor("aqp", [128, D], F32))
            yp = [cs.enter_context(nc.psum_tensor(f"ayp{i}", [128, 512], F32)) for i in range(2)]
            ao = cs.enter_context(nc.psum_tensor("aao", [128, D], F32))

            pipe.add("gpsimd", lambda E: E.dma_start(
                wq_s[:, :].rearrange("p (c j) -> p c j", j=D),
                wq_d.rearrange("(c p) j -> p c j", p=128)), name="ldwq", dma=True,
                stream="wq")
            pipe.add("gpsimd", lambda E: E.dma_start(
                wo_s[:, :].rearrange("p (c j) -> p c j", j=D),
                wo_d.rearrange("(c p) j -> p c j", p=128)), name="ldwo", dma=True,
                stream="wo")

            qmsTv = qmsT[:, :].rearrange("p (c t) -> p c t", t=512)

            aob = cs.enter_context(nc.sbuf_tensor("aaob", [128, D], F32))
            qb = [cs.enter_context(nc.sbuf_tensor(f"aqb{i}", [128, D], F32))
                  for i in range(2)]

            def h1a_vs(j):
                jp, j4 = j % 2, j % 7
                pipe.add("sync",
                         lambda E, j=j, j4=j4: E.dma_start(
                             xb[j4][:, :], x_d[j * 128:(j + 1) * 128, :]),
                         waits=(f"RC.{j-7}",) if j >= 7 else (),
                         name=f"LDX.{j}", dma=True, stream=f"x{j % 7}")
                ln_head(pipe, "", j, xb[j4][:, 0:512], xb[j4][:, 512:1024],
                        st[jp], bnrow,
                        waits=(f"LDX.{j}",) + ((f"P.{j-2}",) if j >= 2 else ()))

                def c_fn(E, jp=jp, j4=j4):
                    return E.tensor_scalar(xc[jp][:, :], xb[j4][:, :], st[jp][:, 2:3],
                                           None, ALU.add)
                pipe.add("vector", c_fn,
                         waits=(f"Q.{j}",) + ((f"P.{j-2}",) if j >= 2 else ()),
                         name=f"C.{j}")
                pipe.add("scalar",
                         lambda E, jp=jp: E.activation(lnx[jp][:, :], xc[jp][:, :],
                                                       AF.Identity,
                                                       scale=st[jp][:, 3:4]),
                         waits=(f"C.{j}",) + ((f"P1.{j-2}",) if j >= 2 else ()),
                         name=f"P.{j}")

            def h1a_pe(j):
                jp = j % 2

                def p1_fn(E, jp=jp):
                    r = None
                    for c in range(8):
                        r = E.transpose(tp[jp][:, c * 128:(c + 1) * 128],
                                        lnx[jp][:, c * 128:(c + 1) * 128],
                                        identb[:, :])
                    return r
                pipe.add("tensor", p1_fn,
                         waits=(f"P.{j}",) + ((f"V8.{j-2}",) if j >= 2 else ()),
                         name=f"P1.{j}")

            def h2a(j):
                g, s_ = j // 4, j % 4
                jp = j % 2
                pipe.add("vector",
                         lambda E, jp=jp: E.tensor_copy(lnxT[:, :], tp[jp][:, :]),
                         waits=(f"P1.{j}",) + ((f"P2.{j-1}",) if j >= 1 else ()),
                         name=f"V3.{j}")
                if j + 2 < S // 128:
                    h1a_vs(j + 2)
                pipe.add("tensor", lambda E: proj(E, lnxT, wq_s, bqb, qp),
                         waits=(f"V3.{j}",)
                         + ((f"SCQ.{j-1}",) if j >= 1 else ("ldwq",)),
                         name=f"P2.{j}")
                pipe.add("scalar",
                         lambda E, jp=jp: E.activation(qb[jp][:, :], qp[:, :], AF.Copy),
                         waits=(f"P2.{j}",) + ((f"V5.{j-2}",) if j >= 2 else ()),
                         name=f"SCQ.{j}")
                pipe.add("vector",
                         lambda E, jp=jp: E.tensor_scalar(t1[:, :], qb[jp][:, :], 0.0,
                                                          None, ALU.min),
                         waits=(f"SCQ.{j}",), name=f"V4.{j}")
                pipe.add("scalar",
                         lambda E: E.activation(t1[:, :], t1[:, :], AF.Exp),
                         waits=(f"V4.{j}",), name=f"S4.{j}")
                pipe.add("vector",
                         lambda E, jp=jp: E.scalar_tensor_tensor(
                             qm[:, :], qb[jp][:, :], 0.0, t1[:, :],
                             ALU.max, ALU.add),
                         waits=(f"S4.{j}",), name=f"V5.{j}")

                def v6(E):
                    E.tensor_mul(t1[:, :], qm[:, :], ksumb[:, :])
                    E.tensor_reduce(zden[:, :],
                                    t1[:, :].rearrange("p (h d) -> p h d", d=64),
                                    AX.X, ALU.add)
                    E.drain()
                    E.tensor_scalar(zden[:, :], zden[:, :], EPS_Z, None,
                                    ALU.add)
                    E.drain()
                    E.reciprocal(zden[:, :], zden[:, :])
                    return E.drain()
                pipe.add("vector", v6, name=f"V6.{j}")

                def v7(E):
                    r = None
                    for h in range(16):
                        r = E.tensor_scalar(qms[:, h * 64:(h + 1) * 64],
                                            qm[:, h * 64:(h + 1) * 64],
                                            zden[:, h:h + 1], None, ALU.mult)
                    return r
                pipe.add("vector", v7,
                         waits=((f"P3.{j-1}",) if j >= 1 else ()), name=f"V7.{j}")

                def p3(E, jp=jp):
                    r = None
                    for c in range(8):
                        r = E.transpose(tp[jp][:, c * 128:(c + 1) * 128],
                                        qms[:, c * 128:(c + 1) * 128],
                                        identb[:, :])
                    return r
                pipe.add("tensor", p3, waits=(f"V7.{j}", f"V3.{j}"), name=f"P3.{j}")
                pipe.add("vector",
                         lambda E, jp=jp, s_=s_: E.tensor_copy(
                             qmsTv[:, :, s_ * 128:(s_ + 1) * 128],
                             tp[jp][:, :].rearrange("p (c t) -> p c t", t=128)),
                         waits=(f"P3.{j}",)
                         + ((f"PY7.{g-1}",) if (s_ == 0 and g >= 1) else ()),
                         name=f"V8.{j}")
                if j + 2 < S // 128:
                    h1a_pe(j + 2)


            def ystage(g):
                for p in range(8):
                    pipe.add("tensor",
                             lambda E, p=p: E.matmul(
                                 yp[p % 2][:, :], kvblk[:, p * 128:(p + 1) * 128],
                                 qmsT[:, p * 512:(p + 1) * 512], start=True, stop=True),
                             waits=((f"V8.{g*4+3}",) if p == 0 else ())
                             + ((f"SY{p-2}.{g}",) if p >= 2 else ()),
                             name=f"PY{p}.{g}")
                    pipe.add("scalar",
                             lambda E, p=p: E.activation(
                                 yT[:, p * 512:(p + 1) * 512],
                                 yp[p % 2][:, :], AF.Copy),
                             waits=(f"PY{p}.{g}",)
                             + ((f"P5.{(g-1)*4+3}",) if (p == 0 and g >= 1) else ()),
                             name=f"SY{p}.{g}")

            def tail(j):
                g, s_ = j // 4, j % 4
                jp, j4 = j % 2, j % 7

                def p5(E, s_=s_):
                    r = None
                    for hf in (0, 1):
                        for c in range(8):
                            E.matmul(ao[:, hf * 512:(hf + 1) * 512],
                                     yT[:, c * 512 + s_ * 128:c * 512 + (s_ + 1) * 128],
                                     wo_s[:, c * D + hf * 512:c * D + hf * 512 + 512],
                                     start=(c == 0), stop=False)
                        r = E.matmul(ao[:, hf * 512:(hf + 1) * 512], onesb[0:1, :],
                                     bob[0:1, hf * 512:(hf + 1) * 512],
                                     start=False, stop=True)
                    return r
                pipe.add("tensor", p5,
                         waits=(f"SY7.{g}",)
                         + ((f"SC.{j-1}",) if j >= 1 else ("ldwo",)),
                         name=f"P5.{j}")
                pipe.add("scalar",
                         lambda E: E.activation(aob[:, :], ao[:, :], AF.Copy),
                         waits=(f"P5.{j}",) + ((f"RC.{j-1}",) if j >= 1 else ()),
                         name=f"SC.{j}")
                ln_head(pipe, "R", j, aob[:, 0:512], aob[:, 512:1024],
                        st2[jp], bnrow2,
                        waits=(f"SC.{j}",) + ((f"RC.{j-2}",) if j >= 2 else ()))

                def rc_fn(E, jp=jp, j4=j4):
                    E.tensor_scalar(xc2[:, :], aob[:, :], st2[jp][:, 2:3],
                                    None, ALU.add)
                    return E.scalar_tensor_tensor(ust[jp][:, :], xc2[:, :],
                                                  st2[jp][:, 3:4], xb[j4][:, :],
                                                  ALU.mult, ALU.add)
                pipe.add("vector", rc_fn,
                         waits=(f"RQ.{j}",) + ((f"STU.{j-2}",) if j >= 2 else ()),
                         name=f"RC.{j}")
                pipe.add("sync",
                         lambda E, j=j, jp=jp: E.dma_start(
                             u_d[j * 128:(j + 1) * 128, :], ust[jp][:, :]),
                         waits=(f"RC.{j}",), name=f"STU.{j}", dma=True,
                         stream=f"u{j % 2}")

            h1a_vs(0)
            h1a_pe(0)
            h1a_vs(1)
            h1a_pe(1)
            for g in range(NG):
                for s_ in range(4):
                    h2a(g * 4 + s_)
                ystage(g)
                for s_ in range(4):
                    tail(g * 4 + s_)

            pipe.add("sync", lambda E: None,
                     waits=(f"STU.{S//128-2}", f"STU.{S//128-1}"))
            pipe.build()
        if stop_after == "a":
            return nc

        with ExitStack() as cs:
            pipe = Pipe(nc, ctx, "b1")
            ub = [cs.enter_context(nc.sbuf_tensor(f"bub{i}", [128, D], F32)) for i in range(4)]
            st = [cs.enter_context(nc.sbuf_tensor(f"bst{i}", [128, 10], F32)) for i in range(4)]
            bnrow = [cs.enter_context(nc.sbuf_tensor(f"bbnrow{i}", [128, 12], F32))
                     for i in range(4)]
            xc = [cs.enter_context(nc.sbuf_tensor(f"bxc{i}", [128, D], F32)) for i in range(2)]
            t1 = cs.enter_context(nc.sbuf_tensor("bt1", [128, D], F32))
            xqg = [cs.enter_context(nc.sbuf_tensor(f"bxqg{i}", [128, D], BF16)) for i in range(4)]
            xqT = [cs.enter_context(nc.sbuf_tensor(f"bxqT{i}", [128, 8 * 512], BF16))
                   for i in range(2)]
            hst = [cs.enter_context(nc.sbuf_tensor(f"bhst{i}", [128, 2048], BF16))
                   for i in range(2)]
            tp = [cs.enter_context(nc.psum_tensor(f"btp{i}", [128, D], BF16)) for i in range(2)]
            hp = [cs.enter_context(nc.psum_tensor(f"bhp{i}", [128, 512], F32)) for i in range(4)]

            def front_vs(j):
                jq = j % 4
                jp = j % 2
                pipe.add("gpsimd",
                         lambda E, j=j, jq=jq: E.dma_start(
                             ub[jq][:, :], u_d[j * 128:(j + 1) * 128, :]),
                         waits=(f"C.{j-4}",) if j >= 4 else (),
                         name=f"LDU.{j}", dma=True, stream=f"u{j % 4}")
                ln_head(pipe, "", j, ub[jq][:, 0:512], ub[jq][:, 512:1024],
                        st[jq], bnrow[jq],
                        waits=(f"LDU.{j}",) + ((f"V5.{j-4}",) if j >= 4 else ()))

                def c_fn(E, jp=jp, jq=jq):
                    E.tensor_scalar(xc[jp][:, :], ub[jq][:, :], st[jq][:, 2:3],
                                    None, ALU.add)
                    E.reduce_max(st[jq][:, 5:6], xc[jp][:, :], AX.X,
                                 apply_absolute_value=True)
                    return E.drain()
                pipe.add("vector", c_fn,
                         waits=(f"Q.{j}",) + ((f"V5.{j-2}",) if j >= 2 else ()),
                         name=f"C.{j}")

                def v4(E, jq=jq):
                    E.tensor_scalar(st[jq][:, 6:7], st[jq][:, 5:6], st[jq][:, 3:4],
                                    None, ALU.mult)
                    E.drain()
                    E.tensor_scalar(st[jq][:, 6:7], st[jq][:, 6:7], QEPS,
                                    1.0 / 127.0, ALU.max, ALU.mult)
                    E.drain()
                    E.reciprocal(st[jq][:, 7:8], st[jq][:, 6:7])
                    E.tensor_scalar(st[jq][:, 8:9], st[jq][:, 6:7], wscal[:, 0:1],
                                    None, ALU.mult)
                    E.drain()
                    E.tensor_scalar(st[jq][:, 9:10], st[jq][:, 7:8], st[jq][:, 3:4],
                                    None, ALU.mult)
                    return E.drain()
                pipe.add("vector", v4, name=f"V4.{j}")

                def v5(E, jp=jp, jq=jq, j4=j % 4):
                    E.tensor_scalar(t1[:, :], xc[jp][:, :], st[jq][:, 9:10], CRND,
                                    ALU.mult, ALU.add)
                    return E.tensor_scalar(xqg[j4][:, :], t1[:, :], CRND,
                                           st[jq][:, 8:9],
                                           ALU.subtract, ALU.mult)
                pipe.add("vector", v5,
                         waits=((f"P1.{j-4}",) if j >= 4 else ()), name=f"V5.{j}")

            def front_pe(j):
                jp, j4 = j % 2, j % 4
                gp = (j // 4) % 2
                s_ = j % 4

                def p1_fn(E, j4=j4, jp=jp):
                    r = None
                    for c in range(8):
                        r = E.transpose(tp[jp][:, c * 128:(c + 1) * 128],
                                        xqg[j4][:, c * 128:(c + 1) * 128],
                                        identb[:, :])
                    return r
                pipe.add("tensor", p1_fn,
                         waits=(f"V5.{j}",) + ((f"V6.{j-2}",) if j >= 2 else ()),
                         name=f"P1.{j}")
                pipe.add("vector",
                         lambda E, jp=jp, gp=gp, s_=s_: E.tensor_copy(
                             xqT[gp][:, :].rearrange("p (c t) -> p c t", t=512)
                             [:, :, s_ * 128:(s_ + 1) * 128],
                             tp[jp][:, :].rearrange("p (c t) -> p c t", t=128)),
                         waits=(f"P1.{j}",)
                         + ((f"PW31.{(j//4)-2}",) if (s_ == 0 and j >= 8) else ()),
                         name=f"V6.{j}")

            for s_ in range(4):
                front_vs(s_)
            for s_ in range(4):
                front_pe(s_)
            pe_at = {8: 0, 16: 1, 24: 2, 28: 3}
            for g in range(NG):
                gp = g % 2
                if g + 1 < NG:
                    for k in range(4):
                        front_vs(4 * (g + 1) + k)
                for c in range(32):
                    F = g * 8 + c // 4

                    def pw(E, c=c, gp=gp, g=g):
                        if g == 0 and c < 4:
                            E.wait_ge(su_we[0], su_we[1])
                        r = None
                        for cf in range(8):
                            r = E.matmul(hp[c % 4][:, :],
                                         w1qT[:, cf * DF + c * 128:cf * DF + (c + 1) * 128],
                                         xqT[gp][:, cf * 512:(cf + 1) * 512],
                                         start=(cf == 0), stop=(cf == 7))
                        return r
                    pipe.add("tensor", pw,
                             waits=(f"V6.{g*4+3}",)
                             + ((f"SE{c-4}.{g}",) if c >= 4 else
                                ((f"SE{28+c}.{g-1}",) if g >= 1 else ())),
                             name=f"PW{c}.{g}")
                    pipe.add("scalar",
                             lambda E, c=c: E.activation(
                                 hst[(c // 4) % 2][:, (c % 4) * 512:(c % 4 + 1) * 512],
                                 hp[c % 4][:, :], AF.Copy),
                             waits=(f"PW{c}.{g}",)
                             + ((f"STH.{F-2}",) if (c % 4 == 0 and F >= 2) else ()),
                             name=f"SE{c}.{g}")
                    if c % 4 == 3:
                        pipe.add("sync",
                                 lambda E, c=c, g=g: E.dma_start(
                                     hview[:, c - 3:c + 1, g * 512:(g + 1) * 512],
                                     hst[(c // 4) % 2][:, :].rearrange(
                                         "p (s t) -> p s t", t=512)),
                                 waits=(f"SE{c}.{g}", f"SE{c-1}.{g}",
                                        f"SE{c-2}.{g}", f"SE{c-3}.{g}"),
                                 name=f"STH.{F}", dma=True, stream=f"h{F % 2}")
                    if c in pe_at and g + 1 < NG:
                        front_pe(4 * (g + 1) + pe_at[c])
            pipe.add("sync", lambda E: None,
                     waits=(f"STH.{NG*8-2}", f"STH.{NG*8-1}"))
            pipe.build()
        if stop_after == "b1":
            return nc
        ctxA.close()

        with ExitStack() as cs:
            pipe = Pipe(nc, ctx, "b2")
            w2T = cs.enter_context(nc.sbuf_tensor("w2T", [128, 32 * D], BF16))
            w2p = [cs.enter_context(nc.sbuf_tensor(f"w2p{i}", [128, 512], BF16))
                   for i in range(4)]
            hload = cs.enter_context(nc.sbuf_tensor("hload", [128, 8 * 514], BF16))
            hsil = cs.enter_context(nc.sbuf_tensor("hsil", [128, 32 * 512], BF16))
            cvt = [cs.enter_context(nc.sbuf_tensor(f"cvt{i}", [128, 512], BF16))
                   for i in range(8)]
            tqf = [cs.enter_context(nc.sbuf_tensor(f"tqf{i}", [128, 512], F32))
                   for i in range(2)]
            rmax = cs.enter_context(nc.sbuf_tensor("rmax", [128, 512], BF16))
            rmin = cs.enter_context(nc.sbuf_tensor("rmin", [128, 512], BF16))
            s2st = cs.enter_context(nc.sbuf_tensor("s2st", [128, 8], F32))
            s2row = cs.enter_context(nc.sbuf_tensor("s2row", [1, 512], F32))
            s2b = cs.enter_context(nc.sbuf_tensor("s2b", [128, 512], BF16))
            hqf = cs.enter_context(nc.sbuf_tensor("hqf", [128, 32 * 512], BF16))
            st2 = [cs.enter_context(nc.sbuf_tensor(f"cst{i}", [128, 10], F32)) for i in range(2)]
            bnrow2 = cs.enter_context(nc.sbuf_tensor("cbnrow", [128, 12], F32))
            xc2 = cs.enter_context(nc.sbuf_tensor("cxc2", [128, D], F32))
            ub = [cs.enter_context(nc.sbuf_tensor(f"cub{i}", [128, D], F32)) for i in range(2)]
            ost = [cs.enter_context(nc.sbuf_tensor(f"cost{i}", [128, D], F32)) for i in range(2)]
            oa = cs.enter_context(nc.psum_tensor("coa", [128, D], F32))
            ob = cs.enter_context(nc.psum_tensor("cob", [128, D], F32))
            tps = cs.enter_context(nc.psum_tensor("ctps", [128, 512], BF16))
            tpf = cs.enter_context(nc.psum_tensor("ctpf", [1, 512], F32))
            s2p = cs.enter_context(nc.psum_tensor("cs2p", [128, 512], F32))
            w2tp = cs.enter_context(nc.psum_tensor("cw2tp", [128, 512], BF16))

            hlv = hload[:, :].rearrange("p (c t) -> p c t", t=514)
            pipe.add("gpsimd",
                     lambda E: E.memset(hlv[:, :, 0:1], 0.0), name="hz0")

            def w2_load(i):
                ccb, fb = i // 8, i % 8

                def ld(E, ccb=ccb, fb=fb, i=i):
                    if i == 0:
                        E.wait_ge(su_ts30[0], su_ts30[1])
                        E.wait_ge(su_ts31[0], su_ts31[1])
                    return E.dma_start(
                        w2p[i % 4][:, :],
                        w2q_d[fb * 128:(fb + 1) * 128,
                              ccb * 512:(ccb + 1) * 512])
                pipe.add("gpsimd", ld,
                         waits=(f"W2T{i-4}",) if i >= 4 else (),
                         name=f"w2pl{i}", dma=True, stream=f"w2p{i % 4}")

            def w2_tr_ev(i):
                ccb, fb = i // 8, i % 8

                def tr(E, i=i):
                    r = None
                    for k in range(4):
                        r = E.transpose(
                            w2tp[:, k * 128:(k + 1) * 128],
                            w2p[i % 4][:, k * 128:(k + 1) * 128], identb[:, :])
                    return r
                pipe.add("tensor", tr,
                         waits=(f"w2pl{i}",)
                         + ((f"W2E{i-1}",) if i >= 1 else ()),
                         name=f"W2T{i}")

                def ev(E, ccb=ccb, fb=fb, i=i):
                    r = None
                    for k in range(4):
                        cc = ccb * 4 + k
                        r = E.activation(
                            w2T[:, cc * D + fb * 128:cc * D + (fb + 1) * 128],
                            w2tp[:, k * 128:(k + 1) * 128], AF.Copy)
                    return r
                pipe.add("scalar", ev, waits=(f"W2T{i}",), name=f"W2E{i}")

            def conv_one(g, c):
                cs8 = c % 8
                lhw = []
                if c >= 8:
                    lhw.append(f"CV{c-8}.{g}")
                elif g >= 1:
                    lhw.append(f"CV{c+24}.{g-1}")
                if g == 0:
                    pipe.add("sync",
                             lambda E, c=c, cs8=cs8: E.dma_start(
                                 hlv[:, cs8:cs8 + 1, 1:514],
                                 hview[:, c:c + 1, 0:513]),
                             waits=tuple(lhw), name=f"LH{c}.{g}", dma=True,
                             stream=f"lh{c % 8}")
                elif g == NG - 1:
                    pipe.add("sync",
                             lambda E, c=c, g=g, cs8=cs8: E.dma_start(
                                 hlv[:, cs8:cs8 + 1, 0:513],
                                 hview[:, c:c + 1, g * 512 - 1:S]),
                             waits=tuple(lhw), name=f"LH{c}.{g}", dma=True,
                             stream=f"lh{c % 8}")
                else:
                    pipe.add("sync",
                             lambda E, c=c, g=g, cs8=cs8: E.dma_start(
                                 hlv[:, cs8:cs8 + 1, :],
                                 hview[:, c:c + 1, g * 512 - 1:g * 512 + 513]),
                             waits=tuple(lhw), name=f"LH{c}.{g}", dma=True,
                             stream=f"lh{c % 8}")

                def cv(E, c=c, cs8=cs8):
                    E.tensor_scalar(cvt[c % 8][:, :], hlv[:, cs8, 0:512],
                                    cw0s[:, c:c + 1], None, ALU.mult)
                    E.scalar_tensor_tensor(cvt[c % 8][:, :], hlv[:, cs8, 1:513],
                                           cw1s[:, c:c + 1], cvt[c % 8][:, :],
                                           ALU.mult, ALU.add)
                    return E.scalar_tensor_tensor(cvt[c % 8][:, :], hlv[:, cs8, 2:514],
                                                  cw2s[:, c:c + 1], cvt[c % 8][:, :],
                                                  ALU.mult, ALU.add)
                waits = [f"LH{c}.{g}"]
                if c >= 8:
                    waits.append(f"SL{c-8}.{g}")
                elif g >= 1:
                    waits.append(f"SL{24+c}.{g-1}")
                if g == 0:
                    waits.append("hz0")
                if g == NG - 1:
                    waits.append("hz15")
                pipe.add("vector", cv, waits=tuple(waits), name=f"CV{c}.{g}")

                def sl(cc):
                    pipe.add("scalar",
                             lambda E, cc=cc: E.activation(
                                 hsil[:, cc * 512:(cc + 1) * 512], cvt[cc % 8][:, :],
                                 AF.Silu, bias=cbs[:, cc:cc + 1]),
                             waits=(f"CV{cc}.{g}",)
                             + ((f"HQ{cc}.{g-1}",) if g >= 1 else ()),
                             name=f"SL{cc}.{g}")

                def mxmn(cc):
                    if cc == 0:
                        pipe.add("vector",
                                 lambda E: E.tensor_copy(rmax[:, :], hsil[:, 0:512]),
                                 waits=(f"SL0.{g}",)
                                 + ((f"TRX.{g-1}",) if g >= 1 else ()),
                                 name=f"MX0.{g}")
                        pipe.add("vector",
                                 lambda E: E.tensor_copy(rmin[:, :], hsil[:, 0:512]),
                                 waits=(f"SL0.{g}",)
                                 + ((f"TRN.{g-1}",) if g >= 1 else ()),
                                 name=f"MN0.{g}")
                    else:
                        pipe.add("vector",
                                 lambda E, cc=cc: E.tensor_max(
                                     rmax[:, :], rmax[:, :],
                                     hsil[:, cc * 512:(cc + 1) * 512]),
                                 waits=(f"SL{cc}.{g}",), name=f"MX{cc}.{g}")
                        pipe.add("vector",
                                 lambda E, cc=cc: E.tensor_tensor(
                                     rmin[:, :], rmin[:, :],
                                     hsil[:, cc * 512:(cc + 1) * 512],
                                     ALU.min),
                                 waits=(f"SL{cc}.{g}",), name=f"MN{cc}.{g}")
                if c >= 1:
                    sl(c - 1)
                if c >= 2:
                    mxmn(c - 2)
                if c == 31:
                    sl(31)
                    mxmn(30)
                    mxmn(31)

            def s2stage(g):
                def trx(E):
                    r = None
                    for b in range(4):
                        r = E.transpose(tps[:, b * 128:(b + 1) * 128],
                                        rmax[:, b * 128:(b + 1) * 128], identb[:, :])
                    return r
                pipe.add("tensor", trx, waits=(f"MX31.{g}",), name=f"TRX.{g}")
                pipe.add("vector",
                         lambda E: E.reduce_max(
                             s2st[:, 0:4],
                             tps[:, :].rearrange("p (b t) -> p b t", t=128), AX.X),
                         waits=(f"TRX.{g}",), name=f"VRX.{g}")

                def trn_(E):
                    r = None
                    for b in range(4):
                        r = E.transpose(tps[:, b * 128:(b + 1) * 128],
                                        rmin[:, b * 128:(b + 1) * 128], identb[:, :])
                    return r
                pipe.add("tensor", trn_, waits=(f"MN31.{g}", f"VRX.{g}"),
                         name=f"TRN.{g}")

                def vrn(E):
                    E.tensor_reduce(s2st[:, 4:8],
                                    tps[:, :].rearrange("p (b t) -> p b t", t=128),
                                    AX.X, ALU.min)
                    E.drain()
                    E.tensor_scalar(s2st[:, 4:8], s2st[:, 4:8], -1.0, None, ALU.mult)
                    E.drain()
                    E.tensor_tensor(s2st[:, 0:4], s2st[:, 0:4], s2st[:, 4:8], ALU.max)
                    E.drain()
                    E.tensor_scalar(s2st[:, 0:4], s2st[:, 0:4], QEPS, 1.0 / 127.0,
                                    ALU.max, ALU.mult)
                    E.drain()
                    E.reciprocal(s2st[:, 0:4], s2st[:, 0:4])
                    return E.drain()
                pipe.add("vector", vrn, waits=(f"TRN.{g}",), name=f"VRN.{g}")

                def trs(E):
                    r = None
                    for b in range(4):
                        r = E.matmul(tpf[0:1, b * 128:(b + 1) * 128],
                                     s2st[:, b:b + 1], identf[:, :],
                                     start=(b == 0), stop=(b == 3))
                    return r
                pipe.add("tensor", trs, waits=(f"VRN.{g}",), name=f"TRS.{g}")
                pipe.add("vector", lambda E: E.tensor_copy(s2row[:, :], tpf[:, :]),
                         waits=(f"TRS.{g}",), name=f"EVS.{g}")
                pipe.add("tensor",
                         lambda E: E.matmul(s2p[:, :], onesfr[0:1, :], s2row[0:1, :],
                                            start=True, stop=True),
                         waits=(f"EVS.{g}",), name=f"S2M.{g}")
                pipe.add("vector", lambda E: E.tensor_copy(s2b[:, :], s2p[:, :]),
                         waits=(f"S2M.{g}",), name=f"S2D.{g}")

            def hq_one(g, c):
                def vhq(E, c=c):
                    E.tensor_mul(tqf[c % 2][:, :],
                                 hsil[:, c * 512:(c + 1) * 512], s2b[:, :])
                    return E.tensor_scalar(hqf[:, c * 512:(c + 1) * 512],
                                           tqf[c % 2][:, :],
                                           CRND, CRND, ALU.add, ALU.subtract)
                w = [f"S2D.{g}"]
                if g >= 1:
                    w.append(f"PHB{c}.{g-1}")
                pipe.add("vector", vhq, waits=tuple(w), name=f"HQ{c}.{g}")

            def passmm(g, hf):
                tag = "AB"[hf]
                if hf == 0:
                    hq_one(g, 0)
                    hq_one(g, 1)
                for c in range(32):
                    if hf == 0 and c + 2 < 32:
                        hq_one(g, c + 2)
                    def ph(E, c=c, hf=hf):
                        r = None
                        for si in (0, 1):
                            ps = oa if si == 0 else ob
                            for oh in (0, 1):
                                r = E.matmul(ps[:, oh * 512:(oh + 1) * 512],
                                             hqf[:, c * 512 + hf * 256 + si * 128:
                                                 c * 512 + hf * 256 + (si + 1) * 128],
                                             w2T[:, c * D + oh * 512:c * D + oh * 512 + 512],
                                             start=(c == 0), stop=(c == 31))
                        return r
                    w = [f"HQ{c}.{g}"]
                    if c == 0:
                        if hf == 1:
                            w.append(f"RC.{g*4+1}")
                        elif g >= 1:
                            w.append(f"RC.{(g-1)*4+3}")
                    if g == 0 and hf == 0:
                        w.append(f"W2E{(c // 4) * 8 + 7}")
                    pipe.add("tensor", ph, waits=tuple(w), name=f"PH{tag}{c}.{g}")

            def ln4(g, hf):
                tag = "AB"[hf]
                for si in (0, 1):
                    j = g * 4 + hf * 2 + si
                    jp = j % 2
                    ps = oa if si == 0 else ob
                    pipe.add("sync",
                             lambda E, j=j, jp=jp: E.dma_start(
                                 ub[jp][:, :], u_d[j * 128:(j + 1) * 128, :]),
                             waits=(f"RC.{j-2}",) if j >= 2 else (),
                             name=f"LDU.{j}", dma=True, stream=f"u{j % 2}")
                    ln_head(pipe, "R", j, ps[:, 0:512], ps[:, 512:1024],
                            st2[jp], bnrow2,
                            waits=(f"PH{tag}31.{g}",)
                            + ((f"RC.{j-2}",) if j >= 2 else ()))

                    def rc_fn(E, jp=jp, ps=ps):
                        E.tensor_scalar(xc2[:, :], ps[:, :], st2[jp][:, 2:3],
                                        None, ALU.add)
                        return E.scalar_tensor_tensor(ost[jp][:, :], xc2[:, :],
                                                      st2[jp][:, 3:4], ub[jp][:, :],
                                                      ALU.mult, ALU.add)
                    pipe.add("vector", rc_fn,
                             waits=(f"RQ.{j}", f"LDU.{j}")
                             + ((f"STO.{j-2}",) if j >= 2 else ()),
                             name=f"RC.{j}")
                    pipe.add("sync",
                             lambda E, j=j, jp=jp: E.dma_start(
                                 out_d[j * 128:(j + 1) * 128, :], ost[jp][:, :]),
                             waits=(f"RC.{j}",), name=f"STO.{j}", dma=True,
                             stream=f"o{j % 2}")

            for c in range(32):
                conv_one(0, c)
                w2_load(2 * c)
                w2_load(2 * c + 1)
                w2_tr_ev(2 * c)
                w2_tr_ev(2 * c + 1)
            for g in range(NG):
                if g + 1 == NG - 1:
                    pipe.add("gpsimd", lambda E: E.memset(hlv[:, :, 513:514], 0.0),
                             waits=(f"CV31.{NG-2}",), name="hz15")
                s2stage(g)
                passmm(g, 0)
                if g + 1 < NG:
                    for c in range(16):
                        conv_one(g + 1, c)
                ln4(g, 0)
                passmm(g, 1)
                if g + 1 < NG:
                    for c in range(16, 32):
                        conv_one(g + 1, c)
                ln4(g, 1)
            pipe.add("sync", lambda E: None,
                     waits=(f"STO.{S//128-2}", f"STO.{S//128-1}"))
            pipe.build()

    return nc


_NC_CACHE = {}


def kernel(L0_attn_x, L0_attn_wq, L0_attn_bq, L0_attn_wk, L0_attn_bk,
           L0_attn_wv, L0_attn_bv, L0_attn_wo, L0_attn_bo,
           L0_ffn_w1, L0_ffn_conv_w, L0_ffn_conv_b, L0_ffn_w2):
    x = np.asarray(L0_attn_x, dtype=np.float32)
    B = x.shape[0]
    assert x.shape == (B, S, D)

    def f32(a, shape):
        return np.ascontiguousarray(np.asarray(a, dtype=np.float32).reshape(shape))

    cw = f32(L0_ffn_conv_w, (32, 128, 3))
    common = {
        "wq": f32(L0_attn_wq, (D, D)), "wk": f32(L0_attn_wk, (D, D)),
        "wv": f32(L0_attn_wv, (D, D)), "wo": f32(L0_attn_wo, (D, D)),
        "bq": f32(L0_attn_bq, (1, D)), "bk": f32(L0_attn_bk, (1, D)),
        "bv": f32(L0_attn_bv, (1, D)), "bo": f32(L0_attn_bo, (1, D)),
        "w1": f32(L0_ffn_w1, (DF, D)), "w2": f32(L0_ffn_w2, (D, DF)),
        "cw0": np.ascontiguousarray(cw[:, :, 0].T),
        "cw1": np.ascontiguousarray(cw[:, :, 1].T),
        "cw2": np.ascontiguousarray(cw[:, :, 2].T),
        "cb": np.ascontiguousarray(f32(L0_ffn_conv_b, (32, 128)).T),
        "identb": np.eye(128, dtype=bfd),
        "identf": np.eye(128, dtype=np.float32),
        "onesb": np.ones((1, 128), dtype=bfd),
        "onesfr": np.ones((1, 128), dtype=np.float32),
        "onesfc": np.ones((128, 1), dtype=np.float32),
    }
    if "prog" not in _NC_CACHE:
        _NC_CACHE["prog"] = build_program()
    nc = _NC_CACHE["prog"]
    in_maps = [dict(common, x=np.ascontiguousarray(x[n])) for n in range(B)]
    res = run_bass_kernel_spmd(nc, in_maps, core_ids=list(range(B)))
    return np.stack([np.asarray(r["out"]) for r in res.results], axis=0)


# version 19
# speedup vs baseline: 1.0465x; 1.0212x over previous
import numpy as np
import ml_dtypes
from contextlib import ExitStack

import concourse.bass as bass
import concourse.mybir as mybir
from concourse.bass_utils import run_bass_kernel_spmd

F32 = mybir.dt.float32
BF16 = mybir.dt.bfloat16
AF = mybir.ActivationFunctionType
ALU = mybir.AluOpType
AX = mybir.AxisListType
bfd = ml_dtypes.bfloat16

S, D, H, HD, DF = 8192, 1024, 16, 64, 4096
NT = S // 128
NG = S // 512
CRND = 12582912.0
EPS_LN = 1e-5
EPS_Z = 1e-6
QEPS = 1e-5
NW1 = float(DF * D)

ENGINES = ("sync", "vector", "scalar", "tensor", "gpsimd")
_SU_MODE = 3


class _Op:
    __slots__ = ("eng", "fn", "waits", "name", "dma", "stream", "inc", "val", "gidx")


class Pipe:

    def __init__(self, nc, ctx, tag):
        self.nc = nc
        self.ctx = ctx
        self.tag = tag
        self.ops = []
        self.byname = {}
        self.sems = {}

    def _sem(self, eng, stream):
        key = (eng, stream)
        if key not in self.sems:
            name = f"{self.tag}_{eng}_{stream}" if stream else f"{self.tag}_{eng}"
            self.sems[key] = self.ctx.enter_context(self.nc.semaphore(name))
        return self.sems[key]

    def add(self, eng, fn, waits=(), name=None, dma=False, stream=None):
        op = _Op()
        op.eng, op.fn, op.name, op.dma = eng, fn, name, dma
        op.stream = (stream or "d") if dma else None
        op.waits = tuple(w for w in waits if w is not None)
        op.inc = False
        op.val = None
        op.gidx = len(self.ops)
        self.ops.append(op)
        if name is not None:
            assert name not in self.byname, name
            self.byname[name] = op
        return op

    def build(self):
        for op in self.ops:
            if op.dma:
                op.inc = True
            for w in op.waits:
                dep = self.byname[w]
                assert dep.gidx < op.gidx, (op.name, w)
                dep.inc = True
        counts = {}
        for op in self.ops:
            if op.inc:
                k = (op.eng, op.stream)
                counts[k] = counts.get(k, 0) + (16 if op.dma else 1)
                op.val = counts[k]
                self._sem(op.eng, op.stream)
        per_eng = {}
        for op in self.ops:
            per_eng.setdefault(op.eng, []).append(op)
        with self.nc.Block() as block:
            for eng in ENGINES:
                if eng not in per_eng:
                    continue
                ops = per_eng[eng]

                def mk(ops=ops):
                    def emit(E):
                        for op in ops:
                            for w in op.waits:
                                dep = self.byname[w]
                                E.wait_ge(self._sem(dep.eng, dep.stream), dep.val)
                            instr = op.fn(E)
                            if op.inc:
                                assert instr is not None, op.name
                                instr.then_inc(self._sem(op.eng, op.stream),
                                               16 if op.dma else 1)
                    return emit

                getattr(block, eng)(mk())


def build_program(dump_internal=False, stop_after=None):
    nc = bass.Bass("TRN2", target_bir_lowering=False, detect_race_conditions=False)

    x_d = nc.dram_tensor("x", [S, D], F32, kind="ExternalInput").ap()
    wq_d = nc.dram_tensor("wq", [D, D], F32, kind="ExternalInput").ap()
    wk_d = nc.dram_tensor("wk", [D, D], F32, kind="ExternalInput").ap()
    wv_d = nc.dram_tensor("wv", [D, D], F32, kind="ExternalInput").ap()
    wo_d = nc.dram_tensor("wo", [D, D], F32, kind="ExternalInput").ap()
    bq_d = nc.dram_tensor("bq", [1, D], F32, kind="ExternalInput").ap()
    bk_d = nc.dram_tensor("bk", [1, D], F32, kind="ExternalInput").ap()
    bv_d = nc.dram_tensor("bv", [1, D], F32, kind="ExternalInput").ap()
    bo_d = nc.dram_tensor("bo", [1, D], F32, kind="ExternalInput").ap()
    w1_d = nc.dram_tensor("w1", [DF, D], F32, kind="ExternalInput").ap()
    w2_d = nc.dram_tensor("w2", [D, DF], F32, kind="ExternalInput").ap()
    cw0_d = nc.dram_tensor("cw0", [128, 32], F32, kind="ExternalInput").ap()
    cw1_d = nc.dram_tensor("cw1", [128, 32], F32, kind="ExternalInput").ap()
    cw2_d = nc.dram_tensor("cw2", [128, 32], F32, kind="ExternalInput").ap()
    cb_d = nc.dram_tensor("cb", [128, 32], F32, kind="ExternalInput").ap()
    idb_d = nc.dram_tensor("identb", [128, 128], BF16, kind="ExternalInput").ap()
    idf_d = nc.dram_tensor("identf", [128, 128], F32, kind="ExternalInput").ap()
    ob_d = nc.dram_tensor("onesb", [1, 128], BF16, kind="ExternalInput").ap()
    ofr_d = nc.dram_tensor("onesfr", [1, 128], F32, kind="ExternalInput").ap()
    ofc_d = nc.dram_tensor("onesfc", [128, 1], F32, kind="ExternalInput").ap()

    out_d = nc.dram_tensor("out", [S, D], F32, kind="ExternalOutput").ap()
    ikind = "ExternalOutput" if dump_internal else "Internal"
    u_d = nc.dram_tensor("u_scr", [S, D], F32, kind=ikind).ap()
    h_d = nc.dram_tensor("h_scr", [DF, S], BF16, kind=ikind).ap()
    w1q_d = nc.dram_tensor("w1q_scr", [DF, D], BF16, kind=ikind).ap()
    w2q_d = nc.dram_tensor("w2q_scr", [D, DF], BF16, kind=ikind).ap()

    hview = h_d.rearrange("(c p) t -> p c t", p=128)
    w1qv = w1q_d.rearrange("(c p) j -> p c j", p=128)

    ctx = ExitStack()
    with ctx:
        def sb(name, shape, dt):
            return ctx.enter_context(nc.sbuf_tensor(name, shape, dt))

        identb = sb("identb_s", [128, 128], BF16)
        identf = sb("identf_s", [128, 128], F32)
        onesb = sb("onesb_s", [1, 128], BF16)
        onesfr = sb("onesfr_s", [1, 128], F32)
        onesfc = sb("onesfc_s", [128, 1], F32)
        bqb = sb("bqb", [1, D], BF16)
        bkb = sb("bkb", [1, D], BF16)
        bvb = sb("bvb", [1, D], BF16)
        bob = sb("bob", [1, D], BF16)
        cw0s = sb("cw0s", [128, 32], F32)
        cw1s = sb("cw1s", [128, 32], F32)
        cw2s = sb("cw2s", [128, 32], F32)
        cbs = sb("cbs", [128, 32], F32)
        wscal = sb("wscal", [128, 4], F32)
        epsln = sb("epsln", [128, 1], F32)

        ctxA = ctx.enter_context(ExitStack())
        w1qT = ctxA.enter_context(nc.sbuf_tensor("w1qT", [128, 8 * DF], BF16))
        kvblk = ctxA.enter_context(nc.sbuf_tensor("kvblk", [128, 8 * 128], BF16))
        ksumb = ctxA.enter_context(nc.sbuf_tensor("ksumb", [128, D], F32))

        def ln_head(pipe, pfx, j, src0, src1, st, bnrow, waits):
            def a_fn(E, src0=src0, src1=src1, st=st, bnrow=bnrow):
                E.bn_stats(bnrow[:, 0:6], src0)
                E.bn_stats(bnrow[:, 6:12], src1)
                E.drain()
                E.bn_aggr(st[:, 0:2], bnrow[:, :])
                E.drain()
                E.tensor_scalar(st[:, 2:3], st[:, 0:1], -1.0, None, ALU.mult)
                return E.drain()
            pipe.add("vector", a_fn, waits=waits, name=f"{pfx}A.{j}")

            def q_fn(E, st=st):
                E.activation(st[:, 4:5], st[:, 1:2], AF.Ln, bias=epsln[:, :])
                E.drain()
                return E.activation(st[:, 3:4], st[:, 4:5], AF.Exp, scale=-0.5)
            pipe.add("scalar", q_fn, waits=(f"{pfx}A.{j}",), name=f"{pfx}Q.{j}")

        with ExitStack() as cs:
            pipe = Pipe(nc, ctx, "su")
            wst = [cs.enter_context(nc.sbuf_tensor(f"wst{i}", [128, 2048], F32))
                   for i in range(2)]
            wtf = [cs.enter_context(nc.sbuf_tensor(f"wtf{i}", [128, 2048], F32))
                   for i in range(2)]
            wqst = [cs.enter_context(nc.sbuf_tensor(f"wqst{i}", [128, 2048], BF16))
                    for i in range(2)]
            wsums = cs.enter_context(nc.sbuf_tensor("wsums", [128, 32], F32))
            vv1 = cs.enter_context(nc.sbuf_tensor("vv1", [128, 2], F32))
            scrow = cs.enter_context(nc.sbuf_tensor("scrow", [1, 4], F32))
            w1p = [cs.enter_context(nc.sbuf_tensor(f"w1p{i}", [128, 512], BF16))
                   for i in range(4)]
            psmB = cs.enter_context(nc.psum_tensor("psmB", [1, 32], F32))
            psmC = cs.enter_context(nc.psum_tensor("psmC", [128, 4], F32))
            sutp = cs.enter_context(nc.psum_tensor("sutp", [128, 512], BF16))

            pipe.add("gpsimd", lambda E: E.memset(epsln[:, :], EPS_LN), name="eps")
            for nm, dst, srch in (("ldidb", identb, idb_d), ("ldidf", identf, idf_d),
                                  ("ldob", onesb, ob_d), ("ldofr", onesfr, ofr_d),
                                  ("ldofc", onesfc, ofc_d), ("ldcw0", cw0s, cw0_d),
                                  ("ldcw1", cw1s, cw1_d), ("ldcw2", cw2s, cw2_d),
                                  ("ldcb", cbs, cb_d)):
                pipe.add("sync", lambda E, d=dst, s=srch: E.dma_start(d[:, :], s[:, :]),
                         name=nm, dma=True, stream="cl")
            for nm, dst, srch in (("ldbq", bqb, bq_d), ("ldbk", bkb, bk_d),
                                  ("ldbv", bvb, bv_d), ("ldbo", bob, bo_d)):
                pipe.add("gpsimd", lambda E, d=dst, s=srch: E.dma_start(d[:, :], s[:, :]),
                         name=nm, dma=True, stream="bl")

            def w_piece_ap(widx, k):
                if widx == 0:
                    return w1_d.rearrange("(c p) j -> p c j", p=128)[:, 2 * k:2 * k + 2, :]
                c, hf = k // 2, k % 2
                return w2_d[c * 128:(c + 1) * 128, hf * 2048:(hf + 1) * 2048]

            def wq_piece_ap(widx, k):
                if widx == 0:
                    return w1qv[:, 2 * k:2 * k + 2, :]
                c, hf = k // 2, k % 2
                return w2q_d[c * 128:(c + 1) * 128, hf * 2048:(hf + 1) * 2048]

            for widx in (0, 1):
                for k in range(16):
                    kk = widx * 16 + k
                    pipe.add("sync",
                             lambda E, w=widx, k=k, kk=kk:
                             E.dma_start(wst[kk % 2][:, :].rearrange(
                                 "p (c j) -> p c j", j=1024) if w == 0
                                 else wst[kk % 2][:, :], w_piece_ap(w, k)),
                             waits=(f"ar{kk-2}",) if kk >= 2 else (),
                             name=f"al{kk}", dma=True, stream=f"a{kk % 2}")
                    pipe.add("vector",
                             lambda E, kk=kk:
                             E.reduce_sum(wsums[:, kk:kk + 1], wst[kk % 2][:, :],
                                          AX.X, apply_absolute_value=True),
                             waits=(f"al{kk}",), name=f"ar{kk}")

            def vra(E):
                E.drain()
                E.reduce_sum(vv1[0:128, 0:1], wsums[:, 0:16], AX.X)
                return E.reduce_sum(vv1[0:128, 1:2], wsums[:, 16:32], AX.X)
            pipe.add("vector", vra, waits=("ar31", "ldcb"), name="vrA")

            def tra(E):
                E.matmul(psmB[0:1, 0:1], vv1[0:128, 0:1], onesfc[:, 0:1],
                         start=True, stop=False)
                return E.matmul(psmB[0:1, 1:2], vv1[0:128, 1:2], onesfc[:, 0:1],
                                start=False, stop=True)
            pipe.add("tensor", tra, waits=("vrA",), name="trB")

            def mchain(E):
                E.tensor_scalar(scrow[0:1, 0:1], psmB[0:1, 0:1], 1.0 / NW1, QEPS,
                                ALU.mult, ALU.max)
                E.tensor_scalar(scrow[0:1, 2:3], psmB[0:1, 1:2], 1.0 / NW1, QEPS,
                                ALU.mult, ALU.max)
                E.drain()
                E.reciprocal(scrow[0:1, 1:2], scrow[0:1, 0:1])
                E.drain()
                return E.reciprocal(scrow[0:1, 3:4], scrow[0:1, 2:3])
            pipe.add("vector", mchain, waits=("trB",), name="mch")

            def bcmm(E):
                r = None
                for j in range(4):
                    r = E.matmul(psmC[:, j:j + 1], onesfr[0:1, :], scrow[0:1, j:j + 1],
                                 start=(j == 0), stop=(j == 3))
                return r
            pipe.add("tensor", bcmm, waits=("mch", "ldcb"), name="bcm")

            def evw(E):
                E.tensor_copy(wscal[:, :], psmC[:, :])
                return E.drain()
            pipe.add("vector", evw, waits=("bcm",), name="evw")

            for widx in (0, 1):
                swcol = wscal[:, 1:2] if widx == 0 else wscal[:, 3:4]
                for k in range(16):
                    kk = widx * 16 + k
                    pipe.add("gpsimd",
                             lambda E, w=widx, k=k, kk=kk:
                             E.dma_start(wst[kk % 2][:, :].rearrange(
                                 "p (c j) -> p c j", j=1024) if w == 0
                                 else wst[kk % 2][:, :], w_piece_ap(w, k)),
                             waits=(f"tq{kk-2}",) if kk >= 2 else ("ar31",),
                             name=f"tl{kk}", dma=True, stream=f"t{kk % 2}")

                    def trn(E, kk=kk, sc=swcol):
                        E.tensor_scalar(wtf[kk % 2][:, :], wst[kk % 2][:, :], sc, CRND,
                                        ALU.mult, ALU.add)
                        E.tensor_scalar(wtf[kk % 2][:, :], wtf[kk % 2][:, :], CRND, 1.0,
                                        ALU.subtract, ALU.min)
                        return E.tensor_scalar(wqst[kk % 2][:, :], wtf[kk % 2][:, :],
                                               -1.0, None, ALU.max)
                    pipe.add("vector", trn,
                             waits=(f"tl{kk}", "evw") + ((f"ts{kk-2}",) if kk >= 2 else ()),
                             name=f"tq{kk}")
                    pipe.add("sync",
                             lambda E, w=widx, k=k, kk=kk:
                             E.dma_start(wq_piece_ap(w, k),
                                         wqst[kk % 2][:, :].rearrange(
                                             "p (c j) -> p c j", j=1024) if w == 0
                                         else wqst[kk % 2][:, :]),
                             waits=(f"tq{kk}",), name=f"ts{kk}", dma=True,
                             stream=f"s{kk % 2}")

            for kb in range(32):
                for cfh in range(2):
                    i = kb * 2 + cfh
                    cf0 = cfh * 4
                    pipe.add("scalar",
                             lambda E, kb=kb, cf0=cf0, i=i: E.dma_start(
                                 w1p[i % 4][:, :],
                                 w1q_d[kb * 128:(kb + 1) * 128,
                                       cf0 * 128:(cf0 + 4) * 128]),
                             waits=((f"ts{kb // 2}",) if _SU_MODE >= 3 else ())
                             + ((f"wpt{i-4}",) if (i >= 4 and _SU_MODE >= 2) else ()),
                             name=f"wpl{i}", dma=True, stream=f"wp{i % 4}")
                    if _SU_MODE < 2:
                        continue

                    def wpt_fn(E, i=i):
                        r = None
                        for k in range(4):
                            r = E.transpose(
                                sutp[:, k * 128:(k + 1) * 128],
                                w1p[i % 4][:, k * 128:(k + 1) * 128], identb[:, :])
                        return r
                    pipe.add("tensor", wpt_fn,
                             waits=(f"wpl{i}",)
                             + ((f"we{i-1}",) if i >= 1 else ("ldidb",)),
                             name=f"wpt{i}")

                    def we_fn(E, kb=kb, cf0=cf0, i=i):
                        r = None
                        for k in range(4):
                            r = E.activation(
                                w1qT[:, (cf0 + k) * DF + kb * 128:
                                     (cf0 + k) * DF + (kb + 1) * 128],
                                sutp[:, k * 128:(k + 1) * 128], AF.Copy)
                        return r
                    pipe.add("scalar", we_fn, waits=(f"wpt{i}",), name=f"we{i}")
            pipe.add("scalar", lambda E: None,
                     waits=(("we63",) if _SU_MODE >= 2 else ()) + ("ts31", "ts30"))
            pipe.add("gpsimd", lambda E: None, waits=("ldbo",))
            pipe.add("sync", lambda E: None, waits=("ldcb",))
            pipe.build()
            su_ts30 = (pipe.sems[("sync", "s0")], pipe.byname["ts30"].val)
            su_ts31 = (pipe.sems[("sync", "s1")], pipe.byname["ts31"].val)
            su_we = ((pipe.sems[("scalar", None)], pipe.byname["we63"].val)
                     if _SU_MODE >= 2 else (None, 0))
        if stop_after == "su":
            return nc

        with ExitStack() as cs:
            pipe = Pipe(nc, ctx, "p1")
            wk_s = cs.enter_context(nc.sbuf_tensor("wk_s", [128, 8 * D], BF16))
            wv_s = cs.enter_context(nc.sbuf_tensor("wv_s", [128, 8 * D], BF16))
            xb = [cs.enter_context(nc.sbuf_tensor(f"xb{i}", [128, D], F32)) for i in range(2)]
            st = [cs.enter_context(nc.sbuf_tensor(f"st{i}", [128, 10], F32)) for i in range(2)]
            bnrow = cs.enter_context(nc.sbuf_tensor("bnrow", [128, 12], F32))
            xc = [cs.enter_context(nc.sbuf_tensor(f"p1xc{i}", [128, D], F32)) for i in range(2)]
            lnx = [cs.enter_context(nc.sbuf_tensor(f"lnx{i}", [128, D], BF16)) for i in range(2)]
            lnxT = [cs.enter_context(nc.sbuf_tensor(f"lnxT{i}", [128, D], BF16)) for i in range(2)]
            t1 = [cs.enter_context(nc.sbuf_tensor(f"t1{i}", [128, D], F32)) for i in range(2)]
            km = [cs.enter_context(nc.sbuf_tensor(f"km{i}", [128, D], BF16)) for i in range(2)]
            vs = [cs.enter_context(nc.sbuf_tensor(f"vs{i}", [128, D], BF16)) for i in range(2)]
            ksr = cs.enter_context(nc.sbuf_tensor("ksr", [128, D], F32))
            kv16ctx = ExitStack()
            kv16 = kv16ctx.enter_context(nc.psum_tensor("kv16", [64, 1024], F32))
            p1ps = ExitStack()
            tp = [p1ps.enter_context(nc.psum_tensor(f"tp{i}", [128, D], BF16)) for i in range(2)]
            pk = p1ps.enter_context(nc.psum_tensor("pk", [128, D], F32))
            pv = p1ps.enter_context(nc.psum_tensor("pv", [128, D], F32))

            pipe.add("gpsimd", lambda E: E.dma_start(
                wk_s[:, :].rearrange("p (c j) -> p c j", j=D),
                wk_d.rearrange("(c p) j -> p c j", p=128)), name="ldwk", dma=True,
                stream="wk")
            pipe.add("gpsimd", lambda E: E.dma_start(
                wv_s[:, :].rearrange("p (c j) -> p c j", j=D),
                wv_d.rearrange("(c p) j -> p c j", p=128)), name="ldwv", dma=True,
                stream="wv")
            pipe.add("gpsimd", lambda E: E.memset(ksr[:, :], 0.0), name="ksz")

            def proj(E, lT, w_s, brow, ps):
                r = None
                for hf in (0, 1):
                    for c in range(8):
                        E.matmul(ps[:, hf * 512:(hf + 1) * 512],
                                 lT[:, c * 128:(c + 1) * 128],
                                 w_s[:, c * D + hf * 512:c * D + hf * 512 + 512],
                                 start=(c == 0), stop=False)
                    r = E.matmul(ps[:, hf * 512:(hf + 1) * 512], onesb[0:1, :],
                                 brow[0:1, hf * 512:(hf + 1) * 512],
                                 start=False, stop=True)
                return r

            def h1_vs(i):
                ip = i % 2
                pipe.add("sync",
                         lambda E, i=i, ip=ip: E.dma_start(
                             xb[ip][:, :], x_d[i * 128:(i + 1) * 128, :]),
                         waits=(f"C.{i-2}",) if i >= 2 else (),
                         name=f"LD.{i}", dma=True, stream=f"x{i % 2}")
                ln_head(pipe, "", i, xb[ip][:, 0:512], xb[ip][:, 512:1024],
                        st[ip], bnrow,
                        waits=(f"LD.{i}",) + ((f"P.{i-2}",) if i >= 2 else ()))

                def c_fn(E, ip=ip):
                    return E.tensor_scalar(xc[ip][:, :], xb[ip][:, :], st[ip][:, 2:3],
                                           None, ALU.add)
                pipe.add("vector", c_fn,
                         waits=(f"Q.{i}",) + ((f"P.{i-2}",) if i >= 2 else ()),
                         name=f"C.{i}")
                pipe.add("scalar",
                         lambda E, ip=ip: E.activation(lnx[ip][:, :], xc[ip][:, :],
                                                       AF.Identity,
                                                       scale=st[ip][:, 3:4]),
                         waits=(f"C.{i}",) + ((f"P1.{i-2}",) if i >= 2 else ()),
                         name=f"P.{i}")

            def h1_pe(i):
                ip = i % 2

                def p1_fn(E, ip=ip):
                    r = None
                    for c in range(8):
                        r = E.transpose(tp[ip][:, c * 128:(c + 1) * 128],
                                        lnx[ip][:, c * 128:(c + 1) * 128], identb[:, :])
                    return r
                pipe.add("tensor", p1_fn,
                         waits=(f"P.{i}",) + ((f"V3.{i-2}",) if i >= 2 else ()),
                         name=f"P1.{i}")

            def h2(i):
                ip = i % 2
                pipe.add("vector",
                         lambda E, ip=ip: E.tensor_copy(lnxT[ip][:, :], tp[ip][:, :]),
                         waits=(f"P1.{i}",) + ((f"P3.{i-2}",) if i >= 2 else ()),
                         name=f"V3.{i}")
                if i + 2 < NT:
                    h1_vs(i + 2)
                pipe.add("tensor", lambda E, ip=ip: proj(E, lnxT[ip], wk_s, bkb, pk),
                         waits=(f"V3.{i}",)
                         + ((f"V5.{i-1}",) if i >= 1 else ("ldwk",)),
                         name=f"P2.{i}")
                pipe.add("tensor", lambda E, ip=ip: proj(E, lnxT[ip], wv_s, bvb, pv),
                         waits=((f"S6.{i-1}",) if i >= 1 else ("ldwv",)),
                         name=f"P3.{i}")
                pipe.add("vector",
                         lambda E, ip=ip: E.tensor_scalar(t1[ip][:, :], pk[:, :], 0.0,
                                                          None, ALU.min),
                         waits=(f"P2.{i}",), name=f"V4.{i}")
                pipe.add("scalar",
                         lambda E, ip=ip: E.activation(t1[ip][:, :], t1[ip][:, :], AF.Exp),
                         waits=(f"V4.{i}",), name=f"S4.{i}")
                pipe.add("vector",
                         lambda E, ip=ip: E.scalar_tensor_tensor(
                             km[ip][:, :], pk[:, :], 0.0, t1[ip][:, :],
                             ALU.max, ALU.add),
                         waits=(f"S4.{i}",)
                         + ((f"P4.{i-2}",) if i >= 2 else ()),
                         name=f"V5.{i}")
                pipe.add("scalar",
                         lambda E, ip=ip: E.activation(vs[ip][:, :], pv[:, :], AF.Copy),
                         waits=(f"P3.{i}",) + ((f"P4.{i-2}",) if i >= 2 else ()),
                         name=f"S6.{i}")
                pipe.add("vector",
                         lambda E, ip=ip: E.tensor_add(ksr[:, :], ksr[:, :], km[ip][:, :]),
                         waits=(("ksz",) if i == 0 else ()), name=f"V6.{i}")

                def p4(E, i=i, ip=ip):
                    r = None
                    for h in range(16):
                        r = E.matmul(kv16[:, h * 64:(h + 1) * 64],
                                     km[ip][:, h * 64:(h + 1) * 64],
                                     vs[ip][:, h * 64:(h + 1) * 64],
                                     start=(i == 0 and h % 8 == 0),
                                     stop=(i == NT - 1 and h % 8 == 7))
                    return r
                pipe.add("tensor", p4, waits=(f"V5.{i}", f"S6.{i}"), name=f"P4.{i}")
                if i + 2 < NT:
                    h1_pe(i + 2)

            h1_vs(0)
            h1_pe(0)
            h1_vs(1)
            h1_pe(1)
            for i in range(NT):
                h2(i)

            pipe.build()
            p1ps.close()

            with ExitStack() as ce:
                pipe = Pipe(nc, ctx, "p1e")
                kvsb = ce.enter_context(nc.sbuf_tensor("kvsb", [64, 1024], BF16))
                ksrow = ce.enter_context(nc.sbuf_tensor("ksrow", [1, D], F32))
                pks = ce.enter_context(nc.psum_tensor("pks", [1, D], F32))
                pkb = ce.enter_context(nc.psum_tensor("pkb", [128, D], F32))
                pipe.add("vector", lambda E: E.tensor_copy(kvsb[:, :], kv16[:, :]),
                         name="evkv")
                pipe.add("gpsimd", lambda E: E.memset(kvblk[:, :], 0.0), name="kvz")
                for h in range(16):
                    p, odd = h // 2, h % 2
                    pipe.add("gpsimd",
                             lambda E, h=h, p=p, odd=odd: E.dma_start(
                                 kvblk[odd * 64:odd * 64 + 64,
                                       p * 128 + odd * 64:p * 128 + odd * 64 + 64],
                                 kvsb[0:64, h * 64:(h + 1) * 64]),
                             waits=("evkv", "kvz"), name=f"kvb{h}", dma=True, stream="kv")

                def cs_mm(E):
                    r = None
                    for hf in (0, 1):
                        r = E.matmul(pks[0:1, hf * 512:(hf + 1) * 512], onesfc[:, 0:1],
                                     ksr[:, hf * 512:(hf + 1) * 512], start=True, stop=True)
                    return r
                pipe.add("tensor", cs_mm, name="csm")
                pipe.add("vector", lambda E: E.tensor_copy(ksrow[:, :], pks[:, :]),
                         waits=("csm",), name="evks")

                def kb_mm(E):
                    r = None
                    for hf in (0, 1):
                        r = E.matmul(pkb[:, hf * 512:(hf + 1) * 512], onesfr[0:1, :],
                                     ksrow[0:1, hf * 512:(hf + 1) * 512],
                                     start=True, stop=True)
                    return r
                pipe.add("tensor", kb_mm, waits=("evks",), name="kbm")
                pipe.add("vector", lambda E: E.tensor_copy(ksumb[:, :], pkb[:, :]),
                         waits=("kbm",), name="evkb")
                pipe.add("gpsimd", lambda E: None, waits=("kvb15",))
                pipe.build()
            kv16ctx.close()
        if stop_after == "p1":
            return nc

        with ExitStack() as cs:
            pipe = Pipe(nc, ctx, "a")
            wq_s = cs.enter_context(nc.sbuf_tensor("wq_s", [128, 8 * D], BF16))
            wo_s = cs.enter_context(nc.sbuf_tensor("wo_s", [128, 8 * D], BF16))
            xb = [cs.enter_context(nc.sbuf_tensor(f"axb{i}", [128, D], F32)) for i in range(7)]
            st = [cs.enter_context(nc.sbuf_tensor(f"ast{i}", [128, 10], F32)) for i in range(2)]
            st2 = [cs.enter_context(nc.sbuf_tensor(f"ast2{i}", [128, 10], F32)) for i in range(2)]
            bnrow = cs.enter_context(nc.sbuf_tensor("abnrow", [128, 12], F32))
            bnrow2 = cs.enter_context(nc.sbuf_tensor("abnrow2", [128, 12], F32))
            xc = [cs.enter_context(nc.sbuf_tensor(f"axc{i}", [128, D], F32)) for i in range(2)]
            xc2 = cs.enter_context(nc.sbuf_tensor("axc2", [128, D], F32))
            lnx = [cs.enter_context(nc.sbuf_tensor(f"alnx{i}", [128, D], BF16)) for i in range(2)]
            lnxT = cs.enter_context(nc.sbuf_tensor("alnxT", [128, D], BF16))
            t1 = cs.enter_context(nc.sbuf_tensor("at1", [128, D], F32))
            qm = cs.enter_context(nc.sbuf_tensor("aqm", [128, D], F32))
            zden = cs.enter_context(nc.sbuf_tensor("azden", [128, 16], F32))
            qms = cs.enter_context(nc.sbuf_tensor("aqms", [128, D], BF16))
            qmsT = cs.enter_context(nc.sbuf_tensor("aqmsT", [128, 8 * 512], BF16))
            yT = cs.enter_context(nc.sbuf_tensor("ayT", [128, 8 * 512], BF16))
            ust = [cs.enter_context(nc.sbuf_tensor(f"aust{i}", [128, D], F32)) for i in range(2)]
            tp = [cs.enter_context(nc.psum_tensor(f"atp{i}", [128, D], BF16)) for i in range(2)]
            qp = cs.enter_context(nc.psum_tensor("aqp", [128, D], F32))
            yp = [cs.enter_context(nc.psum_tensor(f"ayp{i}", [128, 512], F32)) for i in range(2)]
            ao = cs.enter_context(nc.psum_tensor("aao", [128, D], F32))

            pipe.add("gpsimd", lambda E: E.dma_start(
                wq_s[:, :].rearrange("p (c j) -> p c j", j=D),
                wq_d.rearrange("(c p) j -> p c j", p=128)), name="ldwq", dma=True,
                stream="wq")
            pipe.add("gpsimd", lambda E: E.dma_start(
                wo_s[:, :].rearrange("p (c j) -> p c j", j=D),
                wo_d.rearrange("(c p) j -> p c j", p=128)), name="ldwo", dma=True,
                stream="wo")

            qmsTv = qmsT[:, :].rearrange("p (c t) -> p c t", t=512)

            aob = cs.enter_context(nc.sbuf_tensor("aaob", [128, D], F32))
            qb = [cs.enter_context(nc.sbuf_tensor(f"aqb{i}", [128, D], F32))
                  for i in range(2)]

            def h1a_vs(j):
                jp, j4 = j % 2, j % 7
                pipe.add("sync",
                         lambda E, j=j, j4=j4: E.dma_start(
                             xb[j4][:, :], x_d[j * 128:(j + 1) * 128, :]),
                         waits=(f"RC.{j-7}",) if j >= 7 else (),
                         name=f"LDX.{j}", dma=True, stream=f"x{j % 7}")
                ln_head(pipe, "", j, xb[j4][:, 0:512], xb[j4][:, 512:1024],
                        st[jp], bnrow,
                        waits=(f"LDX.{j}",) + ((f"P.{j-2}",) if j >= 2 else ()))

                def c_fn(E, jp=jp, j4=j4):
                    return E.tensor_scalar(xc[jp][:, :], xb[j4][:, :], st[jp][:, 2:3],
                                           None, ALU.add)
                pipe.add("vector", c_fn,
                         waits=(f"Q.{j}",) + ((f"P.{j-2}",) if j >= 2 else ()),
                         name=f"C.{j}")
                pipe.add("scalar",
                         lambda E, jp=jp: E.activation(lnx[jp][:, :], xc[jp][:, :],
                                                       AF.Identity,
                                                       scale=st[jp][:, 3:4]),
                         waits=(f"C.{j}",) + ((f"P1.{j-2}",) if j >= 2 else ()),
                         name=f"P.{j}")

            def h1a_pe(j):
                jp = j % 2

                def p1_fn(E, jp=jp):
                    r = None
                    for c in range(8):
                        r = E.transpose(tp[jp][:, c * 128:(c + 1) * 128],
                                        lnx[jp][:, c * 128:(c + 1) * 128],
                                        identb[:, :])
                    return r
                pipe.add("tensor", p1_fn,
                         waits=(f"P.{j}",) + ((f"V8.{j-2}",) if j >= 2 else ()),
                         name=f"P1.{j}")

            def h2a(j):
                g, s_ = j // 4, j % 4
                jp = j % 2
                pipe.add("vector",
                         lambda E, jp=jp: E.tensor_copy(lnxT[:, :], tp[jp][:, :]),
                         waits=(f"P1.{j}",) + ((f"P2.{j-1}",) if j >= 1 else ()),
                         name=f"V3.{j}")
                if j + 2 < S // 128:
                    h1a_vs(j + 2)
                pipe.add("tensor", lambda E: proj(E, lnxT, wq_s, bqb, qp),
                         waits=(f"V3.{j}",)
                         + ((f"SCQ.{j-1}",) if j >= 1 else ("ldwq",)),
                         name=f"P2.{j}")
                pipe.add("scalar",
                         lambda E, jp=jp: E.activation(qb[jp][:, :], qp[:, :], AF.Copy),
                         waits=(f"P2.{j}",) + ((f"V5.{j-2}",) if j >= 2 else ()),
                         name=f"SCQ.{j}")
                pipe.add("vector",
                         lambda E, jp=jp: E.tensor_scalar(t1[:, :], qb[jp][:, :], 0.0,
                                                          None, ALU.min),
                         waits=(f"SCQ.{j}",), name=f"V4.{j}")
                pipe.add("scalar",
                         lambda E: E.activation(t1[:, :], t1[:, :], AF.Exp),
                         waits=(f"V4.{j}",), name=f"S4.{j}")
                pipe.add("vector",
                         lambda E, jp=jp: E.scalar_tensor_tensor(
                             qm[:, :], qb[jp][:, :], 0.0, t1[:, :],
                             ALU.max, ALU.add),
                         waits=(f"S4.{j}",), name=f"V5.{j}")

                def v6(E):
                    E.tensor_mul(t1[:, :], qm[:, :], ksumb[:, :])
                    E.tensor_reduce(zden[:, :],
                                    t1[:, :].rearrange("p (h d) -> p h d", d=64),
                                    AX.X, ALU.add)
                    E.drain()
                    E.tensor_scalar(zden[:, :], zden[:, :], EPS_Z, None,
                                    ALU.add)
                    E.drain()
                    E.reciprocal(zden[:, :], zden[:, :])
                    return E.drain()
                pipe.add("vector", v6, name=f"V6.{j}")

                def v7(E):
                    r = None
                    for h in range(16):
                        r = E.tensor_scalar(qms[:, h * 64:(h + 1) * 64],
                                            qm[:, h * 64:(h + 1) * 64],
                                            zden[:, h:h + 1], None, ALU.mult)
                    return r
                pipe.add("vector", v7,
                         waits=((f"P3.{j-1}",) if j >= 1 else ()), name=f"V7.{j}")

                def p3(E, jp=jp):
                    r = None
                    for c in range(8):
                        r = E.transpose(tp[jp][:, c * 128:(c + 1) * 128],
                                        qms[:, c * 128:(c + 1) * 128],
                                        identb[:, :])
                    return r
                pipe.add("tensor", p3, waits=(f"V7.{j}", f"V3.{j}"), name=f"P3.{j}")
                pipe.add("vector",
                         lambda E, jp=jp, s_=s_: E.tensor_copy(
                             qmsTv[:, :, s_ * 128:(s_ + 1) * 128],
                             tp[jp][:, :].rearrange("p (c t) -> p c t", t=128)),
                         waits=(f"P3.{j}",)
                         + ((f"PY7.{g-1}",) if (s_ == 0 and g >= 1) else ()),
                         name=f"V8.{j}")
                if j + 2 < S // 128:
                    h1a_pe(j + 2)


            def ystage(g):
                for p in range(8):
                    pipe.add("tensor",
                             lambda E, p=p: E.matmul(
                                 yp[p % 2][:, :], kvblk[:, p * 128:(p + 1) * 128],
                                 qmsT[:, p * 512:(p + 1) * 512], start=True, stop=True),
                             waits=((f"V8.{g*4+3}",) if p == 0 else ())
                             + ((f"SY{p-2}.{g}",) if p >= 2 else ()),
                             name=f"PY{p}.{g}")
                    pipe.add("scalar",
                             lambda E, p=p: E.activation(
                                 yT[:, p * 512:(p + 1) * 512],
                                 yp[p % 2][:, :], AF.Copy),
                             waits=(f"PY{p}.{g}",)
                             + ((f"P5.{(g-1)*4+3}",) if (p == 0 and g >= 1) else ()),
                             name=f"SY{p}.{g}")

            def tail(j):
                g, s_ = j // 4, j % 4
                jp, j4 = j % 2, j % 7

                def p5(E, s_=s_):
                    r = None
                    for hf in (0, 1):
                        for c in range(8):
                            E.matmul(ao[:, hf * 512:(hf + 1) * 512],
                                     yT[:, c * 512 + s_ * 128:c * 512 + (s_ + 1) * 128],
                                     wo_s[:, c * D + hf * 512:c * D + hf * 512 + 512],
                                     start=(c == 0), stop=False)
                        r = E.matmul(ao[:, hf * 512:(hf + 1) * 512], onesb[0:1, :],
                                     bob[0:1, hf * 512:(hf + 1) * 512],
                                     start=False, stop=True)
                    return r
                pipe.add("tensor", p5,
                         waits=(f"SY7.{g}",)
                         + ((f"SC.{j-1}",) if j >= 1 else ("ldwo",)),
                         name=f"P5.{j}")
                pipe.add("scalar",
                         lambda E: E.activation(aob[:, :], ao[:, :], AF.Copy),
                         waits=(f"P5.{j}",) + ((f"RC.{j-1}",) if j >= 1 else ()),
                         name=f"SC.{j}")
                ln_head(pipe, "R", j, aob[:, 0:512], aob[:, 512:1024],
                        st2[jp], bnrow2,
                        waits=(f"SC.{j}",) + ((f"RC.{j-2}",) if j >= 2 else ()))

                def rc_fn(E, jp=jp, j4=j4):
                    E.tensor_scalar(xc2[:, :], aob[:, :], st2[jp][:, 2:3],
                                    None, ALU.add)
                    return E.scalar_tensor_tensor(ust[jp][:, :], xc2[:, :],
                                                  st2[jp][:, 3:4], xb[j4][:, :],
                                                  ALU.mult, ALU.add)
                pipe.add("vector", rc_fn,
                         waits=(f"RQ.{j}",) + ((f"STU.{j-2}",) if j >= 2 else ()),
                         name=f"RC.{j}")
                pipe.add("sync",
                         lambda E, j=j, jp=jp: E.dma_start(
                             u_d[j * 128:(j + 1) * 128, :], ust[jp][:, :]),
                         waits=(f"RC.{j}",), name=f"STU.{j}", dma=True,
                         stream=f"u{j % 2}")

            h1a_vs(0)
            h1a_pe(0)
            h1a_vs(1)
            h1a_pe(1)
            for g in range(NG):
                for s_ in range(4):
                    h2a(g * 4 + s_)
                ystage(g)
                for s_ in range(4):
                    tail(g * 4 + s_)

            pipe.add("sync", lambda E: None,
                     waits=(f"STU.{S//128-2}", f"STU.{S//128-1}"))
            pipe.build()
        if stop_after == "a":
            return nc

        with ExitStack() as cs:
            pipe = Pipe(nc, ctx, "b1")
            ub = [cs.enter_context(nc.sbuf_tensor(f"bub{i}", [128, D], F32)) for i in range(4)]
            st = [cs.enter_context(nc.sbuf_tensor(f"bst{i}", [128, 10], F32)) for i in range(4)]
            bnrow = [cs.enter_context(nc.sbuf_tensor(f"bbnrow{i}", [128, 12], F32))
                     for i in range(4)]
            xc = [cs.enter_context(nc.sbuf_tensor(f"bxc{i}", [128, D], F32)) for i in range(2)]
            t1 = cs.enter_context(nc.sbuf_tensor("bt1", [128, D], F32))
            xqg = [cs.enter_context(nc.sbuf_tensor(f"bxqg{i}", [128, D], BF16)) for i in range(4)]
            xqT = [cs.enter_context(nc.sbuf_tensor(f"bxqT{i}", [128, 8 * 512], BF16))
                   for i in range(2)]
            hst = [cs.enter_context(nc.sbuf_tensor(f"bhst{i}", [128, 2048], BF16))
                   for i in range(2)]
            tp = [cs.enter_context(nc.psum_tensor(f"btp{i}", [128, D], BF16)) for i in range(2)]
            hp = [cs.enter_context(nc.psum_tensor(f"bhp{i}", [128, 512], F32)) for i in range(4)]

            def front_vs(j):
                jq = j % 4
                jp = j % 2
                pipe.add("gpsimd",
                         lambda E, j=j, jq=jq: E.dma_start(
                             ub[jq][:, :], u_d[j * 128:(j + 1) * 128, :]),
                         waits=(f"C.{j-4}",) if j >= 4 else (),
                         name=f"LDU.{j}", dma=True, stream=f"u{j % 4}")
                ln_head(pipe, "", j, ub[jq][:, 0:512], ub[jq][:, 512:1024],
                        st[jq], bnrow[jq],
                        waits=(f"LDU.{j}",) + ((f"V5.{j-4}",) if j >= 4 else ()))

                def c_fn(E, jp=jp, jq=jq):
                    E.tensor_scalar(xc[jp][:, :], ub[jq][:, :], st[jq][:, 2:3],
                                    None, ALU.add)
                    E.reduce_max(st[jq][:, 5:6], xc[jp][:, :], AX.X,
                                 apply_absolute_value=True)
                    return E.drain()
                pipe.add("vector", c_fn,
                         waits=(f"Q.{j}",) + ((f"V5.{j-2}",) if j >= 2 else ()),
                         name=f"C.{j}")

                def v4(E, jq=jq):
                    E.tensor_scalar(st[jq][:, 6:7], st[jq][:, 5:6], st[jq][:, 3:4],
                                    None, ALU.mult)
                    E.drain()
                    E.tensor_scalar(st[jq][:, 6:7], st[jq][:, 6:7], QEPS,
                                    1.0 / 127.0, ALU.max, ALU.mult)
                    E.drain()
                    E.reciprocal(st[jq][:, 7:8], st[jq][:, 6:7])
                    E.tensor_scalar(st[jq][:, 8:9], st[jq][:, 6:7], wscal[:, 0:1],
                                    None, ALU.mult)
                    E.drain()
                    E.tensor_scalar(st[jq][:, 9:10], st[jq][:, 7:8], st[jq][:, 3:4],
                                    None, ALU.mult)
                    return E.drain()
                pipe.add("vector", v4, name=f"V4.{j}")

                def v5(E, jp=jp, jq=jq, j4=j % 4):
                    E.tensor_scalar(t1[:, :], xc[jp][:, :], st[jq][:, 9:10], CRND,
                                    ALU.mult, ALU.add)
                    return E.tensor_scalar(xqg[j4][:, :], t1[:, :], CRND,
                                           st[jq][:, 8:9],
                                           ALU.subtract, ALU.mult)
                pipe.add("vector", v5,
                         waits=((f"P1.{j-4}",) if j >= 4 else ()), name=f"V5.{j}")

            def front_pe(j):
                jp, j4 = j % 2, j % 4
                gp = (j // 4) % 2
                s_ = j % 4

                def p1_fn(E, j4=j4, jp=jp):
                    r = None
                    for c in range(8):
                        r = E.transpose(tp[jp][:, c * 128:(c + 1) * 128],
                                        xqg[j4][:, c * 128:(c + 1) * 128],
                                        identb[:, :])
                    return r
                pipe.add("tensor", p1_fn,
                         waits=(f"V5.{j}",) + ((f"V6.{j-2}",) if j >= 2 else ()),
                         name=f"P1.{j}")
                pipe.add("vector",
                         lambda E, jp=jp, gp=gp, s_=s_: E.tensor_copy(
                             xqT[gp][:, :].rearrange("p (c t) -> p c t", t=512)
                             [:, :, s_ * 128:(s_ + 1) * 128],
                             tp[jp][:, :].rearrange("p (c t) -> p c t", t=128)),
                         waits=(f"P1.{j}",)
                         + ((f"PW31.{(j//4)-2}",) if (s_ == 0 and j >= 8) else ()),
                         name=f"V6.{j}")

            for s_ in range(4):
                front_vs(s_)
            for s_ in range(4):
                front_pe(s_)
            pe_at = {8: 0, 16: 1, 24: 2, 28: 3}
            for g in range(NG):
                gp = g % 2
                if g + 1 < NG:
                    for k in range(4):
                        front_vs(4 * (g + 1) + k)
                for c in range(32):
                    F = g * 8 + c // 4

                    def pw(E, c=c, gp=gp, g=g):
                        if g == 0 and c < 4:
                            E.wait_ge(su_we[0], su_we[1])
                        r = None
                        for cf in range(8):
                            r = E.matmul(hp[c % 4][:, :],
                                         w1qT[:, cf * DF + c * 128:cf * DF + (c + 1) * 128],
                                         xqT[gp][:, cf * 512:(cf + 1) * 512],
                                         start=(cf == 0), stop=(cf == 7))
                        return r
                    pipe.add("tensor", pw,
                             waits=(f"V6.{g*4+3}",)
                             + ((f"SE{c-4}.{g}",) if c >= 4 else
                                ((f"SE{28+c}.{g-1}",) if g >= 1 else ())),
                             name=f"PW{c}.{g}")
                    pipe.add("scalar",
                             lambda E, c=c: E.activation(
                                 hst[(c // 4) % 2][:, (c % 4) * 512:(c % 4 + 1) * 512],
                                 hp[c % 4][:, :], AF.Copy),
                             waits=(f"PW{c}.{g}",)
                             + ((f"STH.{F-2}",) if (c % 4 == 0 and F >= 2) else ()),
                             name=f"SE{c}.{g}")
                    if c % 4 == 3:
                        pipe.add("sync",
                                 lambda E, c=c, g=g: E.dma_start(
                                     hview[:, c - 3:c + 1, g * 512:(g + 1) * 512],
                                     hst[(c // 4) % 2][:, :].rearrange(
                                         "p (s t) -> p s t", t=512)),
                                 waits=(f"SE{c}.{g}", f"SE{c-1}.{g}",
                                        f"SE{c-2}.{g}", f"SE{c-3}.{g}"),
                                 name=f"STH.{F}", dma=True, stream=f"h{F % 2}")
                    if c in pe_at and g + 1 < NG:
                        front_pe(4 * (g + 1) + pe_at[c])
            pipe.add("sync", lambda E: None,
                     waits=(f"STH.{NG*8-2}", f"STH.{NG*8-1}"))
            pipe.build()
        if stop_after == "b1":
            return nc
        ctxA.close()

        with ExitStack() as cs:
            pipe = Pipe(nc, ctx, "b2")
            w2T = cs.enter_context(nc.sbuf_tensor("w2T", [128, 32 * D], BF16))
            w2p = [cs.enter_context(nc.sbuf_tensor(f"w2p{i}", [128, 512], BF16))
                   for i in range(4)]
            hload = cs.enter_context(nc.sbuf_tensor("hload", [128, 8 * 514], BF16))
            hsil = cs.enter_context(nc.sbuf_tensor("hsil", [128, 32 * 512], BF16))
            cvt = [cs.enter_context(nc.sbuf_tensor(f"cvt{i}", [128, 512], BF16))
                   for i in range(8)]
            tqf = [cs.enter_context(nc.sbuf_tensor(f"tqf{i}", [128, 512], F32))
                   for i in range(2)]
            rmax = cs.enter_context(nc.sbuf_tensor("rmax", [128, 512], BF16))
            rmin = cs.enter_context(nc.sbuf_tensor("rmin", [128, 512], BF16))
            s2st = cs.enter_context(nc.sbuf_tensor("s2st", [128, 8], F32))
            s2row = cs.enter_context(nc.sbuf_tensor("s2row", [1, 512], F32))
            s2b = cs.enter_context(nc.sbuf_tensor("s2b", [128, 512], BF16))
            hqf = cs.enter_context(nc.sbuf_tensor("hqf", [128, 32 * 512], BF16))
            st2 = [cs.enter_context(nc.sbuf_tensor(f"cst{i}", [128, 10], F32)) for i in range(2)]
            bnrow2 = cs.enter_context(nc.sbuf_tensor("cbnrow", [128, 12], F32))
            xc2 = cs.enter_context(nc.sbuf_tensor("cxc2", [128, D], F32))
            ub = [cs.enter_context(nc.sbuf_tensor(f"cub{i}", [128, D], F32)) for i in range(2)]
            ost = [cs.enter_context(nc.sbuf_tensor(f"cost{i}", [128, D], F32)) for i in range(2)]
            oa = cs.enter_context(nc.psum_tensor("coa", [128, D], F32))
            ob = cs.enter_context(nc.psum_tensor("cob", [128, D], F32))
            tps = cs.enter_context(nc.psum_tensor("ctps", [128, 512], BF16))
            tpf = cs.enter_context(nc.psum_tensor("ctpf", [1, 512], F32))
            s2p = cs.enter_context(nc.psum_tensor("cs2p", [128, 512], F32))
            w2tp = cs.enter_context(nc.psum_tensor("cw2tp", [128, 512], BF16))

            hlv = hload[:, :].rearrange("p (c t) -> p c t", t=514)
            pipe.add("gpsimd",
                     lambda E: E.memset(hlv[:, :, 0:1], 0.0), name="hz0")

            def w2_load(i):
                ccb, fb = i // 8, i % 8

                def ld(E, ccb=ccb, fb=fb, i=i):
                    if i == 0:
                        E.wait_ge(su_ts30[0], su_ts30[1])
                        E.wait_ge(su_ts31[0], su_ts31[1])
                    return E.dma_start(
                        w2p[i % 4][:, :],
                        w2q_d[fb * 128:(fb + 1) * 128,
                              ccb * 512:(ccb + 1) * 512])
                pipe.add("gpsimd", ld,
                         waits=(f"W2T{i-4}",) if i >= 4 else (),
                         name=f"w2pl{i}", dma=True, stream=f"w2p{i % 4}")

            def w2_tr_ev(i):
                ccb, fb = i // 8, i % 8

                def tr(E, i=i):
                    r = None
                    for k in range(4):
                        r = E.transpose(
                            w2tp[:, k * 128:(k + 1) * 128],
                            w2p[i % 4][:, k * 128:(k + 1) * 128], identb[:, :])
                    return r
                pipe.add("tensor", tr,
                         waits=(f"w2pl{i}",)
                         + ((f"W2E{i-1}",) if i >= 1 else ()),
                         name=f"W2T{i}")

                def ev(E, ccb=ccb, fb=fb, i=i):
                    r = None
                    for k in range(4):
                        cc = ccb * 4 + k
                        r = E.activation(
                            w2T[:, cc * D + fb * 128:cc * D + (fb + 1) * 128],
                            w2tp[:, k * 128:(k + 1) * 128], AF.Copy)
                    return r
                pipe.add("scalar", ev, waits=(f"W2T{i}",), name=f"W2E{i}")

            def conv_one(g, c):
                cs8 = c % 8
                lhw = []
                if c >= 8:
                    lhw.append(f"CV{c-8}.{g}")
                elif g >= 1:
                    lhw.append(f"CV{c+24}.{g-1}")
                if g == 0:
                    pipe.add("sync",
                             lambda E, c=c, cs8=cs8: E.dma_start(
                                 hlv[:, cs8:cs8 + 1, 1:514],
                                 hview[:, c:c + 1, 0:513]),
                             waits=tuple(lhw), name=f"LH{c}.{g}", dma=True,
                             stream=f"lh{c % 8}")
                elif g == NG - 1:
                    pipe.add("sync",
                             lambda E, c=c, g=g, cs8=cs8: E.dma_start(
                                 hlv[:, cs8:cs8 + 1, 0:513],
                                 hview[:, c:c + 1, g * 512 - 1:S]),
                             waits=tuple(lhw), name=f"LH{c}.{g}", dma=True,
                             stream=f"lh{c % 8}")
                else:
                    pipe.add("sync",
                             lambda E, c=c, g=g, cs8=cs8: E.dma_start(
                                 hlv[:, cs8:cs8 + 1, :],
                                 hview[:, c:c + 1, g * 512 - 1:g * 512 + 513]),
                             waits=tuple(lhw), name=f"LH{c}.{g}", dma=True,
                             stream=f"lh{c % 8}")

                def cv(E, c=c, cs8=cs8):
                    E.tensor_scalar(cvt[c % 8][:, :], hlv[:, cs8, 0:512],
                                    cw0s[:, c:c + 1], None, ALU.mult)
                    E.scalar_tensor_tensor(cvt[c % 8][:, :], hlv[:, cs8, 1:513],
                                           cw1s[:, c:c + 1], cvt[c % 8][:, :],
                                           ALU.mult, ALU.add)
                    return E.scalar_tensor_tensor(cvt[c % 8][:, :], hlv[:, cs8, 2:514],
                                                  cw2s[:, c:c + 1], cvt[c % 8][:, :],
                                                  ALU.mult, ALU.add)
                waits = [f"LH{c}.{g}"]
                if c >= 8:
                    waits.append(f"SL{c-8}.{g}")
                elif g >= 1:
                    waits.append(f"SL{24+c}.{g-1}")
                if g == 0:
                    waits.append("hz0")
                if g == NG - 1:
                    waits.append("hz15")
                pipe.add("vector", cv, waits=tuple(waits), name=f"CV{c}.{g}")

                def sl(cc):
                    pipe.add("scalar",
                             lambda E, cc=cc: E.activation(
                                 hsil[:, cc * 512:(cc + 1) * 512], cvt[cc % 8][:, :],
                                 AF.Silu, bias=cbs[:, cc:cc + 1]),
                             waits=(f"CV{cc}.{g}",)
                             + ((f"HQ{cc}.{g-1}",) if g >= 1 else ()),
                             name=f"SL{cc}.{g}")

                def mxmn(cc):
                    if cc == 0:
                        pipe.add("vector",
                                 lambda E: E.tensor_copy(rmax[:, :], hsil[:, 0:512]),
                                 waits=(f"SL0.{g}",)
                                 + ((f"TRX.{g-1}",) if g >= 1 else ()),
                                 name=f"MX0.{g}")
                        pipe.add("vector",
                                 lambda E: E.tensor_copy(rmin[:, :], hsil[:, 0:512]),
                                 waits=(f"SL0.{g}",)
                                 + ((f"TRN.{g-1}",) if g >= 1 else ()),
                                 name=f"MN0.{g}")
                    else:
                        pipe.add("vector",
                                 lambda E, cc=cc: E.tensor_max(
                                     rmax[:, :], rmax[:, :],
                                     hsil[:, cc * 512:(cc + 1) * 512]),
                                 waits=(f"SL{cc}.{g}",), name=f"MX{cc}.{g}")
                        pipe.add("vector",
                                 lambda E, cc=cc: E.tensor_tensor(
                                     rmin[:, :], rmin[:, :],
                                     hsil[:, cc * 512:(cc + 1) * 512],
                                     ALU.min),
                                 waits=(f"SL{cc}.{g}",), name=f"MN{cc}.{g}")
                if c >= 1:
                    sl(c - 1)
                if c >= 2:
                    mxmn(c - 2)
                if c == 31:
                    sl(31)
                    mxmn(30)
                    mxmn(31)

            def s2stage(g):
                def trx(E):
                    r = None
                    for b in range(4):
                        r = E.transpose(tps[:, b * 128:(b + 1) * 128],
                                        rmax[:, b * 128:(b + 1) * 128], identb[:, :])
                    return r
                pipe.add("tensor", trx, waits=(f"MX31.{g}",), name=f"TRX.{g}")
                pipe.add("vector",
                         lambda E: E.reduce_max(
                             s2st[:, 0:4],
                             tps[:, :].rearrange("p (b t) -> p b t", t=128), AX.X),
                         waits=(f"TRX.{g}",), name=f"VRX.{g}")

                def trn_(E):
                    r = None
                    for b in range(4):
                        r = E.transpose(tps[:, b * 128:(b + 1) * 128],
                                        rmin[:, b * 128:(b + 1) * 128], identb[:, :])
                    return r
                pipe.add("tensor", trn_, waits=(f"MN31.{g}", f"VRX.{g}"),
                         name=f"TRN.{g}")

                def vrn(E):
                    E.tensor_reduce(s2st[:, 4:8],
                                    tps[:, :].rearrange("p (b t) -> p b t", t=128),
                                    AX.X, ALU.min)
                    E.drain()
                    E.tensor_scalar(s2st[:, 4:8], s2st[:, 4:8], -1.0, None, ALU.mult)
                    E.drain()
                    E.tensor_tensor(s2st[:, 0:4], s2st[:, 0:4], s2st[:, 4:8], ALU.max)
                    E.drain()
                    E.tensor_scalar(s2st[:, 0:4], s2st[:, 0:4], QEPS, 1.0 / 127.0,
                                    ALU.max, ALU.mult)
                    E.drain()
                    E.reciprocal(s2st[:, 0:4], s2st[:, 0:4])
                    return E.drain()
                pipe.add("vector", vrn, waits=(f"TRN.{g}",), name=f"VRN.{g}")

                def trs(E):
                    r = None
                    for b in range(4):
                        r = E.matmul(tpf[0:1, b * 128:(b + 1) * 128],
                                     s2st[:, b:b + 1], identf[:, :],
                                     start=(b == 0), stop=(b == 3))
                    return r
                pipe.add("tensor", trs, waits=(f"VRN.{g}",), name=f"TRS.{g}")
                pipe.add("vector", lambda E: E.tensor_copy(s2row[:, :], tpf[:, :]),
                         waits=(f"TRS.{g}",), name=f"EVS.{g}")
                pipe.add("tensor",
                         lambda E: E.matmul(s2p[:, :], onesfr[0:1, :], s2row[0:1, :],
                                            start=True, stop=True),
                         waits=(f"EVS.{g}",), name=f"S2M.{g}")
                pipe.add("vector", lambda E: E.tensor_copy(s2b[:, :], s2p[:, :]),
                         waits=(f"S2M.{g}",), name=f"S2D.{g}")

            def hq_one(g, c):
                def vhq(E, c=c):
                    E.tensor_mul(tqf[c % 2][:, :],
                                 hsil[:, c * 512:(c + 1) * 512], s2b[:, :])
                    return E.tensor_scalar(hqf[:, c * 512:(c + 1) * 512],
                                           tqf[c % 2][:, :],
                                           CRND, CRND, ALU.add, ALU.subtract)
                w = [f"S2D.{g}"]
                if g >= 1:
                    w.append(f"PHB{c}.{g-1}")
                pipe.add("vector", vhq, waits=tuple(w), name=f"HQ{c}.{g}")

            def passmm(g, hf):
                tag = "AB"[hf]
                if hf == 0:
                    hq_one(g, 0)
                    hq_one(g, 1)
                for c in range(32):
                    if hf == 0 and c + 2 < 32:
                        hq_one(g, c + 2)
                    def ph(E, c=c, hf=hf):
                        r = None
                        for si in (0, 1):
                            ps = oa if si == 0 else ob
                            for oh in (0, 1):
                                r = E.matmul(ps[:, oh * 512:(oh + 1) * 512],
                                             hqf[:, c * 512 + hf * 256 + si * 128:
                                                 c * 512 + hf * 256 + (si + 1) * 128],
                                             w2T[:, c * D + oh * 512:c * D + oh * 512 + 512],
                                             start=(c == 0), stop=(c == 31))
                        return r
                    w = [f"HQ{c}.{g}"]
                    if c == 0:
                        if hf == 1:
                            w.append(f"RC.{g*4+1}")
                        elif g >= 1:
                            w.append(f"RC.{(g-1)*4+3}")
                    if g == 0 and hf == 0:
                        w.append(f"W2E{(c // 4) * 8 + 7}")
                    pipe.add("tensor", ph, waits=tuple(w), name=f"PH{tag}{c}.{g}")

            def ln4(g, hf):
                tag = "AB"[hf]
                for si in (0, 1):
                    j = g * 4 + hf * 2 + si
                    jp = j % 2
                    ps = oa if si == 0 else ob
                    pipe.add("sync",
                             lambda E, j=j, jp=jp: E.dma_start(
                                 ub[jp][:, :], u_d[j * 128:(j + 1) * 128, :]),
                             waits=(f"RC.{j-2}",) if j >= 2 else (),
                             name=f"LDU.{j}", dma=True, stream=f"u{j % 2}")
                    ln_head(pipe, "R", j, ps[:, 0:512], ps[:, 512:1024],
                            st2[jp], bnrow2,
                            waits=(f"PH{tag}31.{g}",)
                            + ((f"RC.{j-2}",) if j >= 2 else ()))

                    def rc_fn(E, jp=jp, ps=ps):
                        E.tensor_scalar(xc2[:, :], ps[:, :], st2[jp][:, 2:3],
                                        None, ALU.add)
                        return E.scalar_tensor_tensor(ost[jp][:, :], xc2[:, :],
                                                      st2[jp][:, 3:4], ub[jp][:, :],
                                                      ALU.mult, ALU.add)
                    pipe.add("vector", rc_fn,
                             waits=(f"RQ.{j}", f"LDU.{j}")
                             + ((f"STO.{j-2}",) if j >= 2 else ()),
                             name=f"RC.{j}")
                    pipe.add("sync",
                             lambda E, j=j, jp=jp: E.dma_start(
                                 out_d[j * 128:(j + 1) * 128, :], ost[jp][:, :]),
                             waits=(f"RC.{j}",), name=f"STO.{j}", dma=True,
                             stream=f"o{j % 2}")

            for c in range(32):
                conv_one(0, c)
                w2_load(2 * c)
                w2_load(2 * c + 1)
                w2_tr_ev(2 * c)
                w2_tr_ev(2 * c + 1)
            for g in range(NG):
                if g + 1 == NG - 1:
                    pipe.add("gpsimd", lambda E: E.memset(hlv[:, :, 513:514], 0.0),
                             waits=(f"CV31.{NG-2}",), name="hz15")
                s2stage(g)
                passmm(g, 0)
                if g + 1 < NG:
                    for c in range(16):
                        conv_one(g + 1, c)
                ln4(g, 0)
                passmm(g, 1)
                if g + 1 < NG:
                    for c in range(16, 32):
                        conv_one(g + 1, c)
                ln4(g, 1)
            pipe.add("sync", lambda E: None,
                     waits=(f"STO.{S//128-2}", f"STO.{S//128-1}"))
            pipe.build()

    return nc


_NC_CACHE = {}


def kernel(L0_attn_x, L0_attn_wq, L0_attn_bq, L0_attn_wk, L0_attn_bk,
           L0_attn_wv, L0_attn_bv, L0_attn_wo, L0_attn_bo,
           L0_ffn_w1, L0_ffn_conv_w, L0_ffn_conv_b, L0_ffn_w2):
    x = np.asarray(L0_attn_x, dtype=np.float32)
    B = x.shape[0]
    assert x.shape == (B, S, D)

    def f32(a, shape):
        return np.ascontiguousarray(np.asarray(a, dtype=np.float32).reshape(shape))

    cw = f32(L0_ffn_conv_w, (32, 128, 3))
    common = {
        "wq": f32(L0_attn_wq, (D, D)), "wk": f32(L0_attn_wk, (D, D)),
        "wv": f32(L0_attn_wv, (D, D)), "wo": f32(L0_attn_wo, (D, D)),
        "bq": f32(L0_attn_bq, (1, D)), "bk": f32(L0_attn_bk, (1, D)),
        "bv": f32(L0_attn_bv, (1, D)), "bo": f32(L0_attn_bo, (1, D)),
        "w1": f32(L0_ffn_w1, (DF, D)), "w2": f32(L0_ffn_w2, (D, DF)),
        "cw0": np.ascontiguousarray(cw[:, :, 0].T),
        "cw1": np.ascontiguousarray(cw[:, :, 1].T),
        "cw2": np.ascontiguousarray(cw[:, :, 2].T),
        "cb": np.ascontiguousarray(f32(L0_ffn_conv_b, (32, 128)).T),
        "identb": np.eye(128, dtype=bfd),
        "identf": np.eye(128, dtype=np.float32),
        "onesb": np.ones((1, 128), dtype=bfd),
        "onesfr": np.ones((1, 128), dtype=np.float32),
        "onesfc": np.ones((128, 1), dtype=np.float32),
    }
    if "prog" not in _NC_CACHE:
        _NC_CACHE["prog"] = build_program()
    nc = _NC_CACHE["prog"]
    in_maps = [dict(common, x=np.ascontiguousarray(x[n])) for n in range(B)]
    res = run_bass_kernel_spmd(nc, in_maps, core_ids=list(range(B)))
    return np.stack([np.asarray(r["out"]) for r in res.results], axis=0)


# version 20
# speedup vs baseline: 1.0596x; 1.0125x over previous
import numpy as np
import ml_dtypes
from contextlib import ExitStack

import concourse.bass as bass
import concourse.mybir as mybir
from concourse.bass_utils import run_bass_kernel_spmd

F32 = mybir.dt.float32
BF16 = mybir.dt.bfloat16
AF = mybir.ActivationFunctionType
ALU = mybir.AluOpType
AX = mybir.AxisListType
bfd = ml_dtypes.bfloat16

S, D, H, HD, DF = 8192, 1024, 16, 64, 4096
NT = S // 128
NG = S // 512
CRND = 12582912.0
EPS_LN = 1e-5
EPS_Z = 1e-6
QEPS = 1e-5
NW1 = float(DF * D)

ENGINES = ("sync", "vector", "scalar", "tensor", "gpsimd")
_SU_MODE = 3


class _Op:
    __slots__ = ("eng", "fn", "waits", "name", "dma", "stream", "inc", "val", "gidx")


class Pipe:

    def __init__(self, nc, ctx, tag):
        self.nc = nc
        self.ctx = ctx
        self.tag = tag
        self.ops = []
        self.byname = {}
        self.sems = {}

    def _sem(self, eng, stream):
        key = (eng, stream)
        if key not in self.sems:
            name = f"{self.tag}_{eng}_{stream}" if stream else f"{self.tag}_{eng}"
            self.sems[key] = self.ctx.enter_context(self.nc.semaphore(name))
        return self.sems[key]

    def add(self, eng, fn, waits=(), name=None, dma=False, stream=None):
        op = _Op()
        op.eng, op.fn, op.name, op.dma = eng, fn, name, dma
        op.stream = (stream or "d") if dma else None
        op.waits = tuple(w for w in waits if w is not None)
        op.inc = False
        op.val = None
        op.gidx = len(self.ops)
        self.ops.append(op)
        if name is not None:
            assert name not in self.byname, name
            self.byname[name] = op
        return op

    def build(self):
        for op in self.ops:
            if op.dma:
                op.inc = True
            for w in op.waits:
                dep = self.byname[w]
                assert dep.gidx < op.gidx, (op.name, w)
                dep.inc = True
        counts = {}
        for op in self.ops:
            if op.inc:
                k = (op.eng, op.stream)
                counts[k] = counts.get(k, 0) + (16 if op.dma else 1)
                op.val = counts[k]
                self._sem(op.eng, op.stream)
        per_eng = {}
        for op in self.ops:
            per_eng.setdefault(op.eng, []).append(op)
        with self.nc.Block() as block:
            for eng in ENGINES:
                if eng not in per_eng:
                    continue
                ops = per_eng[eng]

                def mk(ops=ops):
                    def emit(E):
                        for op in ops:
                            for w in op.waits:
                                dep = self.byname[w]
                                E.wait_ge(self._sem(dep.eng, dep.stream), dep.val)
                            instr = op.fn(E)
                            if op.inc:
                                assert instr is not None, op.name
                                instr.then_inc(self._sem(op.eng, op.stream),
                                               16 if op.dma else 1)
                    return emit

                getattr(block, eng)(mk())


def build_program(dump_internal=False, stop_after=None):
    nc = bass.Bass("TRN2", target_bir_lowering=False, detect_race_conditions=False)

    x_d = nc.dram_tensor("x", [S, D], F32, kind="ExternalInput").ap()
    wq_d = nc.dram_tensor("wq", [D, D], F32, kind="ExternalInput").ap()
    wk_d = nc.dram_tensor("wk", [D, D], F32, kind="ExternalInput").ap()
    wv_d = nc.dram_tensor("wv", [D, D], F32, kind="ExternalInput").ap()
    wo_d = nc.dram_tensor("wo", [D, D], F32, kind="ExternalInput").ap()
    bq_d = nc.dram_tensor("bq", [1, D], F32, kind="ExternalInput").ap()
    bk_d = nc.dram_tensor("bk", [1, D], F32, kind="ExternalInput").ap()
    bv_d = nc.dram_tensor("bv", [1, D], F32, kind="ExternalInput").ap()
    bo_d = nc.dram_tensor("bo", [1, D], F32, kind="ExternalInput").ap()
    w1_d = nc.dram_tensor("w1", [DF, D], F32, kind="ExternalInput").ap()
    w2_d = nc.dram_tensor("w2", [D, DF], F32, kind="ExternalInput").ap()
    cw0_d = nc.dram_tensor("cw0", [128, 32], F32, kind="ExternalInput").ap()
    cw1_d = nc.dram_tensor("cw1", [128, 32], F32, kind="ExternalInput").ap()
    cw2_d = nc.dram_tensor("cw2", [128, 32], F32, kind="ExternalInput").ap()
    cb_d = nc.dram_tensor("cb", [128, 32], F32, kind="ExternalInput").ap()
    idb_d = nc.dram_tensor("identb", [128, 128], BF16, kind="ExternalInput").ap()
    idf_d = nc.dram_tensor("identf", [128, 128], F32, kind="ExternalInput").ap()
    ob_d = nc.dram_tensor("onesb", [1, 128], BF16, kind="ExternalInput").ap()
    ofr_d = nc.dram_tensor("onesfr", [1, 128], F32, kind="ExternalInput").ap()
    ofc_d = nc.dram_tensor("onesfc", [128, 1], F32, kind="ExternalInput").ap()

    out_d = nc.dram_tensor("out", [S, D], F32, kind="ExternalOutput").ap()
    ikind = "ExternalOutput" if dump_internal else "Internal"
    u_d = nc.dram_tensor("u_scr", [S, D], F32, kind=ikind).ap()
    h_d = nc.dram_tensor("h_scr", [DF, S], BF16, kind=ikind).ap()
    w1q_d = nc.dram_tensor("w1q_scr", [DF, D], BF16, kind=ikind).ap()
    w2q_d = nc.dram_tensor("w2q_scr", [D, DF], BF16, kind=ikind).ap()

    hview = h_d.rearrange("(c p) t -> p c t", p=128)
    w1qv = w1q_d.rearrange("(c p) j -> p c j", p=128)

    ctx = ExitStack()
    with ctx:
        def sb(name, shape, dt):
            return ctx.enter_context(nc.sbuf_tensor(name, shape, dt))

        identb = sb("identb_s", [128, 128], BF16)
        identf = sb("identf_s", [128, 128], F32)
        onesb = sb("onesb_s", [1, 128], BF16)
        onesfr = sb("onesfr_s", [1, 128], F32)
        onesfc = sb("onesfc_s", [128, 1], F32)
        bqb = sb("bqb", [1, D], BF16)
        bkb = sb("bkb", [1, D], BF16)
        bvb = sb("bvb", [1, D], BF16)
        bob = sb("bob", [1, D], BF16)
        cw0s = sb("cw0s", [128, 32], F32)
        cw1s = sb("cw1s", [128, 32], F32)
        cw2s = sb("cw2s", [128, 32], F32)
        cbs = sb("cbs", [128, 32], F32)
        wscal = sb("wscal", [128, 4], F32)
        epsln = sb("epsln", [128, 1], F32)

        ctxA = ctx.enter_context(ExitStack())
        w1qT = ctxA.enter_context(nc.sbuf_tensor("w1qT", [128, 8 * DF], BF16))
        kvblk = ctxA.enter_context(nc.sbuf_tensor("kvblk", [128, 8 * 128], BF16))
        ksumb = ctxA.enter_context(nc.sbuf_tensor("ksumb", [128, D], F32))

        def ln_head(pipe, pfx, j, src0, src1, st, bnrow, waits):
            def a_fn(E, src0=src0, src1=src1, st=st, bnrow=bnrow):
                E.bn_stats(bnrow[:, 0:6], src0)
                E.bn_stats(bnrow[:, 6:12], src1)
                E.drain()
                E.bn_aggr(st[:, 0:2], bnrow[:, :])
                E.drain()
                E.tensor_scalar(st[:, 2:3], st[:, 0:1], -1.0, None, ALU.mult)
                return E.drain()
            pipe.add("vector", a_fn, waits=waits, name=f"{pfx}A.{j}")

            def q_fn(E, st=st):
                E.activation(st[:, 4:5], st[:, 1:2], AF.Ln, bias=epsln[:, :])
                E.drain()
                return E.activation(st[:, 3:4], st[:, 4:5], AF.Exp, scale=-0.5)
            pipe.add("scalar", q_fn, waits=(f"{pfx}A.{j}",), name=f"{pfx}Q.{j}")

        with ExitStack() as cs:
            pipe = Pipe(nc, ctx, "su")
            wst = [cs.enter_context(nc.sbuf_tensor(f"wst{i}", [128, 2048], F32))
                   for i in range(2)]
            wtf = [cs.enter_context(nc.sbuf_tensor(f"wtf{i}", [128, 2048], F32))
                   for i in range(2)]
            wqst = [cs.enter_context(nc.sbuf_tensor(f"wqst{i}", [128, 2048], BF16))
                    for i in range(2)]
            wsums = cs.enter_context(nc.sbuf_tensor("wsums", [128, 32], F32))
            vv1 = cs.enter_context(nc.sbuf_tensor("vv1", [128, 2], F32))
            scrow = cs.enter_context(nc.sbuf_tensor("scrow", [1, 4], F32))
            w1p = [cs.enter_context(nc.sbuf_tensor(f"w1p{i}", [128, 512], BF16))
                   for i in range(4)]
            psmB = cs.enter_context(nc.psum_tensor("psmB", [1, 32], F32))
            psmC = cs.enter_context(nc.psum_tensor("psmC", [128, 4], F32))
            sutp = cs.enter_context(nc.psum_tensor("sutp", [128, 512], BF16))

            pipe.add("gpsimd", lambda E: E.memset(epsln[:, :], EPS_LN), name="eps")
            for nm, dst, srch in (("ldidb", identb, idb_d), ("ldidf", identf, idf_d),
                                  ("ldob", onesb, ob_d), ("ldofr", onesfr, ofr_d),
                                  ("ldofc", onesfc, ofc_d), ("ldcw0", cw0s, cw0_d),
                                  ("ldcw1", cw1s, cw1_d), ("ldcw2", cw2s, cw2_d),
                                  ("ldcb", cbs, cb_d)):
                pipe.add("sync", lambda E, d=dst, s=srch: E.dma_start(d[:, :], s[:, :]),
                         name=nm, dma=True, stream="cl")
            for nm, dst, srch in (("ldbq", bqb, bq_d), ("ldbk", bkb, bk_d),
                                  ("ldbv", bvb, bv_d), ("ldbo", bob, bo_d)):
                pipe.add("gpsimd", lambda E, d=dst, s=srch: E.dma_start(d[:, :], s[:, :]),
                         name=nm, dma=True, stream="bl")

            def w_piece_ap(widx, k):
                if widx == 0:
                    return w1_d.rearrange("(c p) j -> p c j", p=128)[:, 2 * k:2 * k + 2, :]
                c, hf = k // 2, k % 2
                return w2_d[c * 128:(c + 1) * 128, hf * 2048:(hf + 1) * 2048]

            def wq_piece_ap(widx, k):
                if widx == 0:
                    return w1qv[:, 2 * k:2 * k + 2, :]
                c, hf = k // 2, k % 2
                return w2q_d[c * 128:(c + 1) * 128, hf * 2048:(hf + 1) * 2048]

            for widx in (0, 1):
                for k in range(16):
                    kk = widx * 16 + k
                    pipe.add("sync",
                             lambda E, w=widx, k=k, kk=kk:
                             E.dma_start(wst[kk % 2][:, :].rearrange(
                                 "p (c j) -> p c j", j=1024) if w == 0
                                 else wst[kk % 2][:, :], w_piece_ap(w, k)),
                             waits=(f"ar{kk-2}",) if kk >= 2 else (),
                             name=f"al{kk}", dma=True, stream=f"a{kk % 2}")
                    pipe.add("vector",
                             lambda E, kk=kk:
                             E.reduce_sum(wsums[:, kk:kk + 1], wst[kk % 2][:, :],
                                          AX.X, apply_absolute_value=True),
                             waits=(f"al{kk}",), name=f"ar{kk}")

            def vra(E):
                E.drain()
                E.reduce_sum(vv1[0:128, 0:1], wsums[:, 0:16], AX.X)
                return E.reduce_sum(vv1[0:128, 1:2], wsums[:, 16:32], AX.X)
            pipe.add("vector", vra, waits=("ar31", "ldcb"), name="vrA")

            def tra(E):
                E.matmul(psmB[0:1, 0:1], vv1[0:128, 0:1], onesfc[:, 0:1],
                         start=True, stop=False)
                return E.matmul(psmB[0:1, 1:2], vv1[0:128, 1:2], onesfc[:, 0:1],
                                start=False, stop=True)
            pipe.add("tensor", tra, waits=("vrA",), name="trB")

            def mchain(E):
                E.tensor_scalar(scrow[0:1, 0:1], psmB[0:1, 0:1], 1.0 / NW1, QEPS,
                                ALU.mult, ALU.max)
                E.tensor_scalar(scrow[0:1, 2:3], psmB[0:1, 1:2], 1.0 / NW1, QEPS,
                                ALU.mult, ALU.max)
                E.drain()
                E.reciprocal(scrow[0:1, 1:2], scrow[0:1, 0:1])
                E.drain()
                return E.reciprocal(scrow[0:1, 3:4], scrow[0:1, 2:3])
            pipe.add("vector", mchain, waits=("trB",), name="mch")

            def bcmm(E):
                r = None
                for j in range(4):
                    r = E.matmul(psmC[:, j:j + 1], onesfr[0:1, :], scrow[0:1, j:j + 1],
                                 start=(j == 0), stop=(j == 3))
                return r
            pipe.add("tensor", bcmm, waits=("mch", "ldcb"), name="bcm")

            def evw(E):
                E.tensor_copy(wscal[:, :], psmC[:, :])
                return E.drain()
            pipe.add("vector", evw, waits=("bcm",), name="evw")

            for widx in (0, 1):
                swcol = wscal[:, 1:2] if widx == 0 else wscal[:, 3:4]
                for k in range(16):
                    kk = widx * 16 + k
                    pipe.add("gpsimd",
                             lambda E, w=widx, k=k, kk=kk:
                             E.dma_start(wst[kk % 2][:, :].rearrange(
                                 "p (c j) -> p c j", j=1024) if w == 0
                                 else wst[kk % 2][:, :], w_piece_ap(w, k)),
                             waits=(f"tq{kk-2}",) if kk >= 2 else ("ar31",),
                             name=f"tl{kk}", dma=True, stream=f"t{kk % 2}")

                    def trn(E, kk=kk, sc=swcol):
                        E.tensor_scalar(wtf[kk % 2][:, :], wst[kk % 2][:, :], sc, CRND,
                                        ALU.mult, ALU.add)
                        E.tensor_scalar(wtf[kk % 2][:, :], wtf[kk % 2][:, :], CRND, 1.0,
                                        ALU.subtract, ALU.min)
                        return E.tensor_scalar(wqst[kk % 2][:, :], wtf[kk % 2][:, :],
                                               -1.0, None, ALU.max)
                    pipe.add("vector", trn,
                             waits=(f"tl{kk}", "evw") + ((f"ts{kk-2}",) if kk >= 2 else ()),
                             name=f"tq{kk}")
                    pipe.add("sync",
                             lambda E, w=widx, k=k, kk=kk:
                             E.dma_start(wq_piece_ap(w, k),
                                         wqst[kk % 2][:, :].rearrange(
                                             "p (c j) -> p c j", j=1024) if w == 0
                                         else wqst[kk % 2][:, :]),
                             waits=(f"tq{kk}",), name=f"ts{kk}", dma=True,
                             stream=f"s{kk % 2}")

            for kb in range(32):
                for cfh in range(2):
                    i = kb * 2 + cfh
                    cf0 = cfh * 4
                    pipe.add("scalar",
                             lambda E, kb=kb, cf0=cf0, i=i: E.dma_start(
                                 w1p[i % 4][:, :],
                                 w1q_d[kb * 128:(kb + 1) * 128,
                                       cf0 * 128:(cf0 + 4) * 128]),
                             waits=((f"ts{kb // 2}",) if _SU_MODE >= 3 else ())
                             + ((f"wpt{i-4}",) if (i >= 4 and _SU_MODE >= 2) else ()),
                             name=f"wpl{i}", dma=True, stream=f"wp{i % 4}")
                    if _SU_MODE < 2:
                        continue

                    def wpt_fn(E, i=i):
                        r = None
                        for k in range(4):
                            r = E.transpose(
                                sutp[:, k * 128:(k + 1) * 128],
                                w1p[i % 4][:, k * 128:(k + 1) * 128], identb[:, :])
                        return r
                    pipe.add("tensor", wpt_fn,
                             waits=(f"wpl{i}",)
                             + ((f"we{i-1}",) if i >= 1 else ("ldidb",)),
                             name=f"wpt{i}")

                    def we_fn(E, kb=kb, cf0=cf0, i=i):
                        r = None
                        for k in range(4):
                            r = E.activation(
                                w1qT[:, (cf0 + k) * DF + kb * 128:
                                     (cf0 + k) * DF + (kb + 1) * 128],
                                sutp[:, k * 128:(k + 1) * 128], AF.Copy)
                        return r
                    pipe.add("scalar", we_fn, waits=(f"wpt{i}",), name=f"we{i}")
            pipe.add("scalar", lambda E: None,
                     waits=(("we63",) if _SU_MODE >= 2 else ()) + ("ts31", "ts30"))
            pipe.add("gpsimd", lambda E: None, waits=("ldbo",))
            pipe.add("sync", lambda E: None, waits=("ldcb",))
            pipe.build()
            su_ts30 = (pipe.sems[("sync", "s0")], pipe.byname["ts30"].val)
            su_ts31 = (pipe.sems[("sync", "s1")], pipe.byname["ts31"].val)
            su_we = ((pipe.sems[("scalar", None)], pipe.byname["we63"].val)
                     if _SU_MODE >= 2 else (None, 0))
        if stop_after == "su":
            return nc

        with ExitStack() as cs:
            pipe = Pipe(nc, ctx, "p1")
            wk_s = cs.enter_context(nc.sbuf_tensor("wk_s", [128, 8 * D], BF16))
            wv_s = cs.enter_context(nc.sbuf_tensor("wv_s", [128, 8 * D], BF16))
            xb = [cs.enter_context(nc.sbuf_tensor(f"xb{i}", [128, D], F32)) for i in range(2)]
            st = [cs.enter_context(nc.sbuf_tensor(f"st{i}", [128, 10], F32)) for i in range(2)]
            bnrow = cs.enter_context(nc.sbuf_tensor("bnrow", [128, 12], F32))
            xc = [cs.enter_context(nc.sbuf_tensor(f"p1xc{i}", [128, D], F32)) for i in range(2)]
            lnx = [cs.enter_context(nc.sbuf_tensor(f"lnx{i}", [128, D], BF16)) for i in range(2)]
            lnxT = [cs.enter_context(nc.sbuf_tensor(f"lnxT{i}", [128, D], BF16)) for i in range(2)]
            t1 = [cs.enter_context(nc.sbuf_tensor(f"t1{i}", [128, D], F32)) for i in range(2)]
            km = [cs.enter_context(nc.sbuf_tensor(f"km{i}", [128, D], BF16)) for i in range(2)]
            vs = [cs.enter_context(nc.sbuf_tensor(f"vs{i}", [128, D], BF16)) for i in range(2)]
            ksr = cs.enter_context(nc.sbuf_tensor("ksr", [128, D], F32))
            kv16ctx = ExitStack()
            kv16 = kv16ctx.enter_context(nc.psum_tensor("kv16", [64, 1024], F32))
            p1ps = ExitStack()
            tp = [p1ps.enter_context(nc.psum_tensor(f"tp{i}", [128, D], BF16)) for i in range(2)]
            pk = p1ps.enter_context(nc.psum_tensor("pk", [128, D], F32))
            pv = p1ps.enter_context(nc.psum_tensor("pv", [128, D], F32))

            pipe.add("gpsimd", lambda E: E.dma_start(
                wk_s[:, :].rearrange("p (c j) -> p c j", j=D),
                wk_d.rearrange("(c p) j -> p c j", p=128)), name="ldwk", dma=True,
                stream="wk")
            pipe.add("gpsimd", lambda E: E.dma_start(
                wv_s[:, :].rearrange("p (c j) -> p c j", j=D),
                wv_d.rearrange("(c p) j -> p c j", p=128)), name="ldwv", dma=True,
                stream="wv")
            pipe.add("gpsimd", lambda E: E.memset(ksr[:, :], 0.0), name="ksz")

            def proj(E, lT, w_s, brow, ps):
                r = None
                for hf in (0, 1):
                    for c in range(8):
                        E.matmul(ps[:, hf * 512:(hf + 1) * 512],
                                 lT[:, c * 128:(c + 1) * 128],
                                 w_s[:, c * D + hf * 512:c * D + hf * 512 + 512],
                                 start=(c == 0), stop=False)
                    r = E.matmul(ps[:, hf * 512:(hf + 1) * 512], onesb[0:1, :],
                                 brow[0:1, hf * 512:(hf + 1) * 512],
                                 start=False, stop=True)
                return r

            def h1_vs(i):
                ip = i % 2
                pipe.add("sync",
                         lambda E, i=i, ip=ip: E.dma_start(
                             xb[ip][:, :], x_d[i * 128:(i + 1) * 128, :]),
                         waits=(f"C.{i-2}",) if i >= 2 else (),
                         name=f"LD.{i}", dma=True, stream=f"x{i % 2}")
                ln_head(pipe, "", i, xb[ip][:, 0:512], xb[ip][:, 512:1024],
                        st[ip], bnrow,
                        waits=(f"LD.{i}",) + ((f"P.{i-2}",) if i >= 2 else ()))

                def c_fn(E, ip=ip):
                    return E.tensor_scalar(xc[ip][:, :], xb[ip][:, :], st[ip][:, 2:3],
                                           None, ALU.add)
                pipe.add("vector", c_fn,
                         waits=(f"Q.{i}",) + ((f"P.{i-2}",) if i >= 2 else ()),
                         name=f"C.{i}")
                pipe.add("scalar",
                         lambda E, ip=ip: E.activation(lnx[ip][:, :], xc[ip][:, :],
                                                       AF.Identity,
                                                       scale=st[ip][:, 3:4]),
                         waits=(f"C.{i}",) + ((f"P1.{i-2}",) if i >= 2 else ()),
                         name=f"P.{i}")

            def h1_pe(i):
                ip = i % 2

                def p1_fn(E, ip=ip):
                    r = None
                    for c in range(8):
                        r = E.transpose(tp[ip][:, c * 128:(c + 1) * 128],
                                        lnx[ip][:, c * 128:(c + 1) * 128], identb[:, :])
                    return r
                pipe.add("tensor", p1_fn,
                         waits=(f"P.{i}",) + ((f"V3.{i-2}",) if i >= 2 else ()),
                         name=f"P1.{i}")

            def h2(i):
                ip = i % 2
                pipe.add("vector",
                         lambda E, ip=ip: E.tensor_copy(lnxT[ip][:, :], tp[ip][:, :]),
                         waits=(f"P1.{i}",) + ((f"P3.{i-2}",) if i >= 2 else ()),
                         name=f"V3.{i}")
                if i + 2 < NT:
                    h1_vs(i + 2)
                pipe.add("tensor", lambda E, ip=ip: proj(E, lnxT[ip], wk_s, bkb, pk),
                         waits=(f"V3.{i}",)
                         + ((f"V5.{i-1}",) if i >= 1 else ("ldwk",)),
                         name=f"P2.{i}")
                pipe.add("tensor", lambda E, ip=ip: proj(E, lnxT[ip], wv_s, bvb, pv),
                         waits=((f"S6.{i-1}",) if i >= 1 else ("ldwv",)),
                         name=f"P3.{i}")
                pipe.add("vector",
                         lambda E, ip=ip: E.tensor_scalar(t1[ip][:, :], pk[:, :], 0.0,
                                                          None, ALU.min),
                         waits=(f"P2.{i}",), name=f"V4.{i}")
                pipe.add("scalar",
                         lambda E, ip=ip: E.activation(t1[ip][:, :], t1[ip][:, :], AF.Exp),
                         waits=(f"V4.{i}",), name=f"S4.{i}")
                pipe.add("vector",
                         lambda E, ip=ip: E.scalar_tensor_tensor(
                             km[ip][:, :], pk[:, :], 0.0, t1[ip][:, :],
                             ALU.max, ALU.add),
                         waits=(f"S4.{i}",)
                         + ((f"P4.{i-2}",) if i >= 2 else ()),
                         name=f"V5.{i}")
                pipe.add("scalar",
                         lambda E, ip=ip: E.activation(vs[ip][:, :], pv[:, :], AF.Copy),
                         waits=(f"P3.{i}",) + ((f"P4.{i-2}",) if i >= 2 else ()),
                         name=f"S6.{i}")
                pipe.add("vector",
                         lambda E, ip=ip: E.tensor_add(ksr[:, :], ksr[:, :], km[ip][:, :]),
                         waits=(("ksz",) if i == 0 else ()), name=f"V6.{i}")

                def p4(E, i=i, ip=ip):
                    r = None
                    for h in range(16):
                        r = E.matmul(kv16[:, h * 64:(h + 1) * 64],
                                     km[ip][:, h * 64:(h + 1) * 64],
                                     vs[ip][:, h * 64:(h + 1) * 64],
                                     start=(i == 0 and h % 8 == 0),
                                     stop=(i == NT - 1 and h % 8 == 7))
                    return r
                pipe.add("tensor", p4, waits=(f"V5.{i}", f"S6.{i}"), name=f"P4.{i}")
                if i + 2 < NT:
                    h1_pe(i + 2)

            h1_vs(0)
            h1_pe(0)
            h1_vs(1)
            h1_pe(1)
            for i in range(NT):
                h2(i)

            pipe.build()
            p1ps.close()

            with ExitStack() as ce:
                pipe = Pipe(nc, ctx, "p1e")
                kvsb = ce.enter_context(nc.sbuf_tensor("kvsb", [64, 1024], BF16))
                ksrow = ce.enter_context(nc.sbuf_tensor("ksrow", [1, D], F32))
                pks = ce.enter_context(nc.psum_tensor("pks", [1, D], F32))
                pkb = ce.enter_context(nc.psum_tensor("pkb", [128, D], F32))
                pipe.add("vector", lambda E: E.tensor_copy(kvsb[:, :], kv16[:, :]),
                         name="evkv")
                pipe.add("gpsimd", lambda E: E.memset(kvblk[:, :], 0.0), name="kvz")
                for h in range(16):
                    p, odd = h // 2, h % 2
                    pipe.add("gpsimd",
                             lambda E, h=h, p=p, odd=odd: E.dma_start(
                                 kvblk[odd * 64:odd * 64 + 64,
                                       p * 128 + odd * 64:p * 128 + odd * 64 + 64],
                                 kvsb[0:64, h * 64:(h + 1) * 64]),
                             waits=("evkv", "kvz"), name=f"kvb{h}", dma=True, stream="kv")

                def cs_mm(E):
                    r = None
                    for hf in (0, 1):
                        r = E.matmul(pks[0:1, hf * 512:(hf + 1) * 512], onesfc[:, 0:1],
                                     ksr[:, hf * 512:(hf + 1) * 512], start=True, stop=True)
                    return r
                pipe.add("tensor", cs_mm, name="csm")
                pipe.add("vector", lambda E: E.tensor_copy(ksrow[:, :], pks[:, :]),
                         waits=("csm",), name="evks")

                def kb_mm(E):
                    r = None
                    for hf in (0, 1):
                        r = E.matmul(pkb[:, hf * 512:(hf + 1) * 512], onesfr[0:1, :],
                                     ksrow[0:1, hf * 512:(hf + 1) * 512],
                                     start=True, stop=True)
                    return r
                pipe.add("tensor", kb_mm, waits=("evks",), name="kbm")
                pipe.add("vector", lambda E: E.tensor_copy(ksumb[:, :], pkb[:, :]),
                         waits=("kbm",), name="evkb")
                pipe.add("gpsimd", lambda E: None, waits=("kvb15",))
                pipe.build()
            kv16ctx.close()
        if stop_after == "p1":
            return nc

        with ExitStack() as cs:
            pipe = Pipe(nc, ctx, "a")
            wq_s = cs.enter_context(nc.sbuf_tensor("wq_s", [128, 8 * D], BF16))
            wo_s = cs.enter_context(nc.sbuf_tensor("wo_s", [128, 8 * D], BF16))
            xb = [cs.enter_context(nc.sbuf_tensor(f"axb{i}", [128, D], F32)) for i in range(6)]
            st = [cs.enter_context(nc.sbuf_tensor(f"ast{i}", [128, 10], F32)) for i in range(2)]
            st2 = [cs.enter_context(nc.sbuf_tensor(f"ast2{i}", [128, 10], F32)) for i in range(2)]
            bnrow = cs.enter_context(nc.sbuf_tensor("abnrow", [128, 12], F32))
            bnrow2 = [cs.enter_context(nc.sbuf_tensor(f"abnrow2{i}", [128, 12], F32))
                      for i in range(2)]
            xc = [cs.enter_context(nc.sbuf_tensor(f"axc{i}", [128, D], F32)) for i in range(2)]
            xc2 = cs.enter_context(nc.sbuf_tensor("axc2", [128, D], F32))
            lnx = [cs.enter_context(nc.sbuf_tensor(f"alnx{i}", [128, D], BF16)) for i in range(2)]
            lnxT = cs.enter_context(nc.sbuf_tensor("alnxT", [128, D], BF16))
            t1 = cs.enter_context(nc.sbuf_tensor("at1", [128, D], F32))
            qm = cs.enter_context(nc.sbuf_tensor("aqm", [128, D], F32))
            zden = cs.enter_context(nc.sbuf_tensor("azden", [128, 16], F32))
            qms = cs.enter_context(nc.sbuf_tensor("aqms", [128, D], BF16))
            qmsT = cs.enter_context(nc.sbuf_tensor("aqmsT", [128, 8 * 512], BF16))
            yT = cs.enter_context(nc.sbuf_tensor("ayT", [128, 8 * 512], BF16))
            ust = [cs.enter_context(nc.sbuf_tensor(f"aust{i}", [128, D], F32)) for i in range(2)]
            tp = [cs.enter_context(nc.psum_tensor(f"atp{i}", [128, D], BF16)) for i in range(2)]
            qp = cs.enter_context(nc.psum_tensor("aqp", [128, D], F32))
            yp = [cs.enter_context(nc.psum_tensor(f"ayp{i}", [128, 512], F32)) for i in range(2)]
            ao = cs.enter_context(nc.psum_tensor("aao", [128, D], F32))

            pipe.add("gpsimd", lambda E: E.dma_start(
                wq_s[:, :].rearrange("p (c j) -> p c j", j=D),
                wq_d.rearrange("(c p) j -> p c j", p=128)), name="ldwq", dma=True,
                stream="wq")
            pipe.add("gpsimd", lambda E: E.dma_start(
                wo_s[:, :].rearrange("p (c j) -> p c j", j=D),
                wo_d.rearrange("(c p) j -> p c j", p=128)), name="ldwo", dma=True,
                stream="wo")

            qmsTv = qmsT[:, :].rearrange("p (c t) -> p c t", t=512)

            aob = [cs.enter_context(nc.sbuf_tensor(f"aaob{i}", [128, D], F32))
                   for i in range(2)]
            qb = [cs.enter_context(nc.sbuf_tensor(f"aqb{i}", [128, D], F32))
                  for i in range(2)]

            def h1a_vs(j):
                jp, j4 = j % 2, j % 6
                pipe.add("sync",
                         lambda E, j=j, j4=j4: E.dma_start(
                             xb[j4][:, :], x_d[j * 128:(j + 1) * 128, :]),
                         waits=(f"RC.{j-6}",) if j >= 6 else (),
                         name=f"LDX.{j}", dma=True, stream=f"x{j % 6}")
                ln_head(pipe, "", j, xb[j4][:, 0:512], xb[j4][:, 512:1024],
                        st[jp], bnrow,
                        waits=(f"LDX.{j}",) + ((f"P.{j-2}",) if j >= 2 else ()))

                def c_fn(E, jp=jp, j4=j4):
                    return E.tensor_scalar(xc[jp][:, :], xb[j4][:, :], st[jp][:, 2:3],
                                           None, ALU.add)
                pipe.add("vector", c_fn,
                         waits=(f"Q.{j}",) + ((f"P.{j-2}",) if j >= 2 else ()),
                         name=f"C.{j}")
                pipe.add("scalar",
                         lambda E, jp=jp: E.activation(lnx[jp][:, :], xc[jp][:, :],
                                                       AF.Identity,
                                                       scale=st[jp][:, 3:4]),
                         waits=(f"C.{j}",) + ((f"P1.{j-2}",) if j >= 2 else ()),
                         name=f"P.{j}")

            def h1a_pe(j):
                jp = j % 2

                def p1_fn(E, jp=jp):
                    r = None
                    for c in range(8):
                        r = E.transpose(tp[jp][:, c * 128:(c + 1) * 128],
                                        lnx[jp][:, c * 128:(c + 1) * 128],
                                        identb[:, :])
                    return r
                pipe.add("tensor", p1_fn,
                         waits=(f"P.{j}",) + ((f"V8.{j-2}",) if j >= 2 else ()),
                         name=f"P1.{j}")

            def h2a(j):
                g, s_ = j // 4, j % 4
                jp = j % 2
                pipe.add("vector",
                         lambda E, jp=jp: E.tensor_copy(lnxT[:, :], tp[jp][:, :]),
                         waits=(f"P1.{j}",) + ((f"P2.{j-1}",) if j >= 1 else ()),
                         name=f"V3.{j}")
                if j + 2 < S // 128:
                    h1a_vs(j + 2)
                pipe.add("tensor", lambda E: proj(E, lnxT, wq_s, bqb, qp),
                         waits=(f"V3.{j}",)
                         + ((f"SCQ.{j-1}",) if j >= 1 else ("ldwq",)),
                         name=f"P2.{j}")
                pipe.add("scalar",
                         lambda E, jp=jp: E.activation(qb[jp][:, :], qp[:, :], AF.Copy),
                         waits=(f"P2.{j}",) + ((f"V5.{j-2}",) if j >= 2 else ()),
                         name=f"SCQ.{j}")
                pipe.add("vector",
                         lambda E, jp=jp: E.tensor_scalar(t1[:, :], qb[jp][:, :], 0.0,
                                                          None, ALU.min),
                         waits=(f"SCQ.{j}",), name=f"V4.{j}")
                pipe.add("scalar",
                         lambda E: E.activation(t1[:, :], t1[:, :], AF.Exp),
                         waits=(f"V4.{j}",), name=f"S4.{j}")
                pipe.add("vector",
                         lambda E, jp=jp: E.scalar_tensor_tensor(
                             qm[:, :], qb[jp][:, :], 0.0, t1[:, :],
                             ALU.max, ALU.add),
                         waits=(f"S4.{j}",), name=f"V5.{j}")

                def v6(E):
                    E.tensor_mul(t1[:, :], qm[:, :], ksumb[:, :])
                    E.tensor_reduce(zden[:, :],
                                    t1[:, :].rearrange("p (h d) -> p h d", d=64),
                                    AX.X, ALU.add)
                    E.drain()
                    E.tensor_scalar(zden[:, :], zden[:, :], EPS_Z, None,
                                    ALU.add)
                    E.drain()
                    E.reciprocal(zden[:, :], zden[:, :])
                    return E.drain()
                pipe.add("vector", v6, name=f"V6.{j}")

                def v7(E):
                    r = None
                    for h in range(16):
                        r = E.tensor_scalar(qms[:, h * 64:(h + 1) * 64],
                                            qm[:, h * 64:(h + 1) * 64],
                                            zden[:, h:h + 1], None, ALU.mult)
                    return r
                pipe.add("vector", v7,
                         waits=((f"P3.{j-1}",) if j >= 1 else ()), name=f"V7.{j}")

                def p3(E, jp=jp):
                    r = None
                    for c in range(8):
                        r = E.transpose(tp[jp][:, c * 128:(c + 1) * 128],
                                        qms[:, c * 128:(c + 1) * 128],
                                        identb[:, :])
                    return r
                pipe.add("tensor", p3, waits=(f"V7.{j}", f"V3.{j}"), name=f"P3.{j}")
                pipe.add("vector",
                         lambda E, jp=jp, s_=s_: E.tensor_copy(
                             qmsTv[:, :, s_ * 128:(s_ + 1) * 128],
                             tp[jp][:, :].rearrange("p (c t) -> p c t", t=128)),
                         waits=(f"P3.{j}",)
                         + ((f"PY7.{g-1}",) if (s_ == 0 and g >= 1) else ()),
                         name=f"V8.{j}")
                if j + 2 < S // 128:
                    h1a_pe(j + 2)


            def ystage(g):
                for p in range(8):
                    pipe.add("tensor",
                             lambda E, p=p: E.matmul(
                                 yp[p % 2][:, :], kvblk[:, p * 128:(p + 1) * 128],
                                 qmsT[:, p * 512:(p + 1) * 512], start=True, stop=True),
                             waits=((f"V8.{g*4+3}",) if p == 0 else ())
                             + ((f"SY{p-2}.{g}",) if p >= 2 else ()),
                             name=f"PY{p}.{g}")
                    pipe.add("scalar",
                             lambda E, p=p: E.activation(
                                 yT[:, p * 512:(p + 1) * 512],
                                 yp[p % 2][:, :], AF.Copy),
                             waits=(f"PY{p}.{g}",)
                             + ((f"P5.{(g-1)*4+3}",) if (p == 0 and g >= 1) else ()),
                             name=f"SY{p}.{g}")

            def tail(j):
                g, s_ = j // 4, j % 4
                jp, j4 = j % 2, j % 6

                def p5(E, s_=s_):
                    r = None
                    for hf in (0, 1):
                        for c in range(8):
                            E.matmul(ao[:, hf * 512:(hf + 1) * 512],
                                     yT[:, c * 512 + s_ * 128:c * 512 + (s_ + 1) * 128],
                                     wo_s[:, c * D + hf * 512:c * D + hf * 512 + 512],
                                     start=(c == 0), stop=False)
                        r = E.matmul(ao[:, hf * 512:(hf + 1) * 512], onesb[0:1, :],
                                     bob[0:1, hf * 512:(hf + 1) * 512],
                                     start=False, stop=True)
                    return r
                pipe.add("tensor", p5,
                         waits=(f"SY7.{g}",)
                         + ((f"SC.{j-1}",) if j >= 1 else ("ldwo",)),
                         name=f"P5.{j}")
                pipe.add("scalar",
                         lambda E, jp=jp: E.activation(aob[jp][:, :], ao[:, :],
                                                       AF.Copy),
                         waits=(f"P5.{j}",) + ((f"RC.{j-2}",) if j >= 2 else ()),
                         name=f"SC.{j}")
                ln_head(pipe, "R", j, aob[jp][:, 0:512], aob[jp][:, 512:1024],
                        st2[jp], bnrow2[jp],
                        waits=(f"SC.{j}",) + ((f"RC.{j-2}",) if j >= 2 else ()))

                def rc_fn(E, jp=jp, j4=j4):
                    E.tensor_scalar(xc2[:, :], aob[jp][:, :], st2[jp][:, 2:3],
                                    None, ALU.add)
                    return E.scalar_tensor_tensor(ust[jp][:, :], xc2[:, :],
                                                  st2[jp][:, 3:4], xb[j4][:, :],
                                                  ALU.mult, ALU.add)
                pipe.add("vector", rc_fn,
                         waits=(f"RQ.{j}",) + ((f"STU.{j-2}",) if j >= 2 else ()),
                         name=f"RC.{j}")
                pipe.add("sync",
                         lambda E, j=j, jp=jp: E.dma_start(
                             u_d[j * 128:(j + 1) * 128, :], ust[jp][:, :]),
                         waits=(f"RC.{j}",), name=f"STU.{j}", dma=True,
                         stream=f"u{j % 2}")

            h1a_vs(0)
            h1a_pe(0)
            h1a_vs(1)
            h1a_pe(1)
            for g in range(NG):
                for s_ in range(4):
                    h2a(g * 4 + s_)
                ystage(g)
                for s_ in range(4):
                    tail(g * 4 + s_)

            pipe.add("sync", lambda E: None,
                     waits=(f"STU.{S//128-2}", f"STU.{S//128-1}"))
            pipe.build()
        if stop_after == "a":
            return nc

        with ExitStack() as cs:
            pipe = Pipe(nc, ctx, "b1")
            ub = [cs.enter_context(nc.sbuf_tensor(f"bub{i}", [128, D], F32)) for i in range(4)]
            st = [cs.enter_context(nc.sbuf_tensor(f"bst{i}", [128, 10], F32)) for i in range(4)]
            bnrow = [cs.enter_context(nc.sbuf_tensor(f"bbnrow{i}", [128, 12], F32))
                     for i in range(4)]
            xc = [cs.enter_context(nc.sbuf_tensor(f"bxc{i}", [128, D], F32)) for i in range(2)]
            t1 = cs.enter_context(nc.sbuf_tensor("bt1", [128, D], F32))
            xqg = [cs.enter_context(nc.sbuf_tensor(f"bxqg{i}", [128, D], BF16)) for i in range(4)]
            xqT = [cs.enter_context(nc.sbuf_tensor(f"bxqT{i}", [128, 8 * 512], BF16))
                   for i in range(2)]
            hst = [cs.enter_context(nc.sbuf_tensor(f"bhst{i}", [128, 2048], BF16))
                   for i in range(2)]
            tp = [cs.enter_context(nc.psum_tensor(f"btp{i}", [128, D], BF16)) for i in range(2)]
            hp = [cs.enter_context(nc.psum_tensor(f"bhp{i}", [128, 512], F32)) for i in range(4)]

            def front_vs(j):
                jq = j % 4
                jp = j % 2
                pipe.add("gpsimd",
                         lambda E, j=j, jq=jq: E.dma_start(
                             ub[jq][:, :], u_d[j * 128:(j + 1) * 128, :]),
                         waits=(f"C.{j-4}",) if j >= 4 else (),
                         name=f"LDU.{j}", dma=True, stream=f"u{j % 4}")
                ln_head(pipe, "", j, ub[jq][:, 0:512], ub[jq][:, 512:1024],
                        st[jq], bnrow[jq],
                        waits=(f"LDU.{j}",) + ((f"V5.{j-4}",) if j >= 4 else ()))

                def c_fn(E, jp=jp, jq=jq):
                    E.tensor_scalar(xc[jp][:, :], ub[jq][:, :], st[jq][:, 2:3],
                                    None, ALU.add)
                    E.reduce_max(st[jq][:, 5:6], xc[jp][:, :], AX.X,
                                 apply_absolute_value=True)
                    return E.drain()
                pipe.add("vector", c_fn,
                         waits=(f"Q.{j}",) + ((f"V5.{j-2}",) if j >= 2 else ()),
                         name=f"C.{j}")

                def v4(E, jq=jq):
                    E.tensor_scalar(st[jq][:, 6:7], st[jq][:, 5:6], st[jq][:, 3:4],
                                    None, ALU.mult)
                    E.drain()
                    E.tensor_scalar(st[jq][:, 6:7], st[jq][:, 6:7], QEPS,
                                    1.0 / 127.0, ALU.max, ALU.mult)
                    E.drain()
                    E.reciprocal(st[jq][:, 7:8], st[jq][:, 6:7])
                    E.tensor_scalar(st[jq][:, 8:9], st[jq][:, 6:7], wscal[:, 0:1],
                                    None, ALU.mult)
                    E.drain()
                    E.tensor_scalar(st[jq][:, 9:10], st[jq][:, 7:8], st[jq][:, 3:4],
                                    None, ALU.mult)
                    return E.drain()
                pipe.add("vector", v4, name=f"V4.{j}")

                def v5(E, jp=jp, jq=jq, j4=j % 4):
                    E.tensor_scalar(t1[:, :], xc[jp][:, :], st[jq][:, 9:10], CRND,
                                    ALU.mult, ALU.add)
                    return E.tensor_scalar(xqg[j4][:, :], t1[:, :], CRND,
                                           st[jq][:, 8:9],
                                           ALU.subtract, ALU.mult)
                pipe.add("vector", v5,
                         waits=((f"P1.{j-4}",) if j >= 4 else ()), name=f"V5.{j}")

            def front_pe(j):
                jp, j4 = j % 2, j % 4
                gp = (j // 4) % 2
                s_ = j % 4

                def p1_fn(E, j4=j4, jp=jp):
                    r = None
                    for c in range(8):
                        r = E.transpose(tp[jp][:, c * 128:(c + 1) * 128],
                                        xqg[j4][:, c * 128:(c + 1) * 128],
                                        identb[:, :])
                    return r
                pipe.add("tensor", p1_fn,
                         waits=(f"V5.{j}",) + ((f"V6.{j-2}",) if j >= 2 else ()),
                         name=f"P1.{j}")
                pipe.add("vector",
                         lambda E, jp=jp, gp=gp, s_=s_: E.tensor_copy(
                             xqT[gp][:, :].rearrange("p (c t) -> p c t", t=512)
                             [:, :, s_ * 128:(s_ + 1) * 128],
                             tp[jp][:, :].rearrange("p (c t) -> p c t", t=128)),
                         waits=(f"P1.{j}",)
                         + ((f"PW31.{(j//4)-2}",) if (s_ == 0 and j >= 8) else ()),
                         name=f"V6.{j}")

            for s_ in range(4):
                front_vs(s_)
            for s_ in range(4):
                front_pe(s_)
            pe_at = {8: 0, 16: 1, 24: 2, 28: 3}
            for g in range(NG):
                gp = g % 2
                if g + 1 < NG:
                    for k in range(4):
                        front_vs(4 * (g + 1) + k)
                for c in range(32):
                    F = g * 8 + c // 4

                    def pw(E, c=c, gp=gp, g=g):
                        if g == 0 and c < 4:
                            E.wait_ge(su_we[0], su_we[1])
                        r = None
                        for cf in range(8):
                            r = E.matmul(hp[c % 4][:, :],
                                         w1qT[:, cf * DF + c * 128:cf * DF + (c + 1) * 128],
                                         xqT[gp][:, cf * 512:(cf + 1) * 512],
                                         start=(cf == 0), stop=(cf == 7))
                        return r
                    pipe.add("tensor", pw,
                             waits=(f"V6.{g*4+3}",)
                             + ((f"SE{c-4}.{g}",) if c >= 4 else
                                ((f"SE{28+c}.{g-1}",) if g >= 1 else ())),
                             name=f"PW{c}.{g}")
                    pipe.add("scalar",
                             lambda E, c=c: E.activation(
                                 hst[(c // 4) % 2][:, (c % 4) * 512:(c % 4 + 1) * 512],
                                 hp[c % 4][:, :], AF.Copy),
                             waits=(f"PW{c}.{g}",)
                             + ((f"STH.{F-2}",) if (c % 4 == 0 and F >= 2) else ()),
                             name=f"SE{c}.{g}")
                    if c % 4 == 3:
                        pipe.add("sync",
                                 lambda E, c=c, g=g: E.dma_start(
                                     hview[:, c - 3:c + 1, g * 512:(g + 1) * 512],
                                     hst[(c // 4) % 2][:, :].rearrange(
                                         "p (s t) -> p s t", t=512)),
                                 waits=(f"SE{c}.{g}", f"SE{c-1}.{g}",
                                        f"SE{c-2}.{g}", f"SE{c-3}.{g}"),
                                 name=f"STH.{F}", dma=True, stream=f"h{F % 2}")
                    if c in pe_at and g + 1 < NG:
                        front_pe(4 * (g + 1) + pe_at[c])
            pipe.add("sync", lambda E: None,
                     waits=(f"STH.{NG*8-2}", f"STH.{NG*8-1}"))
            pipe.build()
        if stop_after == "b1":
            return nc
        ctxA.close()

        with ExitStack() as cs:
            pipe = Pipe(nc, ctx, "b2")
            w2T = cs.enter_context(nc.sbuf_tensor("w2T", [128, 32 * D], BF16))
            w2p = [cs.enter_context(nc.sbuf_tensor(f"w2p{i}", [128, 512], BF16))
                   for i in range(4)]
            hload = cs.enter_context(nc.sbuf_tensor("hload", [128, 8 * 514], BF16))
            hsil = cs.enter_context(nc.sbuf_tensor("hsil", [128, 32 * 512], BF16))
            cvt = [cs.enter_context(nc.sbuf_tensor(f"cvt{i}", [128, 512], BF16))
                   for i in range(8)]
            tqf = [cs.enter_context(nc.sbuf_tensor(f"tqf{i}", [128, 512], F32))
                   for i in range(2)]
            rmax = cs.enter_context(nc.sbuf_tensor("rmax", [128, 512], BF16))
            rmin = cs.enter_context(nc.sbuf_tensor("rmin", [128, 512], BF16))
            s2st = cs.enter_context(nc.sbuf_tensor("s2st", [128, 8], F32))
            s2row = cs.enter_context(nc.sbuf_tensor("s2row", [1, 512], F32))
            s2b = cs.enter_context(nc.sbuf_tensor("s2b", [128, 512], BF16))
            hqf = cs.enter_context(nc.sbuf_tensor("hqf", [128, 32 * 512], BF16))
            st2 = [cs.enter_context(nc.sbuf_tensor(f"cst{i}", [128, 10], F32)) for i in range(2)]
            bnrow2 = cs.enter_context(nc.sbuf_tensor("cbnrow", [128, 12], F32))
            xc2 = cs.enter_context(nc.sbuf_tensor("cxc2", [128, D], F32))
            ub = [cs.enter_context(nc.sbuf_tensor(f"cub{i}", [128, D], F32)) for i in range(2)]
            ost = [cs.enter_context(nc.sbuf_tensor(f"cost{i}", [128, D], F32)) for i in range(2)]
            oa = cs.enter_context(nc.psum_tensor("coa", [128, D], F32))
            ob = cs.enter_context(nc.psum_tensor("cob", [128, D], F32))
            tps = cs.enter_context(nc.psum_tensor("ctps", [128, 512], BF16))
            tpf = cs.enter_context(nc.psum_tensor("ctpf", [1, 512], F32))
            s2p = cs.enter_context(nc.psum_tensor("cs2p", [128, 512], F32))
            w2tp = cs.enter_context(nc.psum_tensor("cw2tp", [128, 512], BF16))

            hlv = hload[:, :].rearrange("p (c t) -> p c t", t=514)
            pipe.add("gpsimd",
                     lambda E: E.memset(hlv[:, :, 0:1], 0.0), name="hz0")

            def w2_load(i):
                ccb, fb = i // 8, i % 8

                def ld(E, ccb=ccb, fb=fb, i=i):
                    if i == 0:
                        E.wait_ge(su_ts30[0], su_ts30[1])
                        E.wait_ge(su_ts31[0], su_ts31[1])
                    return E.dma_start(
                        w2p[i % 4][:, :],
                        w2q_d[fb * 128:(fb + 1) * 128,
                              ccb * 512:(ccb + 1) * 512])
                pipe.add("gpsimd", ld,
                         waits=(f"W2T{i-4}",) if i >= 4 else (),
                         name=f"w2pl{i}", dma=True, stream=f"w2p{i % 4}")

            def w2_tr_ev(i):
                ccb, fb = i // 8, i % 8

                def tr(E, i=i):
                    r = None
                    for k in range(4):
                        r = E.transpose(
                            w2tp[:, k * 128:(k + 1) * 128],
                            w2p[i % 4][:, k * 128:(k + 1) * 128], identb[:, :])
                    return r
                pipe.add("tensor", tr,
                         waits=(f"w2pl{i}",)
                         + ((f"W2E{i-1}",) if i >= 1 else ()),
                         name=f"W2T{i}")

                def ev(E, ccb=ccb, fb=fb, i=i):
                    r = None
                    for k in range(4):
                        cc = ccb * 4 + k
                        r = E.activation(
                            w2T[:, cc * D + fb * 128:cc * D + (fb + 1) * 128],
                            w2tp[:, k * 128:(k + 1) * 128], AF.Copy)
                    return r
                pipe.add("scalar", ev, waits=(f"W2T{i}",), name=f"W2E{i}")

            def conv_one(g, c):
                cs8 = c % 8
                lhw = []
                if c >= 8:
                    lhw.append(f"CV{c-8}.{g}")
                elif g >= 1:
                    lhw.append(f"CV{c+24}.{g-1}")
                if g == 0:
                    pipe.add("sync",
                             lambda E, c=c, cs8=cs8: E.dma_start(
                                 hlv[:, cs8:cs8 + 1, 1:514],
                                 hview[:, c:c + 1, 0:513]),
                             waits=tuple(lhw), name=f"LH{c}.{g}", dma=True,
                             stream=f"lh{c % 8}")
                elif g == NG - 1:
                    pipe.add("sync",
                             lambda E, c=c, g=g, cs8=cs8: E.dma_start(
                                 hlv[:, cs8:cs8 + 1, 0:513],
                                 hview[:, c:c + 1, g * 512 - 1:S]),
                             waits=tuple(lhw), name=f"LH{c}.{g}", dma=True,
                             stream=f"lh{c % 8}")
                else:
                    pipe.add("sync",
                             lambda E, c=c, g=g, cs8=cs8: E.dma_start(
                                 hlv[:, cs8:cs8 + 1, :],
                                 hview[:, c:c + 1, g * 512 - 1:g * 512 + 513]),
                             waits=tuple(lhw), name=f"LH{c}.{g}", dma=True,
                             stream=f"lh{c % 8}")

                def cv(E, c=c, cs8=cs8):
                    E.tensor_scalar(cvt[c % 8][:, :], hlv[:, cs8, 0:512],
                                    cw0s[:, c:c + 1], None, ALU.mult)
                    E.scalar_tensor_tensor(cvt[c % 8][:, :], hlv[:, cs8, 1:513],
                                           cw1s[:, c:c + 1], cvt[c % 8][:, :],
                                           ALU.mult, ALU.add)
                    return E.scalar_tensor_tensor(cvt[c % 8][:, :], hlv[:, cs8, 2:514],
                                                  cw2s[:, c:c + 1], cvt[c % 8][:, :],
                                                  ALU.mult, ALU.add)
                waits = [f"LH{c}.{g}"]
                if c >= 8:
                    waits.append(f"SL{c-8}.{g}")
                elif g >= 1:
                    waits.append(f"SL{24+c}.{g-1}")
                if g == 0:
                    waits.append("hz0")
                if g == NG - 1:
                    waits.append("hz15")
                pipe.add("vector", cv, waits=tuple(waits), name=f"CV{c}.{g}")

                def sl(cc):
                    pipe.add("scalar",
                             lambda E, cc=cc: E.activation(
                                 hsil[:, cc * 512:(cc + 1) * 512], cvt[cc % 8][:, :],
                                 AF.Silu, bias=cbs[:, cc:cc + 1]),
                             waits=(f"CV{cc}.{g}",)
                             + ((f"HQ{cc}.{g-1}",) if g >= 1 else ()),
                             name=f"SL{cc}.{g}")

                def mxmn(cc):
                    if cc == 0:
                        pipe.add("vector",
                                 lambda E: E.tensor_copy(rmax[:, :], hsil[:, 0:512]),
                                 waits=(f"SL0.{g}",)
                                 + ((f"TRX.{g-1}",) if g >= 1 else ()),
                                 name=f"MX0.{g}")
                        pipe.add("vector",
                                 lambda E: E.tensor_copy(rmin[:, :], hsil[:, 0:512]),
                                 waits=(f"SL0.{g}",)
                                 + ((f"TRN.{g-1}",) if g >= 1 else ()),
                                 name=f"MN0.{g}")
                    else:
                        pipe.add("vector",
                                 lambda E, cc=cc: E.tensor_max(
                                     rmax[:, :], rmax[:, :],
                                     hsil[:, cc * 512:(cc + 1) * 512]),
                                 waits=(f"SL{cc}.{g}",), name=f"MX{cc}.{g}")
                        pipe.add("vector",
                                 lambda E, cc=cc: E.tensor_tensor(
                                     rmin[:, :], rmin[:, :],
                                     hsil[:, cc * 512:(cc + 1) * 512],
                                     ALU.min),
                                 waits=(f"SL{cc}.{g}",), name=f"MN{cc}.{g}")
                if c >= 1:
                    sl(c - 1)
                if c >= 2:
                    mxmn(c - 2)
                if c == 31:
                    sl(31)
                    mxmn(30)
                    mxmn(31)

            def s2stage(g):
                def trx(E):
                    r = None
                    for b in range(4):
                        r = E.transpose(tps[:, b * 128:(b + 1) * 128],
                                        rmax[:, b * 128:(b + 1) * 128], identb[:, :])
                    return r
                pipe.add("tensor", trx, waits=(f"MX31.{g}",), name=f"TRX.{g}")
                pipe.add("vector",
                         lambda E: E.reduce_max(
                             s2st[:, 0:4],
                             tps[:, :].rearrange("p (b t) -> p b t", t=128), AX.X),
                         waits=(f"TRX.{g}",), name=f"VRX.{g}")

                def trn_(E):
                    r = None
                    for b in range(4):
                        r = E.transpose(tps[:, b * 128:(b + 1) * 128],
                                        rmin[:, b * 128:(b + 1) * 128], identb[:, :])
                    return r
                pipe.add("tensor", trn_, waits=(f"MN31.{g}", f"VRX.{g}"),
                         name=f"TRN.{g}")

                def vrn(E):
                    E.tensor_reduce(s2st[:, 4:8],
                                    tps[:, :].rearrange("p (b t) -> p b t", t=128),
                                    AX.X, ALU.min)
                    E.drain()
                    E.tensor_scalar(s2st[:, 4:8], s2st[:, 4:8], -1.0, None, ALU.mult)
                    E.drain()
                    E.tensor_tensor(s2st[:, 0:4], s2st[:, 0:4], s2st[:, 4:8], ALU.max)
                    E.drain()
                    E.tensor_scalar(s2st[:, 0:4], s2st[:, 0:4], QEPS, 1.0 / 127.0,
                                    ALU.max, ALU.mult)
                    E.drain()
                    E.reciprocal(s2st[:, 0:4], s2st[:, 0:4])
                    return E.drain()
                pipe.add("vector", vrn, waits=(f"TRN.{g}",), name=f"VRN.{g}")

                def trs(E):
                    r = None
                    for b in range(4):
                        r = E.matmul(tpf[0:1, b * 128:(b + 1) * 128],
                                     s2st[:, b:b + 1], identf[:, :],
                                     start=(b == 0), stop=(b == 3))
                    return r
                pipe.add("tensor", trs, waits=(f"VRN.{g}",), name=f"TRS.{g}")
                pipe.add("vector", lambda E: E.tensor_copy(s2row[:, :], tpf[:, :]),
                         waits=(f"TRS.{g}",), name=f"EVS.{g}")
                pipe.add("tensor",
                         lambda E: E.matmul(s2p[:, :], onesfr[0:1, :], s2row[0:1, :],
                                            start=True, stop=True),
                         waits=(f"EVS.{g}",), name=f"S2M.{g}")
                pipe.add("vector", lambda E: E.tensor_copy(s2b[:, :], s2p[:, :]),
                         waits=(f"S2M.{g}",), name=f"S2D.{g}")

            def hq_one(g, c):
                def vhq(E, c=c):
                    E.tensor_mul(tqf[c % 2][:, :],
                                 hsil[:, c * 512:(c + 1) * 512], s2b[:, :])
                    return E.tensor_scalar(hqf[:, c * 512:(c + 1) * 512],
                                           tqf[c % 2][:, :],
                                           CRND, CRND, ALU.add, ALU.subtract)
                w = [f"S2D.{g}"]
                if g >= 1:
                    w.append(f"PHB{c}.{g-1}")
                pipe.add("vector", vhq, waits=tuple(w), name=f"HQ{c}.{g}")

            def passmm(g, hf):
                tag = "AB"[hf]
                if hf == 0:
                    hq_one(g, 0)
                    hq_one(g, 1)
                for c in range(32):
                    if hf == 0 and c + 2 < 32:
                        hq_one(g, c + 2)
                    def ph(E, c=c, hf=hf):
                        r = None
                        for si in (0, 1):
                            ps = oa if si == 0 else ob
                            for oh in (0, 1):
                                r = E.matmul(ps[:, oh * 512:(oh + 1) * 512],
                                             hqf[:, c * 512 + hf * 256 + si * 128:
                                                 c * 512 + hf * 256 + (si + 1) * 128],
                                             w2T[:, c * D + oh * 512:c * D + oh * 512 + 512],
                                             start=(c == 0), stop=(c == 31))
                        return r
                    w = [f"HQ{c}.{g}"]
                    if c == 0:
                        if hf == 1:
                            w.append(f"RC.{g*4+1}")
                        elif g >= 1:
                            w.append(f"RC.{(g-1)*4+3}")
                    if g == 0 and hf == 0:
                        w.append(f"W2E{(c // 4) * 8 + 7}")
                    pipe.add("tensor", ph, waits=tuple(w), name=f"PH{tag}{c}.{g}")

            def ln4(g, hf):
                tag = "AB"[hf]
                for si in (0, 1):
                    j = g * 4 + hf * 2 + si
                    jp = j % 2
                    ps = oa if si == 0 else ob
                    pipe.add("sync",
                             lambda E, j=j, jp=jp: E.dma_start(
                                 ub[jp][:, :], u_d[j * 128:(j + 1) * 128, :]),
                             waits=(f"RC.{j-2}",) if j >= 2 else (),
                             name=f"LDU.{j}", dma=True, stream=f"u{j % 2}")
                    ln_head(pipe, "R", j, ps[:, 0:512], ps[:, 512:1024],
                            st2[jp], bnrow2,
                            waits=(f"PH{tag}31.{g}",)
                            + ((f"RC.{j-2}",) if j >= 2 else ()))

                    def rc_fn(E, jp=jp, ps=ps):
                        E.tensor_scalar(xc2[:, :], ps[:, :], st2[jp][:, 2:3],
                                        None, ALU.add)
                        return E.scalar_tensor_tensor(ost[jp][:, :], xc2[:, :],
                                                      st2[jp][:, 3:4], ub[jp][:, :],
                                                      ALU.mult, ALU.add)
                    pipe.add("vector", rc_fn,
                             waits=(f"RQ.{j}", f"LDU.{j}")
                             + ((f"STO.{j-2}",) if j >= 2 else ()),
                             name=f"RC.{j}")
                    pipe.add("sync",
                             lambda E, j=j, jp=jp: E.dma_start(
                                 out_d[j * 128:(j + 1) * 128, :], ost[jp][:, :]),
                             waits=(f"RC.{j}",), name=f"STO.{j}", dma=True,
                             stream=f"o{j % 2}")

            for c in range(32):
                conv_one(0, c)
                w2_load(2 * c)
                w2_load(2 * c + 1)
                w2_tr_ev(2 * c)
                w2_tr_ev(2 * c + 1)
            for g in range(NG):
                if g + 1 == NG - 1:
                    pipe.add("gpsimd", lambda E: E.memset(hlv[:, :, 513:514], 0.0),
                             waits=(f"CV31.{NG-2}",), name="hz15")
                s2stage(g)
                passmm(g, 0)
                if g + 1 < NG:
                    for c in range(16):
                        conv_one(g + 1, c)
                ln4(g, 0)
                passmm(g, 1)
                if g + 1 < NG:
                    for c in range(16, 32):
                        conv_one(g + 1, c)
                ln4(g, 1)
            pipe.add("sync", lambda E: None,
                     waits=(f"STO.{S//128-2}", f"STO.{S//128-1}"))
            pipe.build()

    return nc


_NC_CACHE = {}


def kernel(L0_attn_x, L0_attn_wq, L0_attn_bq, L0_attn_wk, L0_attn_bk,
           L0_attn_wv, L0_attn_bv, L0_attn_wo, L0_attn_bo,
           L0_ffn_w1, L0_ffn_conv_w, L0_ffn_conv_b, L0_ffn_w2):
    x = np.asarray(L0_attn_x, dtype=np.float32)
    B = x.shape[0]
    assert x.shape == (B, S, D)

    def f32(a, shape):
        return np.ascontiguousarray(np.asarray(a, dtype=np.float32).reshape(shape))

    cw = f32(L0_ffn_conv_w, (32, 128, 3))
    common = {
        "wq": f32(L0_attn_wq, (D, D)), "wk": f32(L0_attn_wk, (D, D)),
        "wv": f32(L0_attn_wv, (D, D)), "wo": f32(L0_attn_wo, (D, D)),
        "bq": f32(L0_attn_bq, (1, D)), "bk": f32(L0_attn_bk, (1, D)),
        "bv": f32(L0_attn_bv, (1, D)), "bo": f32(L0_attn_bo, (1, D)),
        "w1": f32(L0_ffn_w1, (DF, D)), "w2": f32(L0_ffn_w2, (D, DF)),
        "cw0": np.ascontiguousarray(cw[:, :, 0].T),
        "cw1": np.ascontiguousarray(cw[:, :, 1].T),
        "cw2": np.ascontiguousarray(cw[:, :, 2].T),
        "cb": np.ascontiguousarray(f32(L0_ffn_conv_b, (32, 128)).T),
        "identb": np.eye(128, dtype=bfd),
        "identf": np.eye(128, dtype=np.float32),
        "onesb": np.ones((1, 128), dtype=bfd),
        "onesfr": np.ones((1, 128), dtype=np.float32),
        "onesfc": np.ones((128, 1), dtype=np.float32),
    }
    if "prog" not in _NC_CACHE:
        _NC_CACHE["prog"] = build_program()
    nc = _NC_CACHE["prog"]
    in_maps = [dict(common, x=np.ascontiguousarray(x[n])) for n in range(B)]
    res = run_bass_kernel_spmd(nc, in_maps, core_ids=list(range(B)))
    return np.stack([np.asarray(r["out"]) for r in res.results], axis=0)
